# version 24
# baseline (speedup 1.0000x reference)
import numpy as np
import ml_dtypes
import concourse.bass as bass
import concourse.mybir as mybir
from concourse.bass_utils import run_bass_kernel_spmd

F32 = mybir.dt.float32
BF16 = mybir.dt.bfloat16
AF = mybir.ActivationFunctionType
ALU = mybir.AluOpType

D = 1024
N = 4096
NC = 8
H = 16
HD = 64
NBANK_W = 512

BF = np.dtype(ml_dtypes.bfloat16)



class _BankView:
    def __init__(self, p):
        self.p = p
    def __getitem__(self, i):
        if i < 4:
            return self.p.pb_s[:, i * 512:(i + 1) * 512]
        return self.p.pb4[i - 4]


class Ctx:

    ENGINES = ("tensor", "vector", "scalar", "gpsimd", "sync")

    def __init__(self, nc):
        self.nc = nc
        self.q = {e: [] for e in self.ENGINES}
        self.sem = {}
        self.count = {}

    def new_sem(self, name):
        self.sem[name] = None
        self.count[name] = 0
        return name

    def val(self, sem):
        return self.count[sem]

    AUTOSEM = {"scalar": "sc", "vector": "ev", "gpsimd": "gp"}

    def op(self, eng_name, fn, waits=(), inc=None, inc_n=1):
        waits = list(waits)
        auto = self.AUTOSEM.get(eng_name)
        if auto is not None:
            assert inc is None or inc == auto, \
                f"{eng_name} ops always signal via {auto}"
            waits.append((auto, self.count[auto]))
            inc, inc_n = auto, 1
        if inc is not None:
            self.count[inc] += inc_n
        waits = tuple(waits)

        def closure(eng, fn=fn, waits=waits, inc=inc, inc_n=inc_n):
            for s, v in waits:
                if v > 0:
                    eng.wait_ge(self.sem[s], v)
            instr = fn(eng)
            if inc is not None:
                assert instr is not None
                instr.then_inc(self.sem[inc], inc_n)

        self.q[eng_name].append(closure)

    def dma(self, eng_name, out, in_, waits=(), inc=None):
        if inc is not None:
            self.count[inc] += 16
        waits = tuple(waits)

        def closure(eng, out=out, in_=in_, waits=waits, inc=inc):
            for s, v in waits:
                if v > 0:
                    eng.wait_ge(self.sem[s], v)
            instr = eng.dma_start(out=out, in_=in_)
            if inc is not None:
                instr.then_inc(self.sem[inc], 16)

        self.q[eng_name].append(closure)

    def fence(self):
        snap = [(s, v) for s, v in self.count.items() if v > 0]
        for e in self.ENGINES:
            def closure(eng, snap=tuple(snap)):
                for s, v in snap:
                    eng.wait_ge(self.sem[s], v)
            self.q[e].append(closure)

    def barrier(self, name):
        tgt = {}
        for e in self.ENGINES:
            s = f"bar_{e}"
            if s not in self.sem:
                self.new_sem(s)
            self.count[s] += 1
            tgt[e] = self.count[s]

        for e in self.ENGINES:
            def closure(eng, e=e, tgt=dict(tgt)):
                if e == "sync":
                    eng.nop().then_inc(self.sem[f"bar_{e}"], 1)
                else:
                    eng.engine_nop().then_inc(self.sem[f"bar_{e}"], 1) \
                        if hasattr(eng, "engine_nop") else \
                        eng.nop().then_inc(self.sem[f"bar_{e}"], 1)
                for e2 in self.ENGINES:
                    eng.wait_ge(self.sem[f"bar_{e2}"], tgt[e2])
            self.q[e].append(closure)

    def finalize(self):
        nc = self.nc
        import contextlib
        with contextlib.ExitStack() as stack:
            for name in list(self.sem):
                self.sem[name] = stack.enter_context(nc.semaphore(name))
            blk = stack.enter_context(nc.Block())
            for eng_name in self.ENGINES:
                ops = self.q[eng_name]
                getattr(blk, eng_name)(lambda eng, ops=ops: [f(eng) for f in ops])



def _bf16(a):
    return np.ascontiguousarray(a.astype(BF))


def host_prepare(inputs):
    d = {}
    x = inputs["L0_attn_x"]
    d["x_fm"] = [np.ascontiguousarray(x[b].T) for b in range(8)]

    Wqkv = inputs["L0_attn_W_qkv"]
    WqkvT = np.ascontiguousarray(Wqkv.T)
    perm = np.empty(HD, np.int64)
    perm[:32] = np.arange(0, 64, 2)
    perm[32:] = np.arange(1, 64, 2)
    cols = np.arange(3 * D)
    for h in range(H):
        cols[h * HD:(h + 1) * HD] = h * HD + perm
        cols[D + h * HD:D + (h + 1) * HD] = D + h * HD + perm
    d["WqkvT"] = np.ascontiguousarray(WqkvT[:, cols])
    d["WprojT"] = np.ascontiguousarray(inputs["L0_attn_W_proj"].T)

    inv = 1.0 / (10000.0 ** (np.arange(0, HD, 2, np.float32) / HD))
    tt = np.arange(N, dtype=np.float32)
    fr = np.outer(tt, inv)
    c_fm = np.ascontiguousarray(np.cos(fr).T.astype(np.float32))
    s_fm = np.ascontiguousarray(np.sin(fr).T.astype(np.float32))
    sc = 1.0 / np.sqrt(HD)
    C1q = np.tile(c_fm * sc, (4, 1))
    C2q = np.concatenate([-s_fm * sc, s_fm * sc, -s_fm * sc, s_fm * sc], 0)
    d["rope_c1q"], d["rope_c2q"] = _bf16(C1q), _bf16(C2q)

    W1 = inputs["L0_ffn_W1"]
    d["W1a"] = np.ascontiguousarray(W1[:, :, :2048])
    d["W1g"] = np.ascontiguousarray(W1[:, :, 2048:])
    d["W2"] = np.ascontiguousarray(inputs["L0_ffn_W2"])
    b1 = inputs["L0_ffn_b1"]
    d["b1a"] = np.ascontiguousarray(
        b1[:, :2048].reshape(8, 16, 128).transpose(2, 0, 1).reshape(128, 128))
    d["b1g"] = np.ascontiguousarray(
        b1[:, 2048:].reshape(8, 16, 128).transpose(2, 0, 1).reshape(128, 128))
    d["b2"] = np.ascontiguousarray(inputs["L0_ffn_b2"])
    d["Wr"] = np.ascontiguousarray(inputs["L0_ffn_Wr"])
    d["br"] = np.ascontiguousarray(inputs["L0_ffn_br"])

    hcperm = np.empty(D, np.int64)
    for h in range(H):
        hcperm[(h % 2) * 512 + (h // 2) * 64:(h % 2) * 512 + (h // 2) * 64
               + HD] = np.arange(h * HD, (h + 1) * HD)
    for k in ("Wq", "Wk", "Wgamma", "Wf1", "Wf2", "Wog1"):
        d[k] = np.ascontiguousarray(inputs[f"L1_attn_{k}"])
    d["Wv"] = np.ascontiguousarray(inputs["L1_attn_Wv"][:, hcperm])
    d["Wog2"] = np.ascontiguousarray(inputs["L1_attn_Wog2"][:, hcperm])
    d["norm_w"] = np.ascontiguousarray(inputs["L1_attn_norm_w"][hcperm])
    d["Wo"] = np.ascontiguousarray(inputs["L1_attn_Wo"][hcperm, :])

    d["WrT1"] = np.ascontiguousarray(inputs["L1_ffn_W_r"].T)
    d["WkT1"] = np.ascontiguousarray(inputs["L1_ffn_W_k"].T)
    d["WvT1"] = np.ascontiguousarray(inputs["L1_ffn_W_v"].T)
    d["mu_r"] = np.ascontiguousarray(inputs["L1_ffn_mu_r"].reshape(NC, 128).T)
    d["mu_k"] = np.ascontiguousarray(inputs["L1_ffn_mu_k"].reshape(NC, 128).T)
    return d


def _inline_consts(nc):
    c = {}
    ident = np.eye(128, dtype=BF)
    c["ident"] = nc.inline_tensor(ident, "c_ident")
    ones = np.ones((128, 128), dtype=BF)
    c["ones"] = nc.inline_tensor(ones, "c_ones")
    swap = np.zeros((128, 128), dtype=BF)
    for g in range(2):
        b = g * 64
        for i in range(32):
            swap[b + 32 + i, b + i] = 1.0
            swap[b + i, b + 32 + i] = 1.0
    c["swap"] = nc.inline_tensor(swap, "c_swap")
    ut = np.triu(np.ones((64, 64), np.float32)).astype(BF)
    c["ut64"] = nc.inline_tensor(np.tile(ut, (2, 1)), "c_ut64")
    c["ident_f32"] = nc.inline_tensor(np.eye(128, dtype=np.float32), "c_idf")
    sel = np.zeros((8, 8 * 128), dtype=BF)
    for e in range(8):
        sel[e, e * 128:(e + 1) * 128] = 1.0
    c["sel"] = nc.inline_tensor(sel, "c_sel")
    sl = np.tril(np.ones((64, 64), np.float32), -1).astype(np.float32)
    il = np.tril(np.ones((64, 64), np.float32)).astype(np.float32)
    iu = il.T.copy()
    su = sl.T.copy()
    c["mask_sl"] = nc.inline_tensor(
        np.tile(np.tile(sl, (1, 8)), (2, 1)).astype(BF), "c_msl")
    c["mask_iu"] = nc.inline_tensor(
        np.tile(np.tile(iu, (1, 8)), (2, 1)).astype(BF), "c_miu")
    c["mask_su"] = nc.inline_tensor(
        np.tile(np.tile(su, (1, 8)), (2, 1)).astype(BF), "c_msu")
    i64b = np.zeros((128, 1024), dtype=BF)
    for pr in range(128):
        for h in range(16):
            i64b[pr, h * 64 + (pr % 64)] = 1.0
    c["i64b"] = nc.inline_tensor(i64b, "c_i64b")
    return c



class Prog:

    def __init__(self, nc, dbg=()):
        self.nc = nc
        self.dbg = set(dbg)
        self.c = _inline_consts(nc)
        ap = lambda t: t.ap()

        di = {}
        di["x_fm"] = nc.dram_tensor("x_fm", [D, N], F32, kind="ExternalInput")
        di["WqkvT"] = nc.dram_tensor("WqkvT", [D, 3 * D], F32, kind="ExternalInput")
        di["WprojT"] = nc.dram_tensor("WprojT", [D, D], F32, kind="ExternalInput")
        for k in ("rope_c1q", "rope_c2q"):
            di[k] = nc.dram_tensor(k, [128, N], BF16, kind="ExternalInput")
        di["W1a"] = nc.dram_tensor("W1a", [8, D, 2048], F32, kind="ExternalInput")
        di["W1g"] = nc.dram_tensor("W1g", [8, D, 2048], F32, kind="ExternalInput")
        di["W2"] = nc.dram_tensor("W2", [8, 2048, D], F32, kind="ExternalInput")
        di["b1a"] = nc.dram_tensor("b1a", [128, 128], F32, kind="ExternalInput")
        di["b1g"] = nc.dram_tensor("b1g", [128, 128], F32, kind="ExternalInput")
        di["b2"] = nc.dram_tensor("b2", [8, D], F32, kind="ExternalInput")
        di["Wr"] = nc.dram_tensor("Wr", [D, 8], F32, kind="ExternalInput")
        di["br"] = nc.dram_tensor("br", [8], F32, kind="ExternalInput")
        for k in ("Wq", "Wk", "Wv", "Wo"):
            di[k] = nc.dram_tensor(k, [D, D], F32, kind="ExternalInput")
        di["Wgamma"] = nc.dram_tensor("Wgamma", [D, H], F32, kind="ExternalInput")
        di["Wf1"] = nc.dram_tensor("Wf1", [D, HD], F32, kind="ExternalInput")
        di["Wf2"] = nc.dram_tensor("Wf2", [HD, D], F32, kind="ExternalInput")
        di["Wog1"] = nc.dram_tensor("Wog1", [D, HD], F32, kind="ExternalInput")
        di["Wog2"] = nc.dram_tensor("Wog2", [HD, D], F32, kind="ExternalInput")
        di["norm_w"] = nc.dram_tensor("norm_w", [D], F32, kind="ExternalInput")
        di["WrT1"] = nc.dram_tensor("WrT1", [D, D], F32, kind="ExternalInput")
        di["WkT1"] = nc.dram_tensor("WkT1", [D, D], F32, kind="ExternalInput")
        di["WvT1"] = nc.dram_tensor("WvT1", [D, D], F32, kind="ExternalInput")
        di["mu_r"] = nc.dram_tensor("mu_r", [128, NC], F32, kind="ExternalInput")
        di["mu_k"] = nc.dram_tensor("mu_k", [128, NC], F32, kind="ExternalInput")
        self.di = {k: ap(v) for k, v in di.items()}

        self.y = ap(nc.dram_tensor("y", [D, N], F32, kind="ExternalOutput"))

        self.x1_dram = ap(nc.dram_tensor("x1_scratch", [D, N], F32, kind="Internal"))
        self.logits_dram = ap(nc.dram_tensor("logits_scratch", [8, N], F32,
                                             kind="Internal"))
        self.o_dram = ap(nc.dram_tensor("o_scratch", [8, 128, N], BF16, kind="Internal"))

        A = nc.alloc_sbuf_tensor
        self.xn = ap(A("xn", [128, NC, N], BF16))
        self.big = ap(A("big", [128, NC, N], BF16))
        self.consts = ap(A("consts", [128, 128 * 3], BF16))
        self.ut64 = ap(A("ut64", [128, 64], BF16))
        self.sel = ap(A("sel", [8, 8 * 128], BF16))
        self.identf = ap(A("identf", [128, 128], F32))
        self.i64b = ap(A("i64b", [128, 1024], BF16))
        self.masks = ap(A("masks", [128, 512 * 3], BF16))
        self.pool_bytes = 71 * 1024
        self.pool = ap(A("pool", [128, self.pool_bytes], mybir.dt.uint8))
        self.pool_off = 0
        self.pool_marks = []

        self.pb_s = ap(nc.alloc_psum_tensor("pbs", [128, 2048], F32))
        self.pb4 = [ap(nc.alloc_psum_tensor(f"pb{i}", [128, 512], F32))
                    for i in range(4)]
        self.sconst = ap(A("sconst", [128, 8], F32))
        self.pb = _BankView(self)

        self.ctx = Ctx(nc)
        for s in ("ld", "ldx0", "ldx1", "mm", "ev", "sc", "gp", "st",
                  "aux", "gld", "rld", "bld", "wrf", "tbl", "lgl", "lgs", "da1s",
                  "wl0", "wl1", "wl2", "w2l0", "w2l1", "w2l2",
                  "wpl0", "wpl1", "rt"):
            self.ctx.new_sem(s)
        self.dbg_outs = {}
        self.bank_guard = [("ev", 0)] * 8
        self.w1_guard = [("mm", 0), ("mm", 0), ("mm", 0)]
        self.w2_guard = [("mm", 0), ("mm", 0), ("mm", 0)]
        self.wpj_guard = [("mm", 0), ("mm", 0), ("mm", 0)]

    def palloc(self, nbytes, dtype, free_elems):
        nbytes_al = (nbytes + 31) & ~31
        assert self.pool_off + nbytes_al <= self.pool_bytes, \
            f"pool overflow: {self.pool_off}+{nbytes_al}"
        apx = self.pool[:, self.pool_off:self.pool_off + nbytes_al].bitcast(dtype)
        self.pool_off += nbytes_al
        return apx[:, :free_elems] if free_elems is not None else apx

    def pmark(self, fence=True):
        if fence:
            self.ctx.fence()
        self.pool_marks.append(self.pool_off)

    def prelease(self):
        self.pool_off = self.pool_marks.pop()

    def dbg_out(self, name, shape, dtype=F32):
        if name not in self.dbg_outs:
            self.dbg_outs[name] = self.nc.dram_tensor(
                f"dbg_{name}", list(shape), dtype, kind="ExternalOutput").ap()
        return self.dbg_outs[name]

    @property
    def eps_col(self):
        return self.sconst[:, 0:1]

    @property
    def one_col(self):
        return self.sconst[:, 1:2]

    def load_consts(self):
        cx, c = self.ctx, self.c
        cx.op("vector", lambda e: e.memset(self.sconst[:, 0:1], 1e-5))
        cx.op("vector", lambda e: e.memset(self.sconst[:, 1:2], 1.0))
        self.sconst_ready = ("ev", cx.val("ev"))
        cx.dma("sync", self.consts[:, 0:128], c["ident"].ap(), inc="ld")
        cx.dma("sync", self.consts[:, 128:256], c["ones"].ap(), inc="ld")
        cx.dma("sync", self.consts[:, 256:384], c["swap"].ap(), inc="ld")
        cx.dma("sync", self.ut64, c["ut64"].ap(), inc="ld")
        cx.dma("sync", self.masks[:, 0:512], c["mask_sl"].ap(), inc="ld")
        cx.dma("sync", self.masks[:, 512:1024], c["mask_iu"].ap(), inc="ld")
        cx.dma("sync", self.masks[:, 1024:1536], c["mask_su"].ap(), inc="ld")
        cx.dma("sync", self.sel, c["sel"].ap(), inc="ld")
        cx.dma("sync", self.identf, c["ident_f32"].ap(), inc="ld")
        cx.dma("sync", self.i64b, c["i64b"].ap(), inc="ld")
        self.consts_ready = cx.val("ld")

    @property
    def ident(self):
        return self.consts[:, 0:128]

    @property
    def ones(self):
        return self.consts[:, 128:256]

    @property
    def swap(self):
        return self.consts[:, 256:384]

    def emit_ln(self, dst, tw, x_stage, stage_wait, ib4,
                scratch_bf, scratch_f32, rows_f32, rows_bf, extra_mm_wait=None):
        cx = self.ctx
        ib_sum, ib_sq, ib_mb, ib_rb = ib4
        pb_sum, pb_sq = self.pb[ib_sum], self.pb[ib_sq]
        pb_mb, pb_rb = self.pb[ib_mb], self.pb[ib_rb]
        w_cast = [stage_wait, ("mm", cx.val("mm"))]
        for c in range(NC):
            cx.op("scalar", lambda e, c=c: e.copy(
                out=scratch_bf[:, c * tw:(c + 1) * tw], in_=x_stage[:, c, :]),
                waits=w_cast if c == 0 else ())
        cast_done = cx.val("sc")
        for c in range(NC):
            cx.op("scalar", lambda e, c=c: e.activation(
                out=scratch_bf[:, (NC + c) * tw:(NC + c + 1) * tw],
                in_=x_stage[:, c, :], func=AF.Square))
        sq_done = cx.val("sc")
        ones_col = self.ones[:, 0:1]
        w0 = [("sc", cast_done), self.bank_guard[ib_sum]]
        if extra_mm_wait is not None:
            w0.append(extra_mm_wait)
        for c in range(NC):
            cx.op("tensor", lambda e, c=c: e.matmul(
                pb_sum[0:1, 0:tw], ones_col, scratch_bf[:, c * tw:(c + 1) * tw],
                start=(c == 0), stop=(c == NC - 1)),
                waits=w0 if c == 0 else ())
        for c in range(NC):
            cx.op("tensor", lambda e, c=c: e.matmul(
                pb_sq[0:1, 0:tw], ones_col,
                scratch_bf[:, (NC + c) * tw:(NC + c + 1) * tw],
                start=(c == 0), stop=(c == NC - 1)),
                waits=[("sc", sq_done), self.bank_guard[ib_sq]] if c == 0 else (),
                inc="mm" if c == NC - 1 else None)
        mmv = cx.val("mm")
        cx.op("scalar", lambda e: e.mul(out=rows_f32[0:1, 0:tw],
                                        in_=pb_sum[0:1, 0:tw], mul=1.0 / D),
              waits=[("mm", mmv)])
        cx.op("scalar", lambda e: e.activation(out=rows_f32[0:1, tw:2 * tw],
                                               in_=rows_f32[0:1, 0:tw],
                                               func=AF.Square))
        scv = cx.val("sc")
        cx.op("vector", lambda e: e.scalar_tensor_tensor(
            out=rows_f32[0:1, 2 * tw:3 * tw], in0=pb_sq[0:1, 0:tw],
            scalar=1.0 / D, in1=rows_f32[0:1, tw:2 * tw],
            op0=ALU.mult, op1=ALU.subtract), waits=[("sc", scv)])
        self.bank_guard[ib_sum] = ("sc", cx.val("sc"))
        self.bank_guard[ib_sq] = ("ev", cx.val("ev"))
        cx.op("scalar", lambda e: e.activation(
            out=rows_f32[0:1, 2 * tw:3 * tw], in_=rows_f32[0:1, 2 * tw:3 * tw],
            func=AF.Ln, bias=self.eps_col[0:1, :]),
            waits=[("ev", cx.val("ev"))])
        cx.op("scalar", lambda e: e.activation(
            out=rows_f32[0:1, 2 * tw:3 * tw], in_=rows_f32[0:1, 2 * tw:3 * tw],
            func=AF.Exp, scale=-0.5))
        cx.op("scalar", lambda e: e.copy(
            out=rows_bf[0:1, 0:tw], in_=rows_f32[0:1, 0:tw]))
        cx.op("scalar", lambda e: e.copy(
            out=rows_bf[0:1, tw:2 * tw], in_=rows_f32[0:1, 2 * tw:3 * tw]))
        scv2 = cx.val("sc")
        ones_row = self.ones[0:1, :]
        cx.op("tensor", lambda e: e.matmul(
            pb_mb[:, 0:tw], ones_row, rows_bf[0:1, 0:tw],
            start=True, stop=True),
            waits=[("sc", scv2), self.bank_guard[ib_mb]])
        cx.op("tensor", lambda e: e.matmul(
            pb_rb[:, 0:tw], ones_row, rows_bf[0:1, tw:2 * tw],
            start=True, stop=True),
            waits=[self.bank_guard[ib_rb]], inc="mm")
        mmv2 = cx.val("mm")
        for c in range(NC):
            cx.op("vector", lambda e, c=c: e.tensor_sub(
                out=scratch_f32, in0=x_stage[:, c, :], in1=pb_mb[:, 0:tw]),
                waits=[("mm", mmv2)] if c == 0 else ())
            cx.op("vector", lambda e, c=c: e.tensor_mul(
                out=dst[:, c, :], in0=scratch_f32, in1=pb_rb[:, 0:tw]))
        self.bank_guard[ib_mb] = ("ev", cx.val("ev"))
        self.bank_guard[ib_rb] = ("ev", cx.val("ev"))


    def emit_l0_attention(self):
        cx = self.ctx
        p = self
        TW = 512
        p.pmark()
        tabs = p.palloc(2 * N * 2, BF16, 2 * N)
        qb = p.palloc(N * 2, BF16, N)
        kb = p.palloc(N * 2, BF16, N)
        vaug = p.palloc(32 * 130 * 2, BF16, 32 * 130).rearrange(
            "p (k c) -> p k c", k=32)
        pt = p.palloc(2048 * 2, BF16, 2048)
        rsc = p.palloc(TW * 4, F32, TW)
        rsc2 = p.palloc(TW * 4, F32, TW)
        otmp = p.palloc(128 * 2, BF16, 128)
        rcol = p.palloc(2 * 4, F32, 2)
        wqkv2 = [tuple(p.palloc(NC * 128 * 2, BF16, NC * 128).rearrange(
            "p (c o) -> p c o", c=NC) for _ in range(3)) for _ in range(2)]
        wcs0 = p.palloc(2048 * 2, BF16, 2048)
        wcs1 = p.palloc(2048 * 2, BF16, 2048)
        p.init_weight_cache(wcs0, wcs1)

        for i, name in enumerate(("rope_c1q", "rope_c2q")):
            cx.dma("sync", tabs[:, i * N:(i + 1) * N], p.di[name], inc="tbl")
        tabs_ready = cx.val("tbl")
        t_c1q = tabs[:, 0 * N:1 * N]
        t_c2q = tabs[:, 1 * N:2 * N]

        xn_ready = ("ev", cx.val("ev"))

        def load_pair_weights(hp, mmwait):
            wls = f"wl{hp % 2}"
            srcw = p.di["WqkvT"]
            wq_, wk_, wv_ = wqkv2[hp % 2]
            for wdst, base in ((wq_, hp * 128), (wk_, D + hp * 128),
                               (wv_, 2 * D + hp * 128)):
                cx.dma("gpsimd", wdst,
                       srcw[:, base:base + 128].rearrange(
                           "(c p) o -> p c o", p=128),
                       waits=[("mm", mmwait)], inc=wls)
            return (wls, cx.val(wls))

        next_wl = load_pair_weights(0, cx.val("mm"))
        for hp in range(8):
            wq, wk, wv = wqkv2[hp % 2]
            wl_ready = next_wl
            p.emit_cache_chunk(64)

            if hp == 0:
                cx.op("vector", lambda e: e.memset(vaug, 1.0))
            vaug_clear = cx.val("ev")
            for kc in range(32):
                bk = (2, 3, 6, 7)[kc % 4]
                cx.op("tensor", lambda e, kc=kc, bk=bk, wv=wv: e.matmul(
                    p.pb[bk][:, 0:128], p.xn[:, 0, kc * 128:(kc + 1) * 128],
                    wv[:, 0, :], start=True, stop=False),
                    waits=[p.bank_guard[bk]] + ([wl_ready, xn_ready]
                                                if kc == 0 else []))
                for c in range(1, NC):
                    cx.op("tensor", lambda e, kc=kc, c=c, bk=bk, wv=wv:
                          e.matmul(
                              p.pb[bk][:, 0:128],
                              p.xn[:, c, kc * 128:(kc + 1) * 128],
                              wv[:, c, :], start=False, stop=(c == NC - 1)),
                          inc="mm" if c == NC - 1 else None)
                mmv = cx.val("mm")
                cx.op("vector", lambda e, kc=kc, bk=bk: e.tensor_copy(
                    out=vaug[:, kc, 0:64], in_=p.pb[bk][:, 0:64]),
                    waits=[("mm", mmv), ("ev", vaug_clear)])
                cx.op("vector", lambda e, kc=kc, bk=bk: e.tensor_copy(
                    out=vaug[:, kc, 65:129], in_=p.pb[bk][:, 64:128]))
                p.bank_guard[bk] = ("ev", cx.val("ev"))
            for dst_sb, wsb, bank in ((qb, wq, 0), (kb, wk, 1)):
                for ti in range(N // TW):
                    t0 = ti * TW
                    for c in range(NC):
                        cx.op("tensor", lambda e, c=c, t0=t0, wsb=wsb, bank=bank:
                              e.matmul(p.pb[bank][:, 0:TW], wsb[:, c, :],
                                       p.xn[:, c, t0:t0 + TW],
                                       start=(c == 0), stop=(c == NC - 1)),
                              waits=([wl_ready, xn_ready, p.bank_guard[bank]]
                                     if c == 0 else ()),
                              inc="mm" if c == NC - 1 else None)
                    mmv = cx.val("mm")
                    cx.op("vector", lambda e, dst_sb=dst_sb, t0=t0, bank=bank:
                          e.tensor_copy(out=dst_sb[:, t0:t0 + TW],
                                        in_=p.pb[bank][:, 0:TW]),
                          waits=[("mm", mmv)])
                    p.bank_guard[bank] = ("ev", cx.val("ev"))

            if hp < 7:
                next_wl = load_pair_weights(hp + 1, cx.val("mm"))

            for buf, gain in ((qb, 1.0), (kb, 8.0)):
                for ti in range(N // TW):
                    t0 = ti * TW
                    evb = cx.val("ev")
                    cx.op("tensor", lambda e, buf=buf, t0=t0: e.matmul(
                        p.pb[3][:, 0:TW], p.swap, buf[:, t0:t0 + TW],
                        start=True, stop=True),
                        waits=[("ev", evb), ("tbl", tabs_ready),
                               p.bank_guard[3]], inc="mm")
                    mmv = cx.val("mm")
                    cx.op("vector", lambda e, t0=t0, gain=gain:
                          e.scalar_tensor_tensor(
                              out=rsc, in0=p.pb[3][:, 0:TW], scalar=gain,
                              in1=t_c2q[:, t0:t0 + TW],
                              op0=ALU.mult, op1=ALU.mult),
                          waits=[("mm", mmv)])
                    p.bank_guard[3] = ("ev", cx.val("ev"))
                    cx.op("vector", lambda e, buf=buf, t0=t0, gain=gain:
                          e.scalar_tensor_tensor(
                              out=rsc2, in0=buf[:, t0:t0 + TW], scalar=gain,
                              in1=t_c1q[:, t0:t0 + TW],
                              op0=ALU.mult, op1=ALU.mult))
                    cx.op("vector", lambda e, buf=buf, t0=t0: e.tensor_add(
                        out=buf[:, t0:t0 + TW], in0=rsc, in1=rsc2))
            rope_done = ("ev", cx.val("ev"))

            for qc in range(32):
                q0 = qc * 128
                exp_done = [0] * 8
                sdone = [0] * 8
                avdone = [0] * 8

                def emit_sbatch(b, qc=qc, q0=q0):
                    bb = 1024 * (b % 2)
                    w_ = [rope_done, p.bank_guard[0 + 2 * (b % 2)]]
                    for j in range(4):
                        kc = b * 4 + j
                        k0 = kc * 128
                        for hb in (0, 64):
                            scol = bb + (hb // 64) * 512 + j * 128
                            cx.op("tensor", lambda e, hb=hb, scol=scol,
                                  k0=k0, q0=q0: e.matmul(
                                      p.pb_s[:, scol:scol + 128],
                                      kb[hb:hb + 64, k0:k0 + 128],
                                      qb[hb:hb + 64, q0:q0 + 128],
                                      start=True, stop=True),
                                  waits=w_ if (j == 0 and hb == 0) else (),
                                  inc="mm" if (j == 3 and hb == 64)
                                  else None)
                    sdone[b] = cx.val("mm")

                def emit_exp(b):
                    bb = 1024 * (b % 2)
                    cx.op("scalar", lambda e, b=b, bb=bb: e.activation(
                        out=pt[:, (b % 2) * 1024:(b % 2) * 1024 + 1024],
                        in_=p.pb_s[:, bb:bb + 1024], func=AF.Exp),
                        waits=[("mm", sdone[b])])
                    exp_done[b] = cx.val("sc")
                    p.bank_guard[0 + 2 * (b % 2)] = ("sc", exp_done[b])
                    p.bank_guard[1 + 2 * (b % 2)] = ("sc", exp_done[b])

                def emit_av(b, qc=qc):
                    w_ = [("sc", exp_done[b])]
                    if b == 0:
                        w_ += [p.bank_guard[4], p.bank_guard[5]]
                    for j in range(4):
                        kc = b * 4 + j
                        for hslot, bank in ((0, 4), (1, 5)):
                            pcol = (b % 2) * 1024 + hslot * 512 + j * 128
                            vcol = hslot * 65
                            cx.op("tensor", lambda e, pcol=pcol, vcol=vcol,
                                  kc=kc, bank=bank, st=(b == 0 and j == 0),
                                  sp=(b == 7 and j == 3): e.matmul(
                                      p.pb[bank][:, 0:65],
                                      pt[:, pcol:pcol + 128],
                                      vaug[:, kc, vcol:vcol + 65],
                                      start=st, stop=sp),
                                  waits=w_ if (j == 0 and hslot == 0) else (),
                                  inc="mm" if (j == 3 and hslot == 1)
                                  else None)
                    avdone[b] = cx.val("mm")

                for b in range(8):
                    emit_sbatch(b)
                    if b >= 1:
                        emit_av(b - 1)
                    emit_exp(b)
                emit_av(7)
                av_done = cx.val("mm")
                cx.op("vector", lambda e: e.reciprocal(
                    out=rcol[:, 0:1], in_=p.pb[4][:, 64:65]),
                    waits=[("mm", av_done)])
                cx.op("vector", lambda e: e.reciprocal(
                    out=rcol[:, 1:2], in_=p.pb[5][:, 64:65]))
                cx.op("vector", lambda e: e.tensor_scalar(
                    out=otmp[:, 0:64], in0=p.pb[4][:, 0:64],
                    scalar1=rcol[:, 0:1], scalar2=None, op0=ALU.mult))
                cx.op("vector", lambda e: e.tensor_scalar(
                    out=otmp[:, 64:128], in0=p.pb[5][:, 0:64],
                    scalar1=rcol[:, 1:2], scalar2=None, op0=ALU.mult))
                p.bank_guard[4] = ("ev", cx.val("ev"))
                p.bank_guard[5] = ("ev", cx.val("ev"))
                evv = cx.val("ev")
                cx.op("tensor", lambda e: e.transpose(
                    p.pb[6].bitcast(BF16)[:, 0:128], otmp, p.ident),
                    waits=[("ev", evv), p.bank_guard[6]], inc="mm")
                mmv = cx.val("mm")
                cx.op("vector", lambda e, hp=hp, q0=q0: e.tensor_copy(
                    out=p.big[:, hp, q0:q0 + 128],
                    in_=p.pb[6].bitcast(BF16)[:, 0:128]),
                    waits=[("mm", mmv)])
                p.bank_guard[6] = ("ev", cx.val("ev"))
        p.prelease()

    def init_weight_cache(self, stage0, stage1):
        nc, cx = self.nc, self.ctx
        self.w1_bf = nc.dram_tensor("w1_bf", [8, 8, 128, 4096], BF16,
                                    kind="Internal").ap()
        self.w2_bf = nc.dram_tensor("w2_bf", [8, 8, 128, 2048], BF16,
                                    kind="Internal").ap()
        self.wc_stage = (stage0, stage1)
        for s in ("wci0", "wci1", "wco0", "wco1"):
            self.ctx.new_sem(s)
        jobs = []
        for e in range(8):
            for dc in range(8):
                for ag, wsrc in ((0, "W1a"), (1, "W1g")):
                    for kh in range(2):
                        s_ = self.di[wsrc][e, dc * 128:(dc + 1) * 128, :] \
                            .rearrange("p (g kh o) -> p g kh o",
                                       g=8, kh=2)[:, :, kh, :]
                        d_ = self.w1_bf[e].rearrange(
                            "g p (kh q) -> p g kh q", kh=2)[
                                :, :, kh, ag * 1024 + dc * 128:
                                ag * 1024 + dc * 128 + 128]
                        jobs.append((s_, d_, 1024, 8))
            for kc in range(16):
                jobs.append((self.di["W2"][e, kc * 128:(kc + 1) * 128, :],
                             self.w2_bf[e, kc // 2, :,
                                        (kc % 2) * 1024:(kc % 2) * 1024
                                        + 1024],
                             1024, None))
        self.wc_jobs = jobs
        self.wc_next = 0

    def emit_cache_chunk(self, njobs):
        cx = self.ctx
        end = min(self.wc_next + njobs, len(self.wc_jobs))
        for i in range(self.wc_next, end):
            src, dst, n, v4 = self.wc_jobs[i]
            par = i % 2
            stage = self.wc_stage[par]
            sview = stage[:, 0:n]
            if v4 is not None:
                sview = sview.rearrange("p (g o) -> p g o", g=v4)
            cx.dma("gpsimd", sview, src,
                   waits=[(f"wco{par}", cx.val(f"wco{par}"))],
                   inc=f"wci{par}")
            cx.dma("sync", dst, sview,
                   waits=[(f"wci{par}", cx.val(f"wci{par}"))],
                   inc=f"wco{par}")
        self.wc_next = end
        if end == len(self.wc_jobs):
            self.wcache_done = [("wco0", cx.val("wco0")),
                                ("wco1", cx.val("wco1"))]

    def emit_router(self):
        cx = self.ctx
        p = self
        wr = p.r_wr
        br_col = p.r_br
        logit = p.r_logit
        ltm = p.r_ltm
        scr = p.r_scr
        w_bf = p.w_bf

        cx.dma("gpsimd", br_col, p.di["br"].rearrange("(p o) -> p o", o=1),
               inc="rld")
        wl = ("rld", cx.val("rld"))
        cx.dma("sync", logit, p.logits_dram,
               waits=[("lgs", cx.val("lgs"))], inc="lgl")
        llv = cx.val("lgl")
        cx.op("vector", lambda e: e.tensor_scalar(
            out=logit, in0=logit, scalar1=br_col, scalar2=None, op0=ALU.add),
            waits=[wl, ("lgl", llv)])
        for blk4 in range(8):
            evv = cx.val("ev")
            for j in range(4):
                blk = blk4 * 4 + j
                cx.op("tensor", lambda e, blk=blk, j=j: e.transpose(
                    p.pb[5][:, j * 8:j * 8 + 8],
                    logit[:, blk * 128:(blk + 1) * 128], p.identf[0:8, 0:8]),
                    waits=([("ev", evv), p.bank_guard[5]] if j == 0 else ()),
                    inc="mm" if j == 3 else None)
            mmv = cx.val("mm")
            cx.op("vector", lambda e, blk4=blk4: e.tensor_copy(
                out=ltm[:, blk4 * 4:blk4 * 4 + 4, :].rearrange("p a b -> p (a b)"),
                in_=p.pb[5][:, 0:32]), waits=[("mm", mmv)])
            p.bank_guard[5] = ("ev", cx.val("ev"))
        cx.op("vector", lambda e: e.reduce_max(
            out=scr[:, 0:32], in_=ltm, axis=mybir.AxisListType.X))
        for blk in range(32):
            cx.op("vector", lambda e, blk=blk: e.tensor_scalar(
                out=scr[:, 64 + blk * 8:64 + blk * 8 + 8], in0=ltm[:, blk, :],
                scalar1=scr[:, blk:blk + 1], scalar2=None, op0=ALU.is_equal))
        for blk in range(32):
            cx.op("vector", lambda e, blk=blk: e.scalar_tensor_tensor(
                out=ltm[:, blk, :], in0=scr[:, 64 + blk * 8:64 + blk * 8 + 8],
                scalar=-1e30, in1=ltm[:, blk, :], op0=ALU.mult, op1=ALU.add))
        cx.op("vector", lambda e: e.reduce_max(
            out=scr[:, 32:64], in_=ltm, axis=mybir.AxisListType.X))
        for blk in range(32):
            cx.op("vector", lambda e, blk=blk: e.tensor_scalar(
                out=scr[:, 320 + blk * 8:320 + blk * 8 + 8], in0=ltm[:, blk, :],
                scalar1=scr[:, 32 + blk:32 + blk + 1], scalar2=None,
                op0=ALU.is_equal))
        cx.op("vector", lambda e: e.tensor_sub(
            out=scr[:, 576:608], in0=scr[:, 0:32], in1=scr[:, 32:64]))
        scv = cx.val("ev")
        cx.op("scalar", lambda e: e.activation(
            out=scr[:, 576:608], in_=scr[:, 576:608], func=AF.Sigmoid),
            waits=[("ev", scv)])
        scs = cx.val("sc")
        cx.op("vector", lambda e: e.tensor_scalar(
            out=scr[:, 608:640], in0=scr[:, 576:608], scalar1=-1.0, scalar2=1.0,
            op0=ALU.mult, op1=ALU.add), waits=[("sc", scs)])
        for blk in range(32):
            cx.op("vector", lambda e, blk=blk: e.tensor_scalar(
                out=ltm[:, blk, :], in0=scr[:, 64 + blk * 8:64 + blk * 8 + 8],
                scalar1=scr[:, 576 + blk:576 + blk + 1], scalar2=None,
                op0=ALU.mult))
            cx.op("vector", lambda e, blk=blk: e.scalar_tensor_tensor(
                out=ltm[:, blk, :], in0=scr[:, 320 + blk * 8:320 + blk * 8 + 8],
                scalar=scr[:, 608 + blk:608 + blk + 1], in1=ltm[:, blk, :],
                op0=ALU.mult, op1=ALU.add))
        for blk4 in range(8):
            evv = cx.val("ev")
            for j in range(4):
                blk = blk4 * 4 + j
                cx.op("tensor", lambda e, blk=blk, j=j: e.transpose(
                    p.pb[6][0:8, j * 128:(j + 1) * 128],
                    ltm[:, blk, :], p.identf),
                    waits=([("ev", evv), p.bank_guard[6]] if j == 0 else ()),
                    inc="mm" if j == 3 else None)
            mmv = cx.val("mm")
            cx.op("vector", lambda e, blk4=blk4: e.tensor_copy(
                out=w_bf[0:8, blk4 * 512:(blk4 + 1) * 512],
                in_=p.pb[6][0:8, 0:512]), waits=[("mm", mmv)])
            p.bank_guard[6] = ("ev", cx.val("ev"))

    def emit_l0_moe_residual(self):
        cx = self.ctx
        p = self
        TW = 512
        nt = N // TW
        act = p.m_act
        df = p.m_df
        xs = p.m_xs
        wbc = p.m_wbc
        sab = p.m_sab
        b1a = p.m_b1a
        b1g = p.m_b1g
        b2s = p.m_b2
        lnbf = p.m_lnbf
        lnf = p.m_lnf
        lrows = p.m_lrows
        lrbf = p.m_lrbf

        cx.dma("gpsimd", b1a, p.di["b1a"], inc="bld")
        cx.dma("gpsimd", b1g, p.di["b1g"], inc="bld")
        cx.dma("gpsimd", b2s, p.di["b2"], inc="bld")
        bias_ready = ("bld", cx.val("bld"))

        for ti in range(nt):
            t0 = ti * TW
            wfb = ("ev", cx.val("ev"))
            for c in range(NC):
                bdf = 4 + c % 2
                cx.op("tensor", lambda e, c=c, t0=t0, bdf=bdf: e.matmul(
                    p.pb[bdf][:, 0:TW], b2s[0:8, c * 128:(c + 1) * 128],
                    p.w_bf[0:8, t0:t0 + TW], start=True, stop=True),
                    waits=[wfb, bias_ready, p.bank_guard[bdf]],
                    inc="mm")
                mmv = cx.val("mm")
                cx.op("vector", lambda e, c=c, bdf=bdf: e.tensor_copy(
                    out=df[:, c, :], in_=p.pb[bdf][:, 0:TW]),
                    waits=[("mm", mmv)])
                p.bank_guard[bdf] = ("ev", cx.val("ev"))

            for ex in range(8):
                cx.op("tensor", lambda e, ex=ex, t0=t0: e.matmul(
                    p.pb[5][:, 0:TW], p.sel[0:8, ex * 128:(ex + 1) * 128],
                    p.w_bf[0:8, t0:t0 + TW], start=True, stop=True),
                    waits=[p.bank_guard[5]], inc="mm")
                mmv = cx.val("mm")
                cx.op("vector", lambda e: e.tensor_copy(
                    out=wbc, in_=p.pb[5][:, 0:TW]), waits=[("mm", mmv)])
                p.bank_guard[5] = ("ev", cx.val("ev"))
                wbc_ready = cx.val("ev")

                act_ev = [None] * 16
                for grp in range(8):
                    par = grp % 3
                    wsem = f"wl{par}"
                    ring = p.m_ring[par]
                    cx.dma("gpsimd", ring,
                           p.w1_bf[ex, grp],
                           waits=[(wsem, cx.val(wsem)), p.w1_guard[par],
                                  p.w2_guard[par], p.wpj_guard[par]] + (
                               p.wcache_done if ti == 0 and ex == 0 and
                               grp < 2 else []),
                           inc=wsem)
                    wlv = (wsem, cx.val(wsem))
                    for kh in range(2):
                        kc = grp * 2 + kh
                        ba, bg = ((0, 1), (2, 3), (6, 7))[kc % 3]
                        for c in range(NC):
                            cx.op("tensor", lambda e, c=c, t0=t0, ba=ba,
                                  ring=ring, kh=kh: e.matmul(
                                      p.pb[ba][:, 0:TW],
                                      ring[:, kh * 2048 + c * 128:
                                           kh * 2048 + c * 128 + 128],
                                      p.xn[:, c, t0:t0 + TW],
                                      start=(c == 0), stop=(c == NC - 1)),
                                  waits=([wlv, p.bank_guard[ba]]
                                         if c == 0 else ()))
                        for c in range(NC):
                            cx.op("tensor", lambda e, c=c, t0=t0, bg=bg,
                                  ring=ring, kh=kh: e.matmul(
                                      p.pb[bg][:, 0:TW],
                                      ring[:, kh * 2048 + 1024 + c * 128:
                                           kh * 2048 + 1024 + c * 128 + 128],
                                      p.xn[:, c, t0:t0 + TW],
                                      start=(c == 0), stop=(c == NC - 1)),
                                  waits=([p.bank_guard[bg]] if c == 0 else ()),
                                  inc="mm" if c == NC - 1 else None)
                        mmv = cx.val("mm")
                        if kh == 1:
                            p.w1_guard[par] = ("mm", mmv)
                        cx.op("scalar", lambda e, ex=ex, kc=kc, ba=ba:
                              e.activation(
                                  out=sab, in_=p.pb[ba][:, 0:TW],
                                  func=AF.Silu,
                                  bias=b1a[:, ex * 16 + kc:
                                           ex * 16 + kc + 1]),
                              waits=[("mm", mmv), ("ev", cx.val("ev"))])
                        scv = cx.val("sc")
                        cx.op("vector", lambda e, ex=ex, kc=kc, bg=bg:
                              e.scalar_tensor_tensor(
                                  out=act[:, kc, :], in0=p.pb[bg][:, 0:TW],
                                  scalar=b1g[:, ex * 16 + kc:
                                             ex * 16 + kc + 1],
                                  in1=sab, op0=ALU.add, op1=ALU.mult),
                              waits=[("sc", scv)])
                        cx.op("vector", lambda e, kc=kc: e.tensor_mul(
                            out=act[:, kc, :], in0=act[:, kc, :], in1=wbc))
                        p.bank_guard[ba] = ("sc", scv)
                        p.bank_guard[bg] = ("ev", cx.val("ev"))
                        act_ev[kc] = ("ev", cx.val("ev"))
                act_done = ("ev", cx.val("ev"))

                for grp in range(8):
                    par = grp % 3
                    wsem = f"w2l{par}"
                    ring = p.m_ring[par]
                    cx.dma("gpsimd", ring[:, 0:2048],
                           p.w2_bf[ex, grp],
                           waits=[(wsem, cx.val(wsem)), p.w2_guard[par],
                                  p.w1_guard[par]],
                           inc=wsem)
                    wlv = (wsem, cx.val(wsem))
                    for kh in range(2):
                        kc = grp * 2 + kh
                        for oc in range(8):
                            w_ = [wlv, act_ev[kc]]
                            if kc == 0:
                                w_.append(p.bank_guard[oc])
                            cx.op("tensor", lambda e, kc=kc, oc=oc, kh=kh,
                                  ring=ring: e.matmul(
                                      p.pb[oc][:, 0:TW],
                                      ring[:, kh * 1024 + oc * 128:
                                           kh * 1024 + oc * 128 + 128],
                                      act[:, kc, :],
                                      start=(kc == 0), stop=(kc == 15)),
                                  waits=w_ if (oc == 0 or kc == 0) else (),
                                  inc="mm" if oc == 7 else None)
                        p.w2_guard[par] = ("mm", cx.val("mm"))
                mmv = cx.val("mm")
                for oc in range(8):
                    cx.op("vector", lambda e, oc=oc: e.tensor_add(
                        out=df[:, oc, :], in0=df[:, oc, :],
                        in1=p.pb[oc][:, 0:TW]),
                        waits=[("mm", mmv)] if oc == 0 else ())
                    p.bank_guard[oc] = ("ev", cx.val("ev"))

            for half in range(TW // 256):
                s0 = t0 + half * 256
                for og in range(2):
                    par = og % 2
                    wsem = f"wpl{par}"
                    ring = p.m_ring[par]
                    if half == 0:
                        cx.dma("gpsimd",
                               ring.rearrange("p (c o) -> p c o", c=NC),
                               p.di["WprojT"][:, og * 512:(og + 1) * 512]
                               .rearrange("(c p) o -> p c o", p=128),
                               waits=[(wsem, cx.val(wsem)),
                                      p.wpj_guard[par], p.w2_guard[par],
                                      p.w1_guard[par]],
                               inc=wsem)
                    wlv = (wsem, cx.val(wsem))
                    for oq in range(4):
                        oc = og * 4 + oq
                        bank = oc // 2
                        col = (oc % 2) * 256
                        for ec in range(NC):
                            w_ = [wlv]
                            if ec == 0:
                                w_.append(p.bank_guard[bank])
                            cx.op("tensor",
                                  lambda e, ec=ec, ring=ring, oq=oq,
                                  bank=bank, col=col, s0=s0: e.matmul(
                                      p.pb[bank][:, col:col + 256],
                                      ring[:, ec * 512 + oq * 128:
                                           ec * 512 + oq * 128 + 128],
                                      p.big[:, ec, s0:s0 + 256],
                                      start=(ec == 0), stop=(ec == NC - 1)),
                                  waits=w_,
                                  inc="mm" if ec == NC - 1 else None)
                    p.wpj_guard[par] = ("mm", cx.val("mm"))
                mmv_proj = cx.val("mm")

                lds = f"ldx{half % 2}"
                cx.dma("sync", xs,
                       p.di["x_fm"][:, s0:s0 + 256].rearrange(
                           "(c p) t -> p c t", p=128),
                       waits=[("ev", cx.val("ev")), ("sc", cx.val("sc")),
                              ("st", cx.val("st"))], inc=lds)
                ldv = (lds, cx.val(lds))
                for c in range(NC):
                    cx.op("vector", lambda e, c=c, half=half: e.tensor_add(
                        out=xs[:, c, :], in0=xs[:, c, :],
                        in1=df[:, c, half * 256:half * 256 + 256]),
                        waits=[ldv, ("mm", mmv_proj)] if c == 0 else ())
                    cx.op("vector", lambda e, c=c:
                          e.tensor_add(
                              out=xs[:, c, :], in0=xs[:, c, :],
                              in1=p.pb[c // 2][:, (c % 2) * 256:
                                               (c % 2) * 256 + 256]))
                x1_done = cx.val("ev")
                for bk in range(4):
                    p.bank_guard[bk] = ("ev", x1_done)
                cx.dma("sync", p.x1_dram[:, s0:s0 + 256].rearrange(
                    "(c p) t -> p c t", p=128), xs,
                    waits=[("ev", x1_done)], inc="st")
                p.emit_ln(p.xn[:, :, s0:s0 + 256], 256, xs,
                          ("ev", x1_done), (4, 5, 6, 7), lnbf, lnf,
                          lrows, lrbf)

    def emit_l1_attention(self, ncp=32, stages="ABCDEFG"):
        cx = self.ctx
        p = self
        nc = self.nc
        self.da1_dram = nc.dram_tensor("da1_scratch", [D, N], BF16,
                                       kind="Internal").ap()
        bigf = p.big.rearrange("p a b -> p (a b)")
        wq1 = bigf[:, 0:8192].rearrange("p (c o) -> p c o", c=NC)
        wk1 = bigf[:, 8192:16384].rearrange("p (c o) -> p c o", c=NC)
        wv1 = bigf[:, 16384:24576].rearrange("p (c o) -> p c o", c=NC)
        wo1 = bigf[:, 24576:32768].rearrange("p (c o) -> p c o", c=NC)

        p.pmark()
        al = p.palloc
        wgam = al(NC * 16 * 2, BF16, NC * 16).rearrange("p (c o) -> p c o", c=NC)
        wf1 = al(NC * 64 * 2, BF16, NC * 64).rearrange("p (c o) -> p c o", c=NC)
        wog1 = al(NC * 64 * 2, BF16, NC * 64).rearrange("p (c o) -> p c o", c=NC)
        wf2 = al(D * 2, BF16, D)
        wog2 = al(D * 2, BF16, D)
        nwb = al(D * 4, F32, D)
        nwrow = al(D * 2, BF16, D)[0:1, :]
        v_tm = al(D * 2, BF16, D)
        og_tm = al(D * 2, BF16, D)
        kh_tm = al(D * 2, BF16, D)
        a_fm = al(D * 2, BF16, D)
        ktfm = al(D * 2, BF16, D)
        qtfm = al(D * 2, BF16, D)
        Mqt = al(D * 2, BF16, D)
        srow = al(128 * 4, F32, 128)[0:1, :]
        s0 = al(D * 2, BF16, D)
        o_tm = al(D * 2, BF16, D)
        ysb = al(64 * 2, BF16, 64)
        yfm = al(128 * 2, BF16, 128)
        gcol = al(16 * 4, F32, 16)
        rnorm = al(16 * 4, F32, 16)
        lamc = al(32 * 4, F32, 32)
        totf = al(2048 * 4, F32, 2048)[0:1, :]
        trow = al(1024 * 2, BF16, 1024)[0:1, :]
        col1 = al(4, F32, 1)
        f32a = al(D * 4, F32, D)
        f32b = al(D * 4, F32, D)
        s1b = al(D * 2, BF16, D)
        s2b = al(D * 2, BF16, D)
        s3b = al(D * 2, BF16, D)
        s4b = al(D * 2, BF16, D)
        s5b = al(D * 2, BF16, D)
        s6b = al(D * 2, BF16, D)
        s7b = al(D * 2, BF16, D)
        s8b = al(D * 2, BF16, D)
        q_tm = qt_tm = ofm = s1b
        k_tm = kt_tm = dsb = s2b
        ghi = a_tm = obuf = s3b
        glo = bfs = s4b
        lam = Bm = rhs_sb = s5b
        ivl = Cm = u_sb = s6b
        rat = Rmat = sdec = s7b
        ktn = Rp = s8b
        Tt = Rp
        esig = o_tm
        lf32 = f32b

        hv = lambda h: (h % 2) * 64
        hc = lambda h: (h % 2) * 512 + (h // 2) * 64
        MSL = p.masks[:, 0:512]
        MIU = p.masks[:, 512:1024]
        MSU = p.masks[:, 1024:1536]
        PB = p.pb_s
        P01 = PB[:, 0:1024]
        P23 = PB[:, 1024:2048]

        for wdst, name in ((wgam, "Wgamma"), (wf1, "Wf1"), (wog1, "Wog1")):
            cx.dma("gpsimd", wdst,
                   p.di[name].rearrange("(c p) o -> p c o", p=128), inc="gld")
        cx.dma("gpsimd", wf2[0:64, :], p.di["Wf2"], inc="gld")
        cx.dma("gpsimd", wog2[0:64, :], p.di["Wog2"], inc="gld")
        for wdst, name in ((wq1, "Wq"), (wk1, "Wk"), (wv1, "Wv"), (wo1, "Wo")):
            cx.dma("gpsimd", wdst,
                   p.di[name].rearrange("(c p) o -> p c o", p=128), inc="gld")
        cx.dma("sync", f32a[0:1, :], p.di["norm_w"], inc="tbl")
        wl_all = ("gld", cx.val("gld"))
        nw_l = ("tbl", cx.val("tbl"))
        cx.op("scalar", lambda e: e.copy(out=nwrow, in_=f32a[0:1, :]),
              waits=[nw_l])
        scv = cx.val("sc")
        for half in range(2):
            cx.op("tensor", lambda e, half=half: e.matmul(
                p.pb[6][:, 0:512], p.ones[0:1, :],
                nwrow[0:1, half * 512:half * 512 + 512],
                start=True, stop=True),
                waits=[("sc", scv), p.bank_guard[6]], inc="mm")
            mmv = cx.val("mm")
            cx.op("vector", lambda e, half=half: e.tensor_copy(
                out=nwb[:, half * 512:half * 512 + 512],
                in_=p.pb[6][:, 0:512]), waits=[("mm", mmv)])
            p.bank_guard[6] = ("ev", cx.val("ev"))
        cx.op("vector", lambda e: e.memset(s0, 0.0))
        xn_ready = ("ev", cx.val("ev"))

        for cp in range(ncp):
            t0 = cp * 128

            def lowrank_path(wlow, w2low, kind):
                for c in range(NC):
                    cx.op("tensor", lambda e, c=c, t0=t0, wlow=wlow: e.matmul(
                        p.pb[2][:, 16:80], p.xn[:, c, t0:t0 + 128],
                        wlow[:, c, :],
                        start=(c == 0), stop=(c == NC - 1)),
                        waits=[p.bank_guard[2], wl_all, xn_ready]
                        if c == 0 else (),
                        inc="mm" if c == NC - 1 else None)
                mmv = cx.val("mm")
                cx.op("scalar", lambda e: e.copy(out=ysb,
                                                 in_=p.pb[2][:, 16:80]),
                      waits=[("mm", mmv)])
                scv = cx.val("sc")
                p.bank_guard[2] = ("sc", scv)
                cx.op("tensor", lambda e: e.transpose(
                    p.pb[2].bitcast(BF16)[0:64, 256:384], ysb, p.ident),
                    waits=[("sc", scv), p.bank_guard[2]], inc="mm")
                mmv = cx.val("mm")
                cx.op("vector", lambda e: e.tensor_copy(
                    out=yfm[0:64, :], in_=p.pb[2].bitcast(BF16)[0:64, 256:384]),
                    waits=[("mm", mmv)])
                yf_done = cx.val("ev")
                p.bank_guard[2] = ("ev", yf_done)
                for half in range(2):
                    cx.op("tensor", lambda e, half=half, w2low=w2low: e.matmul(
                        p.pb[3][:, 0:512], yfm[0:64, :],
                        w2low[0:64, half * 512:half * 512 + 512],
                        start=True, stop=True),
                        waits=[("ev", yf_done), p.bank_guard[3]], inc="mm")
                    mmv = cx.val("mm")
                    if kind == "z":
                        cx.op("scalar", lambda e, half=half: e.activation(
                            out=f32a[:, half * 512:half * 512 + 512],
                            in_=p.pb[3][:, 0:512], func=AF.Exp, scale=-1.0),
                            waits=[("mm", mmv), ("ev", cx.val("ev"))])
                    else:
                        cx.op("scalar", lambda e, half=half: e.copy(
                            out=og_tm[:, half * 512:half * 512 + 512],
                            in_=p.pb[3][:, 0:512]),
                            waits=[("mm", mmv), ("ev", cx.val("ev"))])
                    p.bank_guard[3] = ("sc", cx.val("sc"))
                if kind == "z":
                    cx.op("vector", lambda e: e.tensor_scalar(
                        out=f32a, in0=f32a, scalar1=1.0, scalar2=None,
                        op0=ALU.add), waits=[("sc", cx.val("sc"))])
                    evv = cx.val("ev")
                    cx.op("scalar", lambda e: e.activation(
                        out=lf32, in_=f32a, func=AF.Ln), waits=[("ev", evv)])
                    cx.op("scalar", lambda e: e.activation(
                        out=esig, in_=lf32, func=AF.Exp, scale=-1.0),
                        waits=[("ev", cx.val("ev"))])
                    cx.op("scalar", lambda e: e.copy(out=ghi, in_=lf32))
                    scv = cx.val("sc")
                    cx.op("vector", lambda e: e.tensor_sub(
                        out=glo, in0=lf32, in1=ghi), waits=[("sc", scv)])

            lowrank_path(wf1, wf2, "z")
            lhilo = ("ev", cx.val("ev"))

            for c in range(NC):
                cx.op("tensor", lambda e, c=c, t0=t0: e.matmul(
                    p.pb[2][:, 0:16], p.xn[:, c, t0:t0 + 128], wgam[:, c, :],
                    start=(c == 0), stop=(c == NC - 1)),
                    waits=[p.bank_guard[2]] if c == 0 else (),
                    inc="mm" if c == NC - 1 else None)
            mmv = cx.val("mm")
            cx.op("scalar", lambda e: e.activation(
                out=gcol, in_=p.pb[2][:, 0:16], func=AF.Sigmoid, scale=-1.0),
                waits=[("mm", mmv)])
            scv = cx.val("sc")
            p.bank_guard[2] = ("sc", scv)
            cx.op("vector", lambda e: e.tensor_scalar(
                out=gcol, in0=gcol, scalar1=-1.0, scalar2=None, op0=ALU.add),
                waits=[("sc", scv)])

            def qkv_unit(wsb, dst, func, half):
                def emit():
                    bk = half
                    for c in range(NC):
                        cx.op("tensor", lambda e, c=c, wsb=wsb, half=half,
                              t0=t0, bk=bk: e.matmul(
                                  p.pb[bk][:, 0:512],
                                  p.xn[:, c, t0:t0 + 128],
                                  wsb[:, c, half * 512:half * 512 + 512],
                                  start=(c == 0), stop=(c == NC - 1)),
                              waits=([wl_all, xn_ready,
                                      p.bank_guard[bk]] if c == 0 else ()),
                              inc="mm" if c == NC - 1 else None)
                    mmv = cx.val("mm")
                    cx.op("scalar", lambda e, dst=dst, half=half, func=func,
                          bk=bk: e.activation(
                              out=dst[:, half * 512:half * 512 + 512],
                              in_=p.pb[bk][:, 0:512], func=func),
                          waits=[("mm", mmv), ("da1s", cx.val("da1s"))])
                    p.bank_guard[bk] = ("sc", cx.val("sc"))
                    return ("sc", cx.val("sc"))
                return emit
            a_units = [qkv_unit(wk1, k_tm, AF.Silu, 0),
                       qkv_unit(wk1, k_tm, AF.Silu, 1),
                       qkv_unit(wq1, q_tm, AF.Silu, 0),
                       qkv_unit(wq1, q_tm, AF.Silu, 1),
                       qkv_unit(wv1, v_tm, AF.Identity, 0),
                       qkv_unit(wv1, v_tm, AF.Identity, 1)]

            if "B" not in stages:
                continue
            first = True
            for par in range(2):
                for half in range(2):
                    for si, srct in enumerate((ghi, glo)):
                        w_ = []
                        if first:
                            w_ = [lhilo, p.bank_guard[4], p.bank_guard[5]]
                            first = False
                        cx.op("tensor", lambda e, par=par, half=half,
                              srct=srct, si=si: e.matmul(
                                  p.pb[4 + half][par * 64:par * 64 + 64,
                                                 0:512],
                                  p.ut64[par * 64:par * 64 + 64, :],
                                  srct[par * 64:par * 64 + 64,
                                       half * 512:half * 512 + 512],
                                  start=(si == 0), stop=(si == 1),
                                  tile_position=(par * 64, par * 64)),
                              waits=w_,
                              inc="mm" if (par == 1 and half == 1 and
                                           si == 1) else None)
            mmv = cx.val("mm")
            for half in range(2):
                cx.op("scalar", lambda e, half=half: e.activation(
                    out=lam[:, half * 512:half * 512 + 512],
                    in_=p.pb[4 + half][:, 0:512], func=AF.Exp, scale=-1.0),
                    waits=[("mm", mmv)] if half == 0 else ())
                cx.op("scalar", lambda e, half=half: e.activation(
                    out=ivl[:, half * 512:half * 512 + 512],
                    in_=p.pb[4 + half][:, 0:512], func=AF.Exp))
            cum_done = cx.val("sc")
            p.bank_guard[4] = ("sc", cum_done)
            p.bank_guard[5] = ("sc", cum_done)
            a_units[0]()

            for par in range(2):
                mmv0 = None
                for half in range(2):
                    bk = 6 + half
                    for si, srct in enumerate((ghi, glo)):
                        w_ = []
                        if half == 0 and si == 0:
                            w_ = [lhilo, p.bank_guard[6], p.bank_guard[7]]
                        cx.op("tensor", lambda e, par=par, half=half,
                              srct=srct, si=si, bk=bk: e.matmul(
                                  p.pb[bk][0:1, 0:512],
                                  p.ones[par * 64:par * 64 + 64, 0:1],
                                  srct[par * 64:par * 64 + 64,
                                       half * 512:half * 512 + 512],
                                  start=(si == 0), stop=(si == 1),
                                  tile_position=(par * 64, 0)),
                              waits=w_,
                              inc="mm" if (half == 1 and si == 1) else None)
                mmv = cx.val("mm")
                for half in range(2):
                    bk = 6 + half
                    off = par * 1024 + half * 512
                    cx.op("scalar", lambda e, bk=bk, off=off: e.copy(
                        out=totf[0:1, off:off + 512],
                        in_=p.pb[bk][0:1, 0:512]),
                        waits=[("mm", mmv)] if half == 0 else ())
                p.bank_guard[6] = p.bank_guard[7] = ("sc", cx.val("sc"))
            tot_done = cx.val("sc")
            k_done = a_units[1]()

            swrows = (f32a[0:1, 0:1024], f32b[0:1, 0:1024])
            for par in range(2):
                cx.op("scalar", lambda e, par=par: e.copy(
                    out=swrows[par].rearrange("p (g q) -> p g q", g=8)
                    [:, :, 0:64],
                    in_=totf[0:1, par * 1024:(par + 1) * 1024].rearrange(
                        "p (g q) -> p g q", g=8)[:, :, 64:128]),
                    waits=[("sc", tot_done)] if par == 0 else ())
                cx.op("scalar", lambda e, par=par: e.copy(
                    out=swrows[par].rearrange("p (g q) -> p g q", g=8)
                    [:, :, 64:128],
                    in_=totf[0:1, par * 1024:(par + 1) * 1024].rearrange(
                        "p (g q) -> p g q", g=8)[:, :, 0:64]))
            swp_done = cx.val("sc")
            slots = (trow, ghi[0:1, 0:1024], glo[0:1, 0:1024],
                     Mqt[0:1, 0:1024])
            rounds = ((0, 0), (0, 1), (1, 0), (1, 1))
            for r, (par, half) in enumerate(rounds):
                off = par * 1024 + half * 512
                cx.op("scalar", lambda e, r=r, off=off: e.copy(
                    out=slots[r][0:1, 0:512], in_=totf[0:1, off:off + 512]))
                scv = cx.val("sc")
                cx.op("vector", lambda e, r=r, off=off: e.tensor_sub(
                    out=slots[r][0:1, 512:1024],
                    in0=totf[0:1, off:off + 512],
                    in1=slots[r][0:1, 0:512]), waits=[("sc", scv)])
            subs_done = cx.val("ev")
            first = True
            for par in range(2):
                for g in range(8):
                    base = par * 1024 + g * 128
                    colA = par * 8 + g
                    colB = 16 + par * 8 + g
                    w_ = []
                    if first:
                        w_ = [("sc", swp_done), p.bank_guard[4]]
                        first = False
                    cx.op("tensor", lambda e, base=base, colA=colA:
                          e.transpose(
                              p.pb[4][:, colA:colA + 1],
                              totf[0:1, base:base + 128],
                              p.identf[0:1, 0:1]),
                          waits=w_)
                    cx.op("tensor", lambda e, par=par, g=g, colB=colB:
                          e.transpose(
                              p.pb[4][:, colB:colB + 1],
                              swrows[par][0:1, g * 128:g * 128 + 128],
                              p.identf[0:1, 0:1]),
                          inc="mm")
            mmv_t = cx.val("mm")
            a_units[2]()
            for r, (par, half) in enumerate(rounds):
                bk = 6 + half
                for si in range(2):
                    w_ = ([("ev", subs_done), p.bank_guard[6],
                           p.bank_guard[7]] if r == 0 and si == 0 else ())
                    cx.op("tensor", lambda e, r=r, par=par, bk=bk, si=si:
                          e.matmul(
                              p.pb[bk][par * 64:par * 64 + 64, 0:512],
                              p.ones[0:1, 0:64],
                              slots[r][0:1, si * 512:si * 512 + 512],
                              start=(si == 0), stop=(si == 1),
                              tile_position=(0, par * 64)),
                          waits=w_,
                          inc="mm" if (r == 3 and si == 1) else None)
            mmv_b = cx.val("mm")
            q_done = a_units[3]()
            cx.op("scalar", lambda e: e.activation(
                out=lamc, in_=p.pb[4][:, 0:32], func=AF.Exp, scale=-1.0),
                waits=[("mm", mmv_t)])
            lamc_done = ("sc", cx.val("sc"))
            p.bank_guard[4] = lamc_done
            for r, (par, half) in enumerate(rounds):
                cx.op("scalar", lambda e, par=par, half=half: e.activation(
                    out=f32a[par * 64:par * 64 + 64,
                             half * 512:half * 512 + 512],
                    in_=p.pb[6 + half][par * 64:par * 64 + 64, 0:512],
                    func=AF.Exp, scale=-1.0),
                    waits=[("mm", mmv_b)] if r == 0 else ())
            b2_done = cx.val("sc")
            p.bank_guard[6] = p.bank_guard[7] = ("sc", b2_done)
            a_units[4]()
            v_done = a_units[5]()

            if "C" not in stages:
                continue
            cx.op("vector", lambda e: e.tensor_mul(out=f32b, in0=k_tm,
                                                   in1=k_tm),
                  waits=[("mm", mmv_t), k_done])
            cx.op("vector", lambda e: e.tensor_reduce(
                out=rnorm, in_=f32b.rearrange("p (h d) -> p h d", h=16),
                axis=mybir.AxisListType.X, op=ALU.add))
            cx.op("vector", lambda e: e.tensor_scalar(
                out=rnorm, in0=rnorm, scalar1=1e-24, scalar2=None,
                op0=ALU.max))
            evv = cx.val("ev")
            cx.op("scalar", lambda e: e.activation(out=rnorm, in_=rnorm,
                                                   func=AF.Ln),
                  waits=[("ev", evv)])
            cx.op("scalar", lambda e: e.activation(out=rnorm, in_=rnorm,
                                                   func=AF.Exp, scale=-0.5))
            scv = cx.val("sc")
            for h in range(16):
                cx.op("vector", lambda e, h=h: e.tensor_scalar(
                    out=ktn[:, h * 64:(h + 1) * 64],
                    in0=k_tm[:, h * 64:(h + 1) * 64],
                    scalar1=rnorm[:, h:h + 1], scalar2=None, op0=ALU.mult),
                    waits=[("sc", scv)] if h == 0 else ())
            cx.op("vector", lambda e: e.tensor_mul(out=kt_tm, in0=ktn,
                                                   in1=ivl))
            cx.op("vector", lambda e: e.tensor_mul(out=f32b, in0=ktn,
                                                   in1=esig),
                  waits=[("sc", cx.val("sc"))])
            cx.op("vector", lambda e: e.tensor_mul(out=bfs, in0=f32b,
                                                   in1=lam),
                  waits=[("mm", mmv_b)])
            for h in range(16):
                cx.op("vector", lambda e, h=h: e.tensor_scalar(
                    out=a_tm[:, h * 64:(h + 1) * 64],
                    in0=bfs[:, h * 64:(h + 1) * 64],
                    scalar1=gcol[:, h:h + 1], scalar2=None, op0=ALU.mult))
            cx.op("vector", lambda e: e.tensor_mul(out=qt_tm, in0=q_tm,
                                                   in1=lam), waits=[q_done])
            c1_done = ("ev", cx.val("ev"))
            lowrank_path(wog1, wog2, "og")

            cx.op("vector", lambda e: e.tensor_mul(out=rat, in0=ivl,
                                                   in1=f32a),
                  waits=[("sc", b2_done)])
            cx.op("vector", lambda e: e.tensor_mul(out=kh_tm, in0=ktn,
                                                   in1=rat),
                  waits=[c1_done])
            dec_done = cx.val("ev")
            dec_done_gp = ("gp", cx.val("gp"))

            if "D" not in stages:
                continue
            for src_t, dst_t in ((a_tm, a_fm), (kt_tm, ktfm), (qt_tm, qtfm)):
                for gb in range(2):
                    evv = cx.val("ev")
                    for j in range(4):
                        g = gb * 4 + j
                        cx.op("tensor", lambda e, g=g, j=j, src_t=src_t:
                              e.transpose(
                                  p.pb[6].bitcast(BF16)[:, j * 128:
                                                        j * 128 + 128],
                                  src_t[:, g * 128:g * 128 + 128], p.ident),
                              waits=([("ev", dec_done), dec_done_gp,
                                      ("ev", evv),
                                      p.bank_guard[6]] if j == 0 else ()),
                              inc="mm" if j == 3 else None)
                    mmv = cx.val("mm")
                    cx.op("vector", lambda e, gb=gb, dst_t=dst_t:
                          e.tensor_copy(
                              out=dst_t[:, gb * 512:gb * 512 + 512],
                              in_=p.pb[6].bitcast(BF16)[:, 0:512]),
                          waits=[("mm", mmv)])
                    p.bank_guard[6] = ("ev", cx.val("ev"))

            if "E" not in stages:
                continue
            def fmsl(t, h, par):
                g = h // 2
                return t[:, g * 128:(g + 1) * 128][hv(h):hv(h) + 64,
                                                   par * 64:par * 64 + 64]

            def mat_mm_fm(dst_pair_base, lhs_t, rhs_t, waits0):
                import os
                hs = range(0, 16, 2) if os.environ.get("L1E1MODE") == "heven"                     else range(16)
                prs = (0,) if os.environ.get("L1E1MODE") == "par0"                     else (0, 1)
                nmm = 0
                nmax = len(list(hs)) * len(prs) - 1
                for h in hs:
                    for par in prs:
                        w_ = list(waits0) if nmm == 0 else []
                        cx.op("tensor", lambda e, h=h, par=par, lhs_t=lhs_t,
                              rhs_t=rhs_t, b=dst_pair_base: e.matmul(
                                  PB[par * 64:par * 64 + 64,
                                     b + hc(h):b + hc(h) + 64],
                                  fmsl(lhs_t, h, par), fmsl(rhs_t, h, par),
                                  start=True, stop=True,
                                  tile_position=(hv(h), par * 64)),
                              waits=w_, inc="mm" if nmm == nmax else None)
                        nmm += 1
                return cx.val("mm")

            def mat_mm_tm(dst_pair_base, lhs_t, rhs_t, waits0):
                nmm = 0
                for h in range(16):
                    for par in range(2):
                        w_ = list(waits0) if nmm == 0 else []
                        cx.op("tensor", lambda e, h=h, par=par, lhs_t=lhs_t,
                              rhs_t=rhs_t, b=dst_pair_base: e.matmul(
                                  PB[par * 64:par * 64 + 64,
                                     b + hc(h):b + hc(h) + 64],
                                  lhs_t[par * 64:par * 64 + 64,
                                        hc(h):hc(h) + 64],
                                  rhs_t[par * 64:par * 64 + 64,
                                        hc(h):hc(h) + 64],
                                  start=True, stop=True,
                                  tile_position=(par * 64, par * 64)),
                              waits=w_, inc="mm" if nmm == 31 else None)
                        nmm += 1
                return cx.val("mm")

            def mask_evac(dst_sb, mask, base):
                import os
                mmv = cx.val("mm")
                if os.environ.get("L1E1MODE") == "nomask":
                    cx.op("vector", lambda e, dst_sb=dst_sb, base=base:
                          e.tensor_copy(out=dst_sb[:, 0:512],
                                        in_=PB[:, base:base + 512]),
                          waits=[("mm", mmv)])
                    cx.op("vector", lambda e, dst_sb=dst_sb, base=base:
                          e.tensor_copy(out=dst_sb[:, 512:1024],
                                        in_=PB[:, base + 512:base + 1024]))
                    return cx.val("ev")
                cx.op("vector", lambda e, dst_sb=dst_sb, mask=mask,
                      base=base: e.tensor_mul(
                          out=dst_sb[:, 0:512], in0=PB[:, base:base + 512],
                          in1=mask), waits=[("mm", mmv)])
                cx.op("vector", lambda e, dst_sb=dst_sb, mask=mask,
                      base=base: e.tensor_mul(
                          out=dst_sb[:, 512:1024],
                          in0=PB[:, base + 512:base + 1024], in1=mask))
                return cx.val("ev")

            mat_mm_fm(0, a_fm, ktfm,
                      [p.bank_guard[0], p.bank_guard[1], ("ev", cx.val("ev"))])
            ev_ = mask_evac(Bm, MSL, 0)
            p.bank_guard[0] = p.bank_guard[1] = ("ev", ev_)
            if "1" in stages and "2" not in stages:
                continue
            mat_mm_fm(1024, ktfm, a_fm, [p.bank_guard[2], p.bank_guard[3]])
            ev_ = mask_evac(Cm, MSU, 1024)
            p.bank_guard[2] = p.bank_guard[3] = ("ev", ev_)
            mat_mm_fm(0, ktfm, qtfm, [p.bank_guard[0], p.bank_guard[1],
                                      ("ev", cx.val("ev"))])
            ev_ = mask_evac(Mqt, MIU, 0)
            p.bank_guard[0] = p.bank_guard[1] = ("ev", ev_)
            if "2" in stages and "3" not in stages:
                continue

            cx.op("vector", lambda e: e.tensor_add(out=Rmat, in0=Bm,
                                                   in1=p.i64b))
            cx.op("vector", lambda e: e.tensor_add(out=Rp, in0=Cm,
                                                   in1=p.i64b))
            r0_done = cx.val("ev")

            def upd_mms(first_waits, with_r):
                nmm = 0
                nmax = (64 if with_r else 32) - 1
                pairs = ([(0, Cm, Rmat)] if with_r else []) + [(2, Rmat, Cm)]
                for pi, (bb, lhs_t, rhs_t) in enumerate(pairs):
                    for h in range(16):
                        for par in range(2):
                            w_ = first_waits if nmm == 0 else []
                            cx.op("tensor", lambda e, h=h, par=par, bb=bb,
                                  lhs_t=lhs_t, rhs_t=rhs_t: e.matmul(
                                      p.pb4[bb + h % 2][
                                          par * 64:par * 64 + 64,
                                          (h // 2) * 64:(h // 2) * 64 + 64],
                                      lhs_t[par * 64:par * 64 + 64,
                                            hc(h):hc(h) + 64],
                                      rhs_t[par * 64:par * 64 + 64,
                                            hc(h):hc(h) + 64],
                                      start=True, stop=True,
                                      tile_position=(par * 64, par * 64)),
                                  waits=w_, inc="mm" if nmm == nmax else None)
                            nmm += 1
                return cx.val("mm")

            def bc_evac(mmv):
                cx.op("vector", lambda e: e.tensor_copy(
                    out=Bm[:, 0:512], in_=PB[:, 0:512]),
                    waits=[("mm", mmv)])
                cx.op("scalar", lambda e: e.copy(
                    out=Bm[:, 512:1024], in_=PB[:, 512:1024]),
                    waits=[("mm", mmv)])
                cx.op("vector", lambda e: e.tensor_copy(
                    out=Cm[:, 0:512], in_=PB[:, 1024:1536]))
                cx.op("scalar", lambda e: e.copy(
                    out=Cm[:, 512:1024], in_=PB[:, 1536:2048]))
                evv, scv = cx.val("ev"), cx.val("sc")
                p.bank_guard[0] = p.bank_guard[2] = ("ev", evv)
                p.bank_guard[1] = p.bank_guard[3] = ("sc", scv)

            def r_evac(mmv, with_r):
                if with_r:
                    cx.op("vector", lambda e: e.tensor_add(
                        out=Rmat[:, 0:512], in0=Rmat[:, 0:512],
                        in1=p.pb4[0][:, 0:512]), waits=[("mm", mmv)])
                    cx.op("vector", lambda e: e.tensor_add(
                        out=Rmat[:, 512:1024], in0=Rmat[:, 512:1024],
                        in1=p.pb4[1][:, 0:512]))
                cx.op("vector", lambda e: e.tensor_add(
                    out=Rp[:, 0:512], in0=Rp[:, 0:512],
                    in1=p.pb4[2][:, 0:512]),
                    waits=[] if with_r else [("mm", mmv)])
                cx.op("vector", lambda e: e.tensor_add(
                    out=Rp[:, 512:1024], in0=Rp[:, 512:1024],
                    in1=p.pb4[3][:, 0:512]))
                evv = cx.val("ev")
                for b in range(4, 8):
                    p.bank_guard[b] = ("ev", evv)

            mat_mm_tm(0, Cm, Bm, [p.bank_guard[0], p.bank_guard[1],
                                  p.bank_guard[2], p.bank_guard[3],
                                  ("ev", cx.val("ev")),
                                  ("sc", cx.val("sc"))])
            mat_mm_tm(1024, Bm, Cm, [])
            bc_evac(cx.val("mm"))
            for s in range(1, 6):
                with_r = (s < 5)
                upd_mms([p.bank_guard[4], p.bank_guard[5],
                         p.bank_guard[6], p.bank_guard[7],
                         p.bank_guard[0], p.bank_guard[1],
                         p.bank_guard[2], p.bank_guard[3],
                         ("ev", r0_done)], with_r)
                if s < 5:
                    mat_mm_tm(0, Cm, Bm, [])
                    mat_mm_tm(1024, Bm, Cm, [])
                mmv = cx.val("mm")
                r_evac(mmv, with_r)
                if s < 5:
                    bc_evac(mmv)

            if "F" not in stages:
                continue
            cx.op("scalar", lambda e: e.activation(
                out=f32a, in_=og_tm, func=AF.Exp, scale=-1.0),
                waits=[("ev", cx.val("ev"))])
            scv = cx.val("sc")
            cx.op("vector", lambda e: e.tensor_scalar(
                out=f32a, in0=f32a, scalar1=1.0, scalar2=None, op0=ALU.add),
                waits=[("sc", scv)])
            evv = cx.val("ev")
            cx.op("scalar", lambda e: e.activation(
                out=f32a, in_=f32a, func=AF.Ln), waits=[("ev", evv)])
            cx.op("scalar", lambda e: e.activation(
                out=f32a, in_=f32a, func=AF.Exp, scale=-1.0))
            sig_done = ("sc", cx.val("sc"))

            def g_ln(waits0):
                sl = slice(0, 128)
                cx.op("vector", lambda e, sl=sl: e.tensor_mul(
                    out=f32b[sl, :], in0=o_tm[sl, :], in1=f32a[sl, :]),
                    waits=list(waits0) + [sig_done])
                cx.op("vector", lambda e, sl=sl: e.tensor_reduce(
                    out=col1[sl, :], in_=f32b[sl, :],
                    axis=mybir.AxisListType.X, op=ALU.add))
                cx.op("vector", lambda e, sl=sl: e.tensor_scalar(
                    out=col1[sl, :], in0=col1[sl, :], scalar1=1.0 / D,
                    scalar2=None, op0=ALU.mult))
                cx.op("vector", lambda e, sl=sl: e.tensor_scalar(
                    out=f32b[sl, :], in0=f32b[sl, :], scalar1=col1[sl, :],
                    scalar2=None, op0=ALU.subtract))
                evv = cx.val("ev")
                cx.op("scalar", lambda e, sl=sl: e.activation(
                    out=f32a[sl, :], in_=f32b[sl, :], func=AF.Square),
                    waits=[("ev", evv)])
                scv = cx.val("sc")
                cx.op("vector", lambda e, sl=sl: e.tensor_reduce(
                    out=col1[sl, :], in_=f32a[sl, :],
                    axis=mybir.AxisListType.X, op=ALU.add),
                    waits=[("sc", scv)])
                cx.op("vector", lambda e, sl=sl: e.tensor_scalar(
                    out=col1[sl, :], in0=col1[sl, :], scalar1=1.0 / D,
                    scalar2=None, op0=ALU.mult))
                evv = cx.val("ev")
                cx.op("scalar", lambda e, sl=sl: e.activation(
                    out=col1[sl, :], in_=col1[sl, :], func=AF.Ln,
                    bias=self.eps_col[sl, :]), waits=[("ev", evv)])
                cx.op("scalar", lambda e, sl=sl: e.activation(
                    out=col1[sl, :], in_=col1[sl, :], func=AF.Exp,
                    scale=-0.5))
                scv = cx.val("sc")
                cx.op("vector", lambda e, sl=sl: e.tensor_scalar(
                    out=f32b[sl, :], in0=f32b[sl, :], scalar1=col1[sl, :],
                    scalar2=None, op0=ALU.mult), waits=[("sc", scv)])
                cx.op("vector", lambda e, sl=sl: e.tensor_mul(
                    out=obuf[sl, :], in0=f32b[sl, :], in1=nwb[sl, :]))
                return ("ev", cx.val("ev"))

            g_done = [None, None]
            for par in range(2):
                evv0 = cx.val("ev")
                scv0 = cx.val("sc")
                sdec_thunks = []
                first_v = [True]
                first_s = [True]
                for h in range(16):
                    for cpy in range(2):
                        colx = par * 8 + h // 2 + (0 if cpy == (h % 2)
                                                   else 16)
                        if (h % 2) == 0:
                            def th(h=h, cpy=cpy, colx=colx):
                                w_ = ([("ev", evv0), ("sc", scv0)]
                                      if first_v[0] else ())
                                first_v[0] = False
                                cx.op("vector", lambda e, h=h, cpy=cpy,
                                      colx=colx: e.tensor_scalar(
                                          out=sdec[cpy * 64:cpy * 64 + 64,
                                                   hc(h):hc(h) + 64],
                                          in0=s0[cpy * 64:cpy * 64 + 64,
                                                 hc(h):hc(h) + 64],
                                          scalar1=lamc[cpy * 64:
                                                       cpy * 64 + 64,
                                                       colx:colx + 1],
                                          scalar2=None, op0=ALU.mult),
                                      waits=w_)
                        else:
                            def th(h=h, cpy=cpy, colx=colx):
                                w_ = ([("ev", evv0), ("sc", scv0)]
                                      if first_s[0] else ())
                                first_s[0] = False
                                cx.op("scalar", lambda e, h=h, cpy=cpy,
                                      colx=colx: e.activation(
                                          out=sdec[cpy * 64:cpy * 64 + 64,
                                                   hc(h):hc(h) + 64],
                                          in_=s0[cpy * 64:cpy * 64 + 64,
                                                 hc(h):hc(h) + 64],
                                          func=AF.Copy,
                                          scale=lamc[cpy * 64:cpy * 64 + 64,
                                                     colx:colx + 1]),
                                      waits=w_)
                        sdec_thunks.append(th)

                def sdec_drain(k):
                    while sdec_thunks and k > 0:
                        sdec_thunks.pop(0)()
                        k -= 1

                nmm = 0
                for h in range(16):
                    w_ = ([p.bank_guard[4], p.bank_guard[5],
                           ("ev", cx.val("ev"))] if nmm == 0 else [])
                    cx.op("tensor", lambda e, h=h, par=par: e.matmul(
                        p.pb4[h % 2][par * 64:par * 64 + 64,
                                     (h // 2) * 64:(h // 2) * 64 + 64],
                        fmsl(a_fm, h, par),
                        s0[hv(h):hv(h) + 64, hc(h):hc(h) + 64],
                        start=True, stop=True,
                        tile_position=(hv(h), par * 64)),
                        waits=w_, inc="mm" if nmm == 15 else None)
                    nmm += 1
                mmv = cx.val("mm")
                cx.op("vector", lambda e, par=par: e.tensor_add(
                    out=rhs_sb[par * 64:par * 64 + 64, 0:512],
                    in0=p.pb4[0][par * 64:par * 64 + 64, 0:512],
                    in1=v_tm[par * 64:par * 64 + 64, 0:512]),
                    waits=[("mm", mmv), v_done])
                cx.op("vector", lambda e, par=par: e.tensor_add(
                    out=rhs_sb[par * 64:par * 64 + 64, 512:1024],
                    in0=p.pb4[1][par * 64:par * 64 + 64, 0:512],
                    in1=v_tm[par * 64:par * 64 + 64, 512:1024]))
                rhs_done = cx.val("ev")
                p.bank_guard[4] = p.bank_guard[5] = ("ev", rhs_done)
                sdec_drain(12)

                nmm = 0
                for h in range(16):
                    w_ = ([p.bank_guard[6], p.bank_guard[7],
                           ("ev", rhs_done)] if nmm == 0 else [])
                    cx.op("tensor", lambda e, h=h, par=par: e.matmul(
                        p.pb4[2 + h % 2][par * 64:par * 64 + 64,
                                         (h // 2) * 64:(h // 2) * 64 + 64],
                        Tt[par * 64:par * 64 + 64, hc(h):hc(h) + 64],
                        rhs_sb[par * 64:par * 64 + 64,
                               hc(h):hc(h) + 64],
                        start=True, stop=True,
                        tile_position=(par * 64, par * 64)),
                        waits=w_, inc="mm" if nmm == 15 else None)
                    nmm += 1
                mmv = cx.val("mm")
                cx.op("vector", lambda e, par=par: e.tensor_copy(
                    out=u_sb[par * 64:par * 64 + 64, 0:512],
                    in_=p.pb4[2][par * 64:par * 64 + 64, 0:512]),
                    waits=[("mm", mmv)])
                cx.op("scalar", lambda e, par=par: e.copy(
                    out=u_sb[par * 64:par * 64 + 64, 512:1024],
                    in_=p.pb4[3][par * 64:par * 64 + 64, 0:512]),
                    waits=[("mm", mmv)])
                u_done = cx.val("ev")
                u_done_s = cx.val("sc")
                p.bank_guard[6] = ("ev", u_done)
                p.bank_guard[7] = ("sc", u_done_s)
                sdec_drain(12)

                nmm = 0
                for h in range(16):
                    w_ = ([p.bank_guard[4], p.bank_guard[5],
                           p.bank_guard[6], p.bank_guard[7],
                           ("ev", u_done), ("sc", u_done_s)]
                          if nmm == 0 else [])
                    cx.op("tensor", lambda e, h=h, par=par: e.matmul(
                        p.pb4[h % 2][par * 64:par * 64 + 64,
                                     (h // 2) * 64:(h // 2) * 64 + 64],
                        fmsl(qtfm, h, par),
                        s0[hv(h):hv(h) + 64, hc(h):hc(h) + 64],
                        start=True, stop=True,
                        tile_position=(hv(h), par * 64)),
                        waits=w_)
                    cx.op("tensor", lambda e, h=h, par=par: e.matmul(
                        p.pb4[2 + h % 2][par * 64:par * 64 + 64,
                                         (h // 2) * 64:(h // 2) * 64 + 64],
                        Mqt[par * 64:par * 64 + 64, hc(h):hc(h) + 64],
                        u_sb[par * 64:par * 64 + 64, hc(h):hc(h) + 64],
                        start=True, stop=True,
                        tile_position=(par * 64, par * 64)),
                        inc="mm" if nmm == 15 else None)
                    nmm += 1
                mmv = cx.val("mm")
                cx.op("scalar", lambda e, par=par: e.copy(
                    out=o_tm[par * 64:par * 64 + 64, 0:512],
                    in_=p.pb4[0][par * 64:par * 64 + 64, 0:512]),
                    waits=[("mm", mmv)])
                cx.op("scalar", lambda e, par=par: e.copy(
                    out=o_tm[par * 64:par * 64 + 64, 512:1024],
                    in_=p.pb4[1][par * 64:par * 64 + 64, 0:512]))
                scv_o = cx.val("sc")
                cx.op("vector", lambda e, par=par: e.tensor_add(
                    out=o_tm[par * 64:par * 64 + 64, 0:512],
                    in0=o_tm[par * 64:par * 64 + 64, 0:512],
                    in1=p.pb4[2][par * 64:par * 64 + 64, 0:512]),
                    waits=[("sc", scv_o)])
                cx.op("vector", lambda e, par=par: e.tensor_add(
                    out=o_tm[par * 64:par * 64 + 64, 512:1024],
                    in0=o_tm[par * 64:par * 64 + 64, 512:1024],
                    in1=p.pb4[3][par * 64:par * 64 + 64, 0:512]))
                o_done = cx.val("ev")
                p.bank_guard[4] = p.bank_guard[5] = ("sc", scv_o)
                p.bank_guard[6] = p.bank_guard[7] = ("ev", o_done)
                sdec_drain(99)
                sdec_done = [("ev", cx.val("ev")), ("sc", cx.val("sc"))]

                nmm = 0
                for h in range(16):
                    for cpy in range(2):
                        w_ = ([p.bank_guard[6], p.bank_guard[7],
                               ("ev", u_done), ("sc", u_done_s)]
                              if nmm == 0 else [])
                        cx.op("tensor", lambda e, h=h, par=par, cpy=cpy:
                              e.matmul(
                                  p.pb4[2 + h % 2][cpy * 64:cpy * 64 + 64,
                                                   (h // 2) * 64:
                                                   (h // 2) * 64 + 64],
                                  kh_tm[par * 64:par * 64 + 64,
                                        h * 64:h * 64 + 64],
                                  u_sb[par * 64:par * 64 + 64,
                                       hc(h):hc(h) + 64],
                                  start=True, stop=True,
                                  tile_position=(par * 64, cpy * 64)),
                              waits=w_, inc="mm" if nmm == 31 else None)
                        nmm += 1
                mmv = cx.val("mm")
                cx.op("vector", lambda e: e.tensor_add(
                    out=s0[:, 0:512], in0=sdec[:, 0:512],
                    in1=p.pb4[2][:, 0:512]),
                    waits=[("mm", mmv)] + sdec_done)
                cx.op("vector", lambda e: e.tensor_add(
                    out=s0[:, 512:1024], in0=sdec[:, 512:1024],
                    in1=p.pb4[3][:, 0:512]))
                s_done = cx.val("ev")
                p.bank_guard[6] = p.bank_guard[7] = ("ev", s_done)
                g_done[par] = ("ev", o_done)

            if "G" not in stages:
                continue
            g_ln([g_done[1]])
            oln_done = cx.val("ev")

            for gb in range(2):
                evv = cx.val("ev")
                for j in range(4):
                    g = gb * 4 + j
                    cx.op("tensor", lambda e, g=g, j=j: e.transpose(
                        p.pb[6].bitcast(BF16)[:, j * 128:j * 128 + 128],
                        obuf[:, g * 128:g * 128 + 128], p.ident),
                        waits=([("ev", oln_done), ("ev", evv),
                                p.bank_guard[6]] if j == 0 else ()),
                        inc="mm" if j == 3 else None)
                mmv = cx.val("mm")
                cx.op("vector", lambda e, gb=gb: e.tensor_copy(
                    out=ofm[:, gb * 512:gb * 512 + 512],
                    in_=p.pb[6].bitcast(BF16)[:, 0:512]),
                    waits=[("mm", mmv)])
                p.bank_guard[6] = ("ev", cx.val("ev"))
            ofm_done = cx.val("ev")

            nmm = 0
            for oc in range(8):
                bk = oc // 4
                col = (oc % 4) * 128
                for dc in range(NC):
                    w_ = []
                    if nmm == 0:
                        w_ = [p.bank_guard[0], p.bank_guard[1],
                              ("ev", ofm_done)]
                    cx.op("tensor", lambda e, oc=oc, dc=dc, bk=bk, col=col:
                          e.matmul(
                              p.pb[bk][:, col:col + 128],
                              wo1[:, dc, oc * 128:oc * 128 + 128],
                              ofm[:, dc * 128:dc * 128 + 128],
                              start=(dc == 0), stop=(dc == NC - 1)),
                          waits=w_, inc="mm" if dc == NC - 1 else None)
                    nmm += 1
            mmv = cx.val("mm")
            cx.op("vector", lambda e: e.tensor_copy(
                out=dsb[:, 0:512], in_=PB[:, 0:512]), waits=[("mm", mmv)])
            cx.op("vector", lambda e: e.tensor_copy(
                out=dsb[:, 512:1024], in_=PB[:, 512:1024]))
            d_done = cx.val("ev")
            p.bank_guard[0] = p.bank_guard[1] = ("ev", d_done)
            cx.dma("sync", p.da1_dram[:, t0:t0 + 128].rearrange(
                "(c p) t -> p c t", p=128),
                dsb.rearrange("p (c t) -> p c t", c=NC),
                waits=[("ev", d_done)], inc="da1s")

        p.prelease()

    def emit_l1_ffn_residual(self):
        cx = self.ctx
        p = self
        TW = 512
        p.pmark()
        al = p.palloc
        mu = al(NC * 4 * 2, F32, NC * 2)
        om = al(NC * 4 * 2, F32, NC * 2)
        xr = al(NC * TW * 2, BF16, NC * TW).rearrange("p (c t) -> p c t", c=NC)
        xk = al(NC * TW * 2, BF16, NC * TW).rearrange("p (c t) -> p c t", c=NC)
        rsb = al(NC * TW * 2, BF16, NC * TW).rearrange("p (c t) -> p c t", c=NC)
        kact = al(NC * TW * 2, BF16, NC * TW).rearrange("p (c t) -> p c t", c=NC)
        x1s = al(NC * TW * 4, F32, NC * TW).rearrange("p (c t) -> p c t", c=NC)
        das = al(NC * TW * 2, BF16, NC * TW).rearrange("p (c t) -> p c t", c=NC)
        wb = [al(NC * 128 * 2, BF16, NC * 128).rearrange(
            "p (c o) -> p c o", c=NC) for _ in range(2)]
        fsc = al(TW * 4, F32, TW)

        cx.dma("sync", mu[:, 0:NC], p.di["mu_r"], inc="tbl")
        cx.dma("sync", mu[:, NC:2 * NC], p.di["mu_k"], inc="tbl")
        muv = ("tbl", cx.val("tbl"))
        cx.op("vector", lambda e: e.tensor_scalar(
            out=om, in0=mu, scalar1=-1.0, scalar2=1.0,
            op0=ALU.mult, op1=ALU.add), waits=[muv])
        self.w4_guard = [("mm", 0), ("mm", 0)]
        xn_ready = ("ev", cx.val("ev"))

        for ti in range(N // TW):
            t0 = ti * TW
            lds = f"ldx{ti % 2}"
            cx.dma("sync", x1s, p.x1_dram[:, t0:t0 + TW].rearrange(
                "(c p) t -> p c t", p=128),
                waits=[("ev", cx.val("ev")), ("st", cx.val("st"))], inc=lds)
            cx.dma("sync", das, p.da1_dram[:, t0:t0 + TW].rearrange(
                "(c p) t -> p c t", p=128),
                waits=[("da1s", cx.val("da1s"))], inc=lds)
            dav = (lds, cx.val(lds))
            x1v = dav

            for dst, mcol in ((xr, 0), (xk, NC)):
                for c in range(NC):
                    w_ = [xn_ready] if (ti == 0 and c == 0 and mcol == 0)                         else []
                    cx.op("vector", lambda e, dst=dst, c=c, mcol=mcol, t0=t0:
                          e.tensor_scalar(
                              out=dst[:, c, :], in0=p.xn[:, c, t0:t0 + TW],
                              scalar1=mu[:, mcol + c:mcol + c + 1],
                              scalar2=None, op0=ALU.mult), waits=w_)
                    if ti == 0:
                        cx.op("vector", lambda e, dst=dst, c=c, mcol=mcol:
                              e.scalar_tensor_tensor(
                                  out=dst[:, c, 1:TW],
                                  in0=p.xn[:, c, 0:TW - 1],
                                  scalar=om[:, mcol + c:mcol + c + 1],
                                  in1=dst[:, c, 1:TW],
                                  op0=ALU.mult, op1=ALU.add))
                    else:
                        cx.op("vector", lambda e, dst=dst, c=c, mcol=mcol,
                              t0=t0: e.scalar_tensor_tensor(
                                  out=dst[:, c, :],
                                  in0=p.xn[:, c, t0 - 1:t0 + TW - 1],
                                  scalar=om[:, mcol + c:mcol + c + 1],
                                  in1=dst[:, c, :],
                                  op0=ALU.mult, op1=ALU.add))
            mix_done = cx.val("ev")

            for wname, src_t, dst, kind in (("WrT1", xr, rsb, "sig"),
                                            ("WkT1", xk, kact, "relu2")):
                for oc in range(8):
                    par = oc % 2
                    wsem = f"wl{par}"
                    cx.dma("gpsimd", wb[par],
                           p.di[wname][:, oc * 128:(oc + 1) * 128].rearrange(
                               "(c p) o -> p c o", p=128),
                           waits=[(wsem, cx.val(wsem)), self.w4_guard[par]],
                           inc=wsem)
                    wlv = (wsem, cx.val(wsem))
                    bk = oc % 2
                    for c in range(NC):
                        w_ = [wlv]
                        if c == 0:
                            w_ += [p.bank_guard[bk], ("ev", mix_done)]
                        cx.op("tensor", lambda e, oc=oc, c=c, par=par, bk=bk,
                              src_t=src_t: e.matmul(
                                  p.pb[bk][:, 0:TW], wb[par][:, c, :],
                                  src_t[:, c, :],
                                  start=(c == 0), stop=(c == NC - 1)),
                              waits=w_, inc="mm" if c == NC - 1 else None)
                    mmv = cx.val("mm")
                    self.w4_guard[par] = ("mm", mmv)
                    if kind == "sig":
                        cx.op("scalar", lambda e, oc=oc, dst=dst, bk=bk:
                              e.activation(out=dst[:, oc, :],
                                           in_=p.pb[bk][:, 0:TW],
                                           func=AF.Sigmoid),
                              waits=[("mm", mmv)])
                        p.bank_guard[bk] = ("sc", cx.val("sc"))
                    else:
                        cx.op("vector", lambda e, oc=oc, bk=bk:
                              e.tensor_scalar(
                                  out=fsc, in0=p.pb[bk][:, 0:TW],
                                  scalar1=0.0, scalar2=None, op0=ALU.max),
                              waits=[("mm", mmv), ("sc", cx.val("sc"))])
                        p.bank_guard[bk] = ("ev", cx.val("ev"))
                        evv = cx.val("ev")
                        cx.op("scalar", lambda e, oc=oc, dst=dst:
                              e.activation(out=dst[:, oc, :], in_=fsc,
                                           func=AF.Square),
                              waits=[("ev", evv)])

            for oc in range(8):
                par = oc % 2
                wsem = f"wl{par}"
                cx.dma("gpsimd", wb[par],
                       p.di["WvT1"][:, oc * 128:(oc + 1) * 128].rearrange(
                           "(c p) o -> p c o", p=128),
                       waits=[(wsem, cx.val(wsem)), self.w4_guard[par]],
                       inc=wsem)
                wlv = (wsem, cx.val(wsem))
                bk = oc % 2
                for c in range(NC):
                    w_ = [wlv]
                    if c == 0:
                        w_ += [p.bank_guard[bk], ("sc", cx.val("sc"))]
                    cx.op("tensor", lambda e, oc=oc, c=c, par=par, bk=bk:
                          e.matmul(
                              p.pb[bk][:, 0:TW], wb[par][:, c, :],
                              kact[:, c, :],
                              start=(c == 0), stop=(c == NC - 1)),
                          waits=w_, inc="mm" if c == NC - 1 else None)
                mmv = cx.val("mm")
                self.w4_guard[par] = ("mm", mmv)
                cx.op("vector", lambda e, oc=oc, bk=bk: e.tensor_mul(
                    out=fsc, in0=rsb[:, oc, :], in1=p.pb[bk][:, 0:TW]),
                    waits=[("mm", mmv), x1v, dav])
                p.bank_guard[bk] = ("ev", cx.val("ev"))
                cx.op("vector", lambda e, oc=oc: e.tensor_add(
                    out=fsc, in0=fsc, in1=das[:, oc, :]))
                cx.op("vector", lambda e, oc=oc: e.tensor_add(
                    out=x1s[:, oc, :], in0=x1s[:, oc, :], in1=fsc))
            y_done = cx.val("ev")
            cx.dma("sync", p.y[:, t0:t0 + TW].rearrange(
                "(c p) t -> p c t", p=128), x1s,
                waits=[("ev", y_done)], inc="st")
        p.prelease()



def build_program(dbg=(), phases=4):
    nc = bass.Bass("TRN2", target_bir_lowering=False, debug=False)
    p = Prog(nc, dbg=dbg)
    cx = p.ctx

    p.load_consts()

    TW = 512
    p.pmark()
    x_stage = [p.palloc(NC * TW * 4, F32, NC * TW).rearrange(
        "p (c t) -> p c t", c=NC) for _ in range(2)]
    scratch_bf = p.palloc(2 * NC * TW * 2, BF16, 2 * NC * TW)
    scratch_f32 = p.palloc(TW * 4, F32, TW)
    rows_f32 = p.palloc(3 * TW * 4, F32, 3 * TW)[0:1, :]
    rows_bf = p.palloc(2 * TW * 2, BF16, 2 * TW)[0:1, :]
    wr_f = p.palloc(NC * 8 * 4, F32, NC * 8).rearrange(
        "p (c o) -> p c o", c=NC)
    ncol = p.palloc(4, F32, 1)
    lrow = p.palloc(TW * 4, F32, TW)[0:8, :]

    cx.dma("sync", wr_f, p.di["Wr"].rearrange("(c p) o -> p c o", p=128),
           inc="wrf")
    wr_ready = cx.val("wrf")
    for c in range(NC):
        cx.op("tensor", lambda e, c=c: e.matmul(
            p.pb[5][0:1, 0:8], p.one_col, wr_f[:, c, :],
            start=(c == 0), stop=(c == NC - 1)),
            waits=[("wrf", wr_ready), p.sconst_ready] if c == 0 else (),
            inc="mm" if c == NC - 1 else None)
    cx.op("scalar", lambda e: e.copy(out=lrow[0:1, 0:8],
                                     in_=p.pb[5][0:1, 0:8]),
          waits=[("mm", cx.val("mm"))])
    scv_cs = cx.val("sc")
    cx.op("tensor", lambda e: e.transpose(
        p.pb[6][0:8, 0:1], lrow[0:1, 0:8], p.identf[0:1, 0:1]),
        waits=[("sc", scv_cs)], inc="mm")
    cx.op("scalar", lambda e: e.mul(out=ncol[0:8, :], in_=p.pb[6][0:8, 0:1],
                                    mul=-1.0), waits=[("mm", cx.val("mm"))])
    ncol_ready = cx.val("sc")

    for ti in range(N // TW):
        t0 = ti * TW
        buf = x_stage[ti % 2]
        lds = f"ldx{ti % 2}"
        waits = [("ev", cx.val("ev")), ("sc", cx.val("sc"))] if ti >= 2             else [("ld", p.consts_ready)]
        cx.dma("sync", buf,
               p.di["x_fm"][:, t0:t0 + TW].rearrange("(c p) t -> p c t", p=128),
               waits=waits, inc=lds)
        b4 = (0, 1, 2, 3)
        p.emit_ln(p.xn[:, :, t0:t0 + TW], TW, buf, (lds, cx.val(lds)),
                  b4, scratch_bf, scratch_f32, rows_f32, rows_bf)
        for c in range(NC):
            cx.op("tensor", lambda e, c=c, buf=buf: e.matmul(
                p.pb[4][0:8, 0:TW], wr_f[:, c, :], buf[:, c, :],
                start=(c == 0), stop=(c == NC - 1)),
                waits=([("sc", ncol_ready), p.bank_guard[4]] if c == 0 else ()),
                inc="mm" if c == NC - 1 else None)
        mmv = cx.val("mm")
        cx.op("scalar", lambda e: e.copy(out=lrow, in_=p.pb[4][0:8, 0:TW]),
              waits=[("mm", mmv), ("lgs", cx.val("lgs"))])
        p.bank_guard[4] = ("sc", cx.val("sc"))
        cx.op("vector", lambda e: e.scalar_tensor_tensor(
            out=lrow, in0=p.pb[2][0:8, 0:TW], scalar=ncol[0:8, :], in1=lrow,
            op0=ALU.mult, op1=ALU.add), waits=[("sc", cx.val("sc"))])
        cx.op("vector", lambda e: e.tensor_mul(
            out=lrow, in0=lrow, in1=p.pb[3][0:8, 0:TW]))
        lgv = cx.val("ev")
        cx.dma("sync", p.logits_dram[:, t0:t0 + TW], lrow,
               waits=[("ev", lgv)], inc="lgs")
    p.prelease()

    if "xn0" in p.dbg:
        out = p.dbg_out("xn0", [128, NC, N], BF16)
        cx.dma("sync", out, p.xn, waits=[("ev", cx.val("ev"))], inc="st")

    p.emit_l0_attention()

    if "o0" in p.dbg:
        out = p.dbg_out("o0", [128, NC, N], BF16)
        cx.dma("sync", out, p.big, waits=[("ev", cx.val("ev")),
                                          ("mm", cx.val("mm"))], inc="st")

    p.pmark()
    p.m_act = p.palloc(16 * 512 * 2, BF16, 16 * 512).rearrange(
        "p (a b) -> p a b", a=16)
    act_flat = p.m_act.rearrange("p a b -> p (a b)")
    dfxs = p.palloc(4096 * 4, F32, 4096)
    p.m_df = dfxs[:, 0:2048].bitcast(BF16).rearrange("p (a b) -> p a b", a=NC)
    p.m_xs = dfxs[:, 2048:4096].rearrange("p (a b) -> p a b", a=NC)
    p.m_ring = [p.palloc(4096 * 2, BF16, 4096) for _ in range(3)]
    p.m_wbc = p.palloc(512 * 4, F32, 512)
    p.m_sab = p.palloc(512 * 2, BF16, 512)
    p.m_b1a = p.palloc(128 * 4, F32, 128)
    p.m_b1g = p.palloc(128 * 4, F32, 128)
    p.m_b2 = p.palloc(1024 * 2, BF16, 1024)[0:8, :]
    p.w_bf = p.palloc(N * 2, BF16, N)[0:8, :]
    p.r_wr = p.palloc(NC * 8 * 2, BF16, NC * 8).rearrange(
        "p (c o) -> p c o", c=NC)
    p.r_br = p.palloc(4, F32, 1)[0:8, :]
    p.r_logit = dfxs[0:8, 0:4096]
    p.r_ltm = act_flat[:, 0:512].bitcast(F32).rearrange(
        "p (a b) -> p a b", a=32)
    p.r_scr = act_flat[:, 512:1792].bitcast(F32)
    p.m_lnbf = act_flat[:, 0:4096]
    p.m_lnf = act_flat[:, 4096:4608].bitcast(F32)
    p.m_lrows = act_flat[:, 4608:6144].bitcast(F32)[0:1, :]
    p.m_lrbf = act_flat[:, 6144:6656][0:1, :]

    p.emit_router()
    p.emit_l0_moe_residual()
    p.prelease()

    if "x1" in p.dbg:
        out = p.dbg_out("x1", [D, N], F32)
        cx.dma("sync", out, p.x1_dram, waits=[("st", cx.val("st"))], inc="st")
    if "xn1" in p.dbg:
        out = p.dbg_out("xn1", [128, NC, N], BF16)
        cx.dma("sync", out, p.xn, waits=[("ev", cx.val("ev"))], inc="st")

    if phases >= 3:
        import os
        p.emit_l1_attention(
            ncp=int(os.environ.get("L1CP", "32")),
            stages=os.environ.get("L1STAGES", "ABCDEFG"))

    if "da1" in p.dbg and phases >= 3:
        out = p.dbg_out("da1", [D, N], BF16)
        cx.dma("sync", out, p.da1_dram,
               waits=[("da1s", cx.val("da1s"))], inc="st")

    if phases >= 4:
        p.emit_l1_ffn_residual()
    else:
        cx.dma("sync", p.y, p.x1_dram, waits=[("st", cx.val("st"))], inc="st")

    cx.op("sync", lambda e: e.nop(), waits=[("st", cx.val("st")),
                                            ("ev", cx.val("ev"))])
    cx.finalize()
    return nc, p



def kernel(**inputs):
    hp = host_prepare(inputs)
    nc, p = build_program()
    shared = {k: v for k, v in hp.items() if k != "x_fm"}
    in_maps = [dict(shared, x_fm=hp["x_fm"][b]) for b in range(8)]
    res = run_bass_kernel_spmd(nc, in_maps, core_ids=list(range(8)))
    out = np.empty((8, N, D), np.float32)
    for b in range(8):
        out[b] = np.asarray(res.results[b]["y"]).T
    return out


# version 25
# speedup vs baseline: 1.1628x; 1.1628x over previous
import numpy as np
import ml_dtypes
import concourse.bass as bass
import concourse.mybir as mybir
from concourse.bass_utils import run_bass_kernel_spmd

F32 = mybir.dt.float32
BF16 = mybir.dt.bfloat16
AF = mybir.ActivationFunctionType
ALU = mybir.AluOpType

D = 1024
N = 4096
NC = 8
H = 16
HD = 64
NBANK_W = 512

BF = np.dtype(ml_dtypes.bfloat16)



class _BankView:
    def __init__(self, p):
        self.p = p
    def __getitem__(self, i):
        if i < 4:
            return self.p.pb_s[:, i * 512:(i + 1) * 512]
        return self.p.pb4[i - 4]


class Ctx:

    ENGINES = ("tensor", "vector", "scalar", "gpsimd", "sync")

    def __init__(self, nc):
        self.nc = nc
        self.q = {e: [] for e in self.ENGINES}
        self.sem = {}
        self.count = {}

    def new_sem(self, name):
        self.sem[name] = None
        self.count[name] = 0
        return name

    def val(self, sem):
        return self.count[sem]

    AUTOSEM = {"scalar": "sc", "vector": "ev", "gpsimd": "gp"}

    def op(self, eng_name, fn, waits=(), inc=None, inc_n=1):
        waits = list(waits)
        auto = self.AUTOSEM.get(eng_name)
        if auto is not None:
            assert inc is None or inc == auto, \
                f"{eng_name} ops always signal via {auto}"
            waits.append((auto, self.count[auto]))
            inc, inc_n = auto, 1
        if inc is not None:
            self.count[inc] += inc_n
        waits = tuple(waits)

        def closure(eng, fn=fn, waits=waits, inc=inc, inc_n=inc_n):
            for s, v in waits:
                if v > 0:
                    eng.wait_ge(self.sem[s], v)
            instr = fn(eng)
            if inc is not None:
                assert instr is not None
                instr.then_inc(self.sem[inc], inc_n)

        self.q[eng_name].append(closure)

    def dma(self, eng_name, out, in_, waits=(), inc=None):
        if inc is not None:
            self.count[inc] += 16
        waits = tuple(waits)

        def closure(eng, out=out, in_=in_, waits=waits, inc=inc):
            for s, v in waits:
                if v > 0:
                    eng.wait_ge(self.sem[s], v)
            instr = eng.dma_start(out=out, in_=in_)
            if inc is not None:
                instr.then_inc(self.sem[inc], 16)

        self.q[eng_name].append(closure)

    def fence(self):
        snap = [(s, v) for s, v in self.count.items() if v > 0]
        for e in self.ENGINES:
            def closure(eng, snap=tuple(snap)):
                for s, v in snap:
                    eng.wait_ge(self.sem[s], v)
            self.q[e].append(closure)

    def barrier(self, name):
        tgt = {}
        for e in self.ENGINES:
            s = f"bar_{e}"
            if s not in self.sem:
                self.new_sem(s)
            self.count[s] += 1
            tgt[e] = self.count[s]

        for e in self.ENGINES:
            def closure(eng, e=e, tgt=dict(tgt)):
                if e == "sync":
                    eng.nop().then_inc(self.sem[f"bar_{e}"], 1)
                else:
                    eng.engine_nop().then_inc(self.sem[f"bar_{e}"], 1) \
                        if hasattr(eng, "engine_nop") else \
                        eng.nop().then_inc(self.sem[f"bar_{e}"], 1)
                for e2 in self.ENGINES:
                    eng.wait_ge(self.sem[f"bar_{e2}"], tgt[e2])
            self.q[e].append(closure)

    def finalize(self):
        nc = self.nc
        import contextlib
        with contextlib.ExitStack() as stack:
            for name in list(self.sem):
                self.sem[name] = stack.enter_context(nc.semaphore(name))
            blk = stack.enter_context(nc.Block())
            for eng_name in self.ENGINES:
                ops = self.q[eng_name]
                getattr(blk, eng_name)(lambda eng, ops=ops: [f(eng) for f in ops])



def _bf16(a):
    return np.ascontiguousarray(a.astype(BF))


def host_prepare(inputs):
    d = {}
    x = inputs["L0_attn_x"]
    d["x_fm"] = [np.ascontiguousarray(x[b].T) for b in range(8)]

    Wqkv = inputs["L0_attn_W_qkv"]
    WqkvT = np.ascontiguousarray(Wqkv.T)
    perm = np.empty(HD, np.int64)
    perm[:32] = np.arange(0, 64, 2)
    perm[32:] = np.arange(1, 64, 2)
    cols = np.arange(3 * D)
    for h in range(H):
        cols[h * HD:(h + 1) * HD] = h * HD + perm
        cols[D + h * HD:D + (h + 1) * HD] = D + h * HD + perm
    d["WqkvT"] = np.ascontiguousarray(WqkvT[:, cols])
    d["WprojT"] = np.ascontiguousarray(inputs["L0_attn_W_proj"].T)

    inv = 1.0 / (10000.0 ** (np.arange(0, HD, 2, np.float32) / HD))
    tt = np.arange(N, dtype=np.float32)
    fr = np.outer(tt, inv)
    c_fm = np.ascontiguousarray(np.cos(fr).T.astype(np.float32))
    s_fm = np.ascontiguousarray(np.sin(fr).T.astype(np.float32))
    sc = 1.0 / np.sqrt(HD)
    C1q = np.tile(c_fm * sc, (4, 1))
    C2q = np.concatenate([-s_fm * sc, s_fm * sc, -s_fm * sc, s_fm * sc], 0)
    d["rope_c1q"], d["rope_c2q"] = _bf16(C1q), _bf16(C2q)

    W1 = inputs["L0_ffn_W1"]
    d["W1a"] = np.ascontiguousarray(W1[:, :, :2048])
    d["W1g"] = np.ascontiguousarray(W1[:, :, 2048:])
    d["W2"] = np.ascontiguousarray(inputs["L0_ffn_W2"])
    b1 = inputs["L0_ffn_b1"]
    d["b1a"] = np.ascontiguousarray(
        b1[:, :2048].reshape(8, 16, 128).transpose(2, 0, 1).reshape(128, 128))
    d["b1g"] = np.ascontiguousarray(
        b1[:, 2048:].reshape(8, 16, 128).transpose(2, 0, 1).reshape(128, 128))
    d["b2"] = np.ascontiguousarray(inputs["L0_ffn_b2"])
    d["Wr"] = np.ascontiguousarray(inputs["L0_ffn_Wr"])
    d["br"] = np.ascontiguousarray(inputs["L0_ffn_br"])

    hcperm = np.empty(D, np.int64)
    for h in range(H):
        hcperm[(h % 2) * 512 + (h // 2) * 64:(h % 2) * 512 + (h // 2) * 64
               + HD] = np.arange(h * HD, (h + 1) * HD)
    for k in ("Wq", "Wk", "Wgamma", "Wf1", "Wf2", "Wog1"):
        d[k] = np.ascontiguousarray(inputs[f"L1_attn_{k}"])
    d["Wv"] = np.ascontiguousarray(inputs["L1_attn_Wv"][:, hcperm])
    d["Wog2"] = np.ascontiguousarray(inputs["L1_attn_Wog2"][:, hcperm])
    d["norm_w"] = np.ascontiguousarray(inputs["L1_attn_norm_w"][hcperm])
    d["Wo"] = np.ascontiguousarray(inputs["L1_attn_Wo"][hcperm, :])

    d["WrT1"] = np.ascontiguousarray(inputs["L1_ffn_W_r"].T)
    d["WkT1"] = np.ascontiguousarray(inputs["L1_ffn_W_k"].T)
    d["WvT1"] = np.ascontiguousarray(inputs["L1_ffn_W_v"].T)
    d["mu_r"] = np.ascontiguousarray(inputs["L1_ffn_mu_r"].reshape(NC, 128).T)
    d["mu_k"] = np.ascontiguousarray(inputs["L1_ffn_mu_k"].reshape(NC, 128).T)
    return d


def _inline_consts(nc):
    c = {}
    ident = np.eye(128, dtype=BF)
    c["ident"] = nc.inline_tensor(ident, "c_ident")
    ones = np.ones((128, 128), dtype=BF)
    c["ones"] = nc.inline_tensor(ones, "c_ones")
    swap = np.zeros((128, 128), dtype=BF)
    for g in range(2):
        b = g * 64
        for i in range(32):
            swap[b + 32 + i, b + i] = 1.0
            swap[b + i, b + 32 + i] = 1.0
    c["swap"] = nc.inline_tensor(swap, "c_swap")
    ut = np.triu(np.ones((64, 64), np.float32)).astype(BF)
    c["ut64"] = nc.inline_tensor(np.tile(ut, (2, 1)), "c_ut64")
    c["ident_f32"] = nc.inline_tensor(np.eye(128, dtype=np.float32), "c_idf")
    sel = np.zeros((8, 8 * 128), dtype=BF)
    for e in range(8):
        sel[e, e * 128:(e + 1) * 128] = 1.0
    c["sel"] = nc.inline_tensor(sel, "c_sel")
    sl = np.tril(np.ones((64, 64), np.float32), -1).astype(np.float32)
    il = np.tril(np.ones((64, 64), np.float32)).astype(np.float32)
    iu = il.T.copy()
    su = sl.T.copy()
    c["mask_sl"] = nc.inline_tensor(
        np.tile(np.tile(sl, (1, 8)), (2, 1)).astype(BF), "c_msl")
    c["mask_iu"] = nc.inline_tensor(
        np.tile(np.tile(iu, (1, 8)), (2, 1)).astype(BF), "c_miu")
    c["mask_su"] = nc.inline_tensor(
        np.tile(np.tile(su, (1, 8)), (2, 1)).astype(BF), "c_msu")
    i64b = np.zeros((128, 1024), dtype=BF)
    for pr in range(128):
        for h in range(16):
            i64b[pr, h * 64 + (pr % 64)] = 1.0
    c["i64b"] = nc.inline_tensor(i64b, "c_i64b")
    return c



class Prog:

    def __init__(self, nc, dbg=()):
        self.nc = nc
        self.dbg = set(dbg)
        self.c = _inline_consts(nc)
        ap = lambda t: t.ap()

        di = {}
        di["x_fm"] = nc.dram_tensor("x_fm", [D, N], F32, kind="ExternalInput")
        di["WqkvT"] = nc.dram_tensor("WqkvT", [D, 3 * D], F32, kind="ExternalInput")
        di["WprojT"] = nc.dram_tensor("WprojT", [D, D], F32, kind="ExternalInput")
        for k in ("rope_c1q", "rope_c2q"):
            di[k] = nc.dram_tensor(k, [128, N], BF16, kind="ExternalInput")
        di["W1a"] = nc.dram_tensor("W1a", [8, D, 2048], F32, kind="ExternalInput")
        di["W1g"] = nc.dram_tensor("W1g", [8, D, 2048], F32, kind="ExternalInput")
        di["W2"] = nc.dram_tensor("W2", [8, 2048, D], F32, kind="ExternalInput")
        di["b1a"] = nc.dram_tensor("b1a", [128, 128], F32, kind="ExternalInput")
        di["b1g"] = nc.dram_tensor("b1g", [128, 128], F32, kind="ExternalInput")
        di["b2"] = nc.dram_tensor("b2", [8, D], F32, kind="ExternalInput")
        di["Wr"] = nc.dram_tensor("Wr", [D, 8], F32, kind="ExternalInput")
        di["br"] = nc.dram_tensor("br", [8], F32, kind="ExternalInput")
        for k in ("Wq", "Wk", "Wv", "Wo"):
            di[k] = nc.dram_tensor(k, [D, D], F32, kind="ExternalInput")
        di["Wgamma"] = nc.dram_tensor("Wgamma", [D, H], F32, kind="ExternalInput")
        di["Wf1"] = nc.dram_tensor("Wf1", [D, HD], F32, kind="ExternalInput")
        di["Wf2"] = nc.dram_tensor("Wf2", [HD, D], F32, kind="ExternalInput")
        di["Wog1"] = nc.dram_tensor("Wog1", [D, HD], F32, kind="ExternalInput")
        di["Wog2"] = nc.dram_tensor("Wog2", [HD, D], F32, kind="ExternalInput")
        di["norm_w"] = nc.dram_tensor("norm_w", [D], F32, kind="ExternalInput")
        di["WrT1"] = nc.dram_tensor("WrT1", [D, D], F32, kind="ExternalInput")
        di["WkT1"] = nc.dram_tensor("WkT1", [D, D], F32, kind="ExternalInput")
        di["WvT1"] = nc.dram_tensor("WvT1", [D, D], F32, kind="ExternalInput")
        di["mu_r"] = nc.dram_tensor("mu_r", [128, NC], F32, kind="ExternalInput")
        di["mu_k"] = nc.dram_tensor("mu_k", [128, NC], F32, kind="ExternalInput")
        self.di = {k: ap(v) for k, v in di.items()}

        self.y = ap(nc.dram_tensor("y", [D, N], F32, kind="ExternalOutput"))

        self.x1_dram = ap(nc.dram_tensor("x1_scratch", [D, N], F32, kind="Internal"))
        self.logits_dram = ap(nc.dram_tensor("logits_scratch", [8, N], F32,
                                             kind="Internal"))
        self.o_dram = ap(nc.dram_tensor("o_scratch", [8, 128, N], BF16, kind="Internal"))

        A = nc.alloc_sbuf_tensor
        self.xn = ap(A("xn", [128, NC, N], BF16))
        self.big = ap(A("big", [128, NC, N], BF16))
        self.consts = ap(A("consts", [128, 128 * 3], BF16))
        self.ut64 = ap(A("ut64", [128, 64], BF16))
        self.sel = ap(A("sel", [8, 8 * 128], BF16))
        self.identf = ap(A("identf", [128, 128], F32))
        self.i64b = ap(A("i64b", [128, 1024], BF16))
        self.masks = ap(A("masks", [128, 512 * 3], BF16))
        self.pool_bytes = 71 * 1024
        self.pool = ap(A("pool", [128, self.pool_bytes], mybir.dt.uint8))
        self.pool_off = 0
        self.pool_marks = []

        self.pb_s = ap(nc.alloc_psum_tensor("pbs", [128, 2048], F32))
        self.pb4 = [ap(nc.alloc_psum_tensor(f"pb{i}", [128, 512], F32))
                    for i in range(4)]
        self.sconst = ap(A("sconst", [128, 8], F32))
        self.pb = _BankView(self)

        self.ctx = Ctx(nc)
        for s in ("ld", "ldx0", "ldx1", "mm", "ev", "sc", "gp", "st",
                  "aux", "gld", "rld", "bld", "wrf", "tbl", "lgl", "lgs", "da1s",
                  "wl0", "wl1", "wl2", "w2l0", "w2l1", "w2l2",
                  "wpl0", "wpl1", "rt"):
            self.ctx.new_sem(s)
        self.dbg_outs = {}
        self.bank_guard = [("ev", 0)] * 8
        self.w1_guard = [("mm", 0), ("mm", 0), ("mm", 0)]
        self.w2_guard = [("mm", 0), ("mm", 0), ("mm", 0)]
        self.wpj_guard = [("mm", 0), ("mm", 0), ("mm", 0)]

    def palloc(self, nbytes, dtype, free_elems):
        nbytes_al = (nbytes + 31) & ~31
        assert self.pool_off + nbytes_al <= self.pool_bytes, \
            f"pool overflow: {self.pool_off}+{nbytes_al}"
        apx = self.pool[:, self.pool_off:self.pool_off + nbytes_al].bitcast(dtype)
        self.pool_off += nbytes_al
        return apx[:, :free_elems] if free_elems is not None else apx

    def pmark(self, fence=True):
        if fence:
            self.ctx.fence()
        self.pool_marks.append(self.pool_off)

    def prelease(self):
        self.pool_off = self.pool_marks.pop()

    def dbg_out(self, name, shape, dtype=F32):
        if name not in self.dbg_outs:
            self.dbg_outs[name] = self.nc.dram_tensor(
                f"dbg_{name}", list(shape), dtype, kind="ExternalOutput").ap()
        return self.dbg_outs[name]

    @property
    def eps_col(self):
        return self.sconst[:, 0:1]

    @property
    def one_col(self):
        return self.sconst[:, 1:2]

    def load_consts(self):
        cx, c = self.ctx, self.c
        cx.op("vector", lambda e: e.memset(self.sconst[:, 0:1], 1e-5))
        cx.op("vector", lambda e: e.memset(self.sconst[:, 1:2], 1.0))
        self.sconst_ready = ("ev", cx.val("ev"))
        cx.dma("sync", self.consts[:, 0:128], c["ident"].ap(), inc="ld")
        cx.dma("sync", self.consts[:, 128:256], c["ones"].ap(), inc="ld")
        cx.dma("sync", self.consts[:, 256:384], c["swap"].ap(), inc="ld")
        cx.dma("sync", self.ut64, c["ut64"].ap(), inc="ld")
        cx.dma("sync", self.masks[:, 0:512], c["mask_sl"].ap(), inc="ld")
        cx.dma("sync", self.masks[:, 512:1024], c["mask_iu"].ap(), inc="ld")
        cx.dma("sync", self.masks[:, 1024:1536], c["mask_su"].ap(), inc="ld")
        cx.dma("sync", self.sel, c["sel"].ap(), inc="ld")
        cx.dma("sync", self.identf, c["ident_f32"].ap(), inc="ld")
        cx.dma("sync", self.i64b, c["i64b"].ap(), inc="ld")
        self.consts_ready = cx.val("ld")

    @property
    def ident(self):
        return self.consts[:, 0:128]

    @property
    def ones(self):
        return self.consts[:, 128:256]

    @property
    def swap(self):
        return self.consts[:, 256:384]

    def emit_ln(self, dst, tw, x_stage, stage_wait, ib4,
                scratch_bf, scratch_f32, rows_f32, rows_bf, extra_mm_wait=None):
        cx = self.ctx
        ib_sum, ib_sq, ib_mb, ib_rb = ib4
        pb_sum, pb_sq = self.pb[ib_sum], self.pb[ib_sq]
        pb_mb, pb_rb = self.pb[ib_mb], self.pb[ib_rb]
        w_cast = [stage_wait, ("mm", cx.val("mm"))]
        for c in range(NC):
            cx.op("scalar", lambda e, c=c: e.copy(
                out=scratch_bf[:, c * tw:(c + 1) * tw], in_=x_stage[:, c, :]),
                waits=w_cast if c == 0 else ())
        cast_done = cx.val("sc")
        for c in range(NC):
            cx.op("scalar", lambda e, c=c: e.activation(
                out=scratch_bf[:, (NC + c) * tw:(NC + c + 1) * tw],
                in_=x_stage[:, c, :], func=AF.Square))
        sq_done = cx.val("sc")
        ones_col = self.ones[:, 0:1]
        w0 = [("sc", cast_done), self.bank_guard[ib_sum]]
        if extra_mm_wait is not None:
            w0.append(extra_mm_wait)
        for c in range(NC):
            cx.op("tensor", lambda e, c=c: e.matmul(
                pb_sum[0:1, 0:tw], ones_col, scratch_bf[:, c * tw:(c + 1) * tw],
                start=(c == 0), stop=(c == NC - 1)),
                waits=w0 if c == 0 else ())
        for c in range(NC):
            cx.op("tensor", lambda e, c=c: e.matmul(
                pb_sq[0:1, 0:tw], ones_col,
                scratch_bf[:, (NC + c) * tw:(NC + c + 1) * tw],
                start=(c == 0), stop=(c == NC - 1)),
                waits=[("sc", sq_done), self.bank_guard[ib_sq]] if c == 0 else (),
                inc="mm" if c == NC - 1 else None)
        mmv = cx.val("mm")
        cx.op("scalar", lambda e: e.mul(out=rows_f32[0:1, 0:tw],
                                        in_=pb_sum[0:1, 0:tw], mul=1.0 / D),
              waits=[("mm", mmv)])
        cx.op("scalar", lambda e: e.activation(out=rows_f32[0:1, tw:2 * tw],
                                               in_=rows_f32[0:1, 0:tw],
                                               func=AF.Square))
        scv = cx.val("sc")
        cx.op("vector", lambda e: e.scalar_tensor_tensor(
            out=rows_f32[0:1, 2 * tw:3 * tw], in0=pb_sq[0:1, 0:tw],
            scalar=1.0 / D, in1=rows_f32[0:1, tw:2 * tw],
            op0=ALU.mult, op1=ALU.subtract), waits=[("sc", scv)])
        self.bank_guard[ib_sum] = ("sc", cx.val("sc"))
        self.bank_guard[ib_sq] = ("ev", cx.val("ev"))
        cx.op("scalar", lambda e: e.activation(
            out=rows_f32[0:1, 2 * tw:3 * tw], in_=rows_f32[0:1, 2 * tw:3 * tw],
            func=AF.Ln, bias=self.eps_col[0:1, :]),
            waits=[("ev", cx.val("ev"))])
        cx.op("scalar", lambda e: e.activation(
            out=rows_f32[0:1, 2 * tw:3 * tw], in_=rows_f32[0:1, 2 * tw:3 * tw],
            func=AF.Exp, scale=-0.5))
        cx.op("scalar", lambda e: e.copy(
            out=rows_bf[0:1, 0:tw], in_=rows_f32[0:1, 0:tw]))
        cx.op("scalar", lambda e: e.copy(
            out=rows_bf[0:1, tw:2 * tw], in_=rows_f32[0:1, 2 * tw:3 * tw]))
        scv2 = cx.val("sc")
        ones_row = self.ones[0:1, :]
        cx.op("tensor", lambda e: e.matmul(
            pb_mb[:, 0:tw], ones_row, rows_bf[0:1, 0:tw],
            start=True, stop=True),
            waits=[("sc", scv2), self.bank_guard[ib_mb]])
        cx.op("tensor", lambda e: e.matmul(
            pb_rb[:, 0:tw], ones_row, rows_bf[0:1, tw:2 * tw],
            start=True, stop=True),
            waits=[self.bank_guard[ib_rb]], inc="mm")
        mmv2 = cx.val("mm")
        for c in range(NC):
            cx.op("vector", lambda e, c=c: e.tensor_sub(
                out=scratch_f32, in0=x_stage[:, c, :], in1=pb_mb[:, 0:tw]),
                waits=[("mm", mmv2)] if c == 0 else ())
            cx.op("vector", lambda e, c=c: e.tensor_mul(
                out=dst[:, c, :], in0=scratch_f32, in1=pb_rb[:, 0:tw]))
        self.bank_guard[ib_mb] = ("ev", cx.val("ev"))
        self.bank_guard[ib_rb] = ("ev", cx.val("ev"))


    def emit_l0_attention(self):
        cx = self.ctx
        p = self
        TW = 512
        p.pmark()
        tabs = p.palloc(2 * N * 2, BF16, 2 * N)
        qb = p.palloc(N * 2, BF16, N)
        kb = p.palloc(N * 2, BF16, N)
        vaug = p.palloc(32 * 130 * 2, BF16, 32 * 130).rearrange(
            "p (k c) -> p k c", k=32)
        pt = p.palloc(2048 * 2, BF16, 2048)
        rsc = p.palloc(TW * 4, F32, TW)
        rsc2 = p.palloc(TW * 4, F32, TW)
        otmp = p.palloc(128 * 2, BF16, 128)
        rcol = p.palloc(2 * 4, F32, 2)
        wqkv2 = [tuple(p.palloc(NC * 128 * 2, BF16, NC * 128).rearrange(
            "p (c o) -> p c o", c=NC) for _ in range(3)) for _ in range(2)]
        wcs0 = p.palloc(2048 * 2, BF16, 2048)
        wcs1 = p.palloc(2048 * 2, BF16, 2048)
        p.init_weight_cache(wcs0, wcs1)

        for i, name in enumerate(("rope_c1q", "rope_c2q")):
            cx.dma("sync", tabs[:, i * N:(i + 1) * N], p.di[name], inc="tbl")
        tabs_ready = cx.val("tbl")
        t_c1q = tabs[:, 0 * N:1 * N]
        t_c2q = tabs[:, 1 * N:2 * N]

        xn_ready = ("ev", cx.val("ev"))

        def load_pair_weights(hp, mmwait):
            wls = f"wl{hp % 2}"
            srcw = p.di["WqkvT"]
            wq_, wk_, wv_ = wqkv2[hp % 2]
            for wdst, base in ((wq_, hp * 128), (wk_, D + hp * 128),
                               (wv_, 2 * D + hp * 128)):
                cx.dma("gpsimd", wdst,
                       srcw[:, base:base + 128].rearrange(
                           "(c p) o -> p c o", p=128),
                       waits=[("mm", mmwait)], inc=wls)
            return (wls, cx.val(wls))

        next_wl = load_pair_weights(0, cx.val("mm"))
        for hp in range(8):
            wq, wk, wv = wqkv2[hp % 2]
            wl_ready = next_wl
            p.emit_cache_chunk(64)

            if hp == 0:
                cx.op("vector", lambda e: e.memset(vaug, 1.0))
            vaug_clear = cx.val("ev")
            for kc in range(32):
                bk = (2, 3, 6, 7)[kc % 4]
                cx.op("tensor", lambda e, kc=kc, bk=bk, wv=wv: e.matmul(
                    p.pb[bk][:, 0:128], p.xn[:, 0, kc * 128:(kc + 1) * 128],
                    wv[:, 0, :], start=True, stop=False),
                    waits=[p.bank_guard[bk]] + ([wl_ready, xn_ready]
                                                if kc == 0 else []))
                for c in range(1, NC):
                    cx.op("tensor", lambda e, kc=kc, c=c, bk=bk, wv=wv:
                          e.matmul(
                              p.pb[bk][:, 0:128],
                              p.xn[:, c, kc * 128:(kc + 1) * 128],
                              wv[:, c, :], start=False, stop=(c == NC - 1)),
                          inc="mm" if c == NC - 1 else None)
                mmv = cx.val("mm")
                cx.op("vector", lambda e, kc=kc, bk=bk: e.tensor_copy(
                    out=vaug[:, kc, 0:64], in_=p.pb[bk][:, 0:64]),
                    waits=[("mm", mmv), ("ev", vaug_clear)])
                cx.op("vector", lambda e, kc=kc, bk=bk: e.tensor_copy(
                    out=vaug[:, kc, 65:129], in_=p.pb[bk][:, 64:128]))
                p.bank_guard[bk] = ("ev", cx.val("ev"))
            for dst_sb, wsb, bank in ((qb, wq, 0), (kb, wk, 1)):
                for ti in range(N // TW):
                    t0 = ti * TW
                    for c in range(NC):
                        cx.op("tensor", lambda e, c=c, t0=t0, wsb=wsb, bank=bank:
                              e.matmul(p.pb[bank][:, 0:TW], wsb[:, c, :],
                                       p.xn[:, c, t0:t0 + TW],
                                       start=(c == 0), stop=(c == NC - 1)),
                              waits=([wl_ready, xn_ready, p.bank_guard[bank]]
                                     if c == 0 else ()),
                              inc="mm" if c == NC - 1 else None)
                    mmv = cx.val("mm")
                    cx.op("vector", lambda e, dst_sb=dst_sb, t0=t0, bank=bank:
                          e.tensor_copy(out=dst_sb[:, t0:t0 + TW],
                                        in_=p.pb[bank][:, 0:TW]),
                          waits=[("mm", mmv)])
                    p.bank_guard[bank] = ("ev", cx.val("ev"))

            if hp < 7:
                next_wl = load_pair_weights(hp + 1, cx.val("mm"))

            for buf, gain in ((qb, 1.0), (kb, 8.0)):
                for ti in range(N // TW):
                    t0 = ti * TW
                    evb = cx.val("ev")
                    cx.op("tensor", lambda e, buf=buf, t0=t0: e.matmul(
                        p.pb[3][:, 0:TW], p.swap, buf[:, t0:t0 + TW],
                        start=True, stop=True),
                        waits=[("ev", evb), ("tbl", tabs_ready),
                               p.bank_guard[3]], inc="mm")
                    mmv = cx.val("mm")
                    cx.op("vector", lambda e, t0=t0, gain=gain:
                          e.scalar_tensor_tensor(
                              out=rsc, in0=p.pb[3][:, 0:TW], scalar=gain,
                              in1=t_c2q[:, t0:t0 + TW],
                              op0=ALU.mult, op1=ALU.mult),
                          waits=[("mm", mmv)])
                    p.bank_guard[3] = ("ev", cx.val("ev"))
                    cx.op("vector", lambda e, buf=buf, t0=t0, gain=gain:
                          e.scalar_tensor_tensor(
                              out=rsc2, in0=buf[:, t0:t0 + TW], scalar=gain,
                              in1=t_c1q[:, t0:t0 + TW],
                              op0=ALU.mult, op1=ALU.mult))
                    cx.op("vector", lambda e, buf=buf, t0=t0: e.tensor_add(
                        out=buf[:, t0:t0 + TW], in0=rsc, in1=rsc2))
            rope_done = ("ev", cx.val("ev"))

            for qc in range(32):
                q0 = qc * 128
                exp_done = [0] * 8
                sdone = [0] * 8
                avdone = [0] * 8

                def emit_sbatch(b, qc=qc, q0=q0):
                    bb = 1024 * (b % 2)
                    w_ = [rope_done, p.bank_guard[0 + 2 * (b % 2)]]
                    for j in range(4):
                        kc = b * 4 + j
                        k0 = kc * 128
                        for hb in (0, 64):
                            scol = bb + (hb // 64) * 512 + j * 128
                            cx.op("tensor", lambda e, hb=hb, scol=scol,
                                  k0=k0, q0=q0: e.matmul(
                                      p.pb_s[:, scol:scol + 128],
                                      kb[hb:hb + 64, k0:k0 + 128],
                                      qb[hb:hb + 64, q0:q0 + 128],
                                      start=True, stop=True),
                                  waits=w_ if (j == 0 and hb == 0) else (),
                                  inc="mm" if (j == 3 and hb == 64)
                                  else None)
                    sdone[b] = cx.val("mm")

                def emit_exp(b):
                    bb = 1024 * (b % 2)
                    cx.op("scalar", lambda e, b=b, bb=bb: e.activation(
                        out=pt[:, (b % 2) * 1024:(b % 2) * 1024 + 1024],
                        in_=p.pb_s[:, bb:bb + 1024], func=AF.Exp),
                        waits=[("mm", sdone[b])])
                    exp_done[b] = cx.val("sc")
                    p.bank_guard[0 + 2 * (b % 2)] = ("sc", exp_done[b])
                    p.bank_guard[1 + 2 * (b % 2)] = ("sc", exp_done[b])

                def emit_av(b, qc=qc):
                    w_ = [("sc", exp_done[b])]
                    if b == 0:
                        w_ += [p.bank_guard[4], p.bank_guard[5]]
                    for j in range(4):
                        kc = b * 4 + j
                        for hslot, bank in ((0, 4), (1, 5)):
                            pcol = (b % 2) * 1024 + hslot * 512 + j * 128
                            vcol = hslot * 65
                            cx.op("tensor", lambda e, pcol=pcol, vcol=vcol,
                                  kc=kc, bank=bank, st=(b == 0 and j == 0),
                                  sp=(b == 7 and j == 3): e.matmul(
                                      p.pb[bank][:, 0:65],
                                      pt[:, pcol:pcol + 128],
                                      vaug[:, kc, vcol:vcol + 65],
                                      start=st, stop=sp),
                                  waits=w_ if (j == 0 and hslot == 0) else (),
                                  inc="mm" if (j == 3 and hslot == 1)
                                  else None)
                    avdone[b] = cx.val("mm")

                for b in range(8):
                    emit_sbatch(b)
                    if b >= 1:
                        emit_av(b - 1)
                    emit_exp(b)
                emit_av(7)
                av_done = cx.val("mm")
                cx.op("vector", lambda e: e.reciprocal(
                    out=rcol[:, 0:1], in_=p.pb[4][:, 64:65]),
                    waits=[("mm", av_done)])
                cx.op("vector", lambda e: e.reciprocal(
                    out=rcol[:, 1:2], in_=p.pb[5][:, 64:65]))
                cx.op("vector", lambda e: e.tensor_scalar(
                    out=otmp[:, 0:64], in0=p.pb[4][:, 0:64],
                    scalar1=rcol[:, 0:1], scalar2=None, op0=ALU.mult))
                cx.op("vector", lambda e: e.tensor_scalar(
                    out=otmp[:, 64:128], in0=p.pb[5][:, 0:64],
                    scalar1=rcol[:, 1:2], scalar2=None, op0=ALU.mult))
                p.bank_guard[4] = ("ev", cx.val("ev"))
                p.bank_guard[5] = ("ev", cx.val("ev"))
                evv = cx.val("ev")
                cx.op("tensor", lambda e: e.transpose(
                    p.pb[6].bitcast(BF16)[:, 0:128], otmp, p.ident),
                    waits=[("ev", evv), p.bank_guard[6]], inc="mm")
                mmv = cx.val("mm")
                cx.op("vector", lambda e, hp=hp, q0=q0: e.tensor_copy(
                    out=p.big[:, hp, q0:q0 + 128],
                    in_=p.pb[6].bitcast(BF16)[:, 0:128]),
                    waits=[("mm", mmv)])
                p.bank_guard[6] = ("ev", cx.val("ev"))
        p.prelease()

    def init_weight_cache(self, stage0, stage1):
        nc, cx = self.nc, self.ctx
        self.w1_bf = nc.dram_tensor("w1_bf", [8, 8, 128, 4096], BF16,
                                    kind="Internal").ap()
        self.w2_bf = nc.dram_tensor("w2_bf", [8, 8, 128, 2048], BF16,
                                    kind="Internal").ap()
        self.wc_stage = (stage0, stage1)
        for s in ("wci0", "wci1", "wco0", "wco1"):
            self.ctx.new_sem(s)
        jobs = []
        for e in range(8):
            for dc in range(8):
                for ag, wsrc in ((0, "W1a"), (1, "W1g")):
                    for kh in range(2):
                        s_ = self.di[wsrc][e, dc * 128:(dc + 1) * 128, :] \
                            .rearrange("p (g kh o) -> p g kh o",
                                       g=8, kh=2)[:, :, kh, :]
                        d_ = self.w1_bf[e].rearrange(
                            "g p (kh q) -> p g kh q", kh=2)[
                                :, :, kh, ag * 1024 + dc * 128:
                                ag * 1024 + dc * 128 + 128]
                        jobs.append((s_, d_, 1024, 8))
            for kc in range(16):
                jobs.append((self.di["W2"][e, kc * 128:(kc + 1) * 128, :],
                             self.w2_bf[e, kc // 2, :,
                                        (kc % 2) * 1024:(kc % 2) * 1024
                                        + 1024],
                             1024, None))
        self.wc_jobs = jobs
        self.wc_next = 0

    def emit_cache_chunk(self, njobs):
        cx = self.ctx
        end = min(self.wc_next + njobs, len(self.wc_jobs))
        for i in range(self.wc_next, end):
            src, dst, n, v4 = self.wc_jobs[i]
            par = i % 2
            stage = self.wc_stage[par]
            sview = stage[:, 0:n]
            if v4 is not None:
                sview = sview.rearrange("p (g o) -> p g o", g=v4)
            cx.dma("gpsimd", sview, src,
                   waits=[(f"wco{par}", cx.val(f"wco{par}"))],
                   inc=f"wci{par}")
            cx.dma("sync", dst, sview,
                   waits=[(f"wci{par}", cx.val(f"wci{par}"))],
                   inc=f"wco{par}")
        self.wc_next = end
        if end == len(self.wc_jobs):
            self.wcache_done = [("wco0", cx.val("wco0")),
                                ("wco1", cx.val("wco1"))]

    def emit_router(self):
        cx = self.ctx
        p = self
        wr = p.r_wr
        br_col = p.r_br
        logit = p.r_logit
        ltm = p.r_ltm
        scr = p.r_scr
        w_bf = p.w_bf

        cx.dma("gpsimd", br_col, p.di["br"].rearrange("(p o) -> p o", o=1),
               inc="rld")
        wl = ("rld", cx.val("rld"))
        cx.dma("sync", logit, p.logits_dram,
               waits=[("lgs", cx.val("lgs"))], inc="lgl")
        llv = cx.val("lgl")
        cx.op("vector", lambda e: e.tensor_scalar(
            out=logit, in0=logit, scalar1=br_col, scalar2=None, op0=ALU.add),
            waits=[wl, ("lgl", llv)])
        for blk4 in range(8):
            evv = cx.val("ev")
            for j in range(4):
                blk = blk4 * 4 + j
                cx.op("tensor", lambda e, blk=blk, j=j: e.transpose(
                    p.pb[5][:, j * 8:j * 8 + 8],
                    logit[:, blk * 128:(blk + 1) * 128], p.identf[0:8, 0:8]),
                    waits=([("ev", evv), p.bank_guard[5]] if j == 0 else ()),
                    inc="mm" if j == 3 else None)
            mmv = cx.val("mm")
            cx.op("vector", lambda e, blk4=blk4: e.tensor_copy(
                out=ltm[:, blk4 * 4:blk4 * 4 + 4, :].rearrange("p a b -> p (a b)"),
                in_=p.pb[5][:, 0:32]), waits=[("mm", mmv)])
            p.bank_guard[5] = ("ev", cx.val("ev"))
        cx.op("vector", lambda e: e.reduce_max(
            out=scr[:, 0:32], in_=ltm, axis=mybir.AxisListType.X))
        for blk in range(32):
            cx.op("vector", lambda e, blk=blk: e.tensor_scalar(
                out=scr[:, 64 + blk * 8:64 + blk * 8 + 8], in0=ltm[:, blk, :],
                scalar1=scr[:, blk:blk + 1], scalar2=None, op0=ALU.is_equal))
        for blk in range(32):
            cx.op("vector", lambda e, blk=blk: e.scalar_tensor_tensor(
                out=ltm[:, blk, :], in0=scr[:, 64 + blk * 8:64 + blk * 8 + 8],
                scalar=-1e30, in1=ltm[:, blk, :], op0=ALU.mult, op1=ALU.add))
        cx.op("vector", lambda e: e.reduce_max(
            out=scr[:, 32:64], in_=ltm, axis=mybir.AxisListType.X))
        for blk in range(32):
            cx.op("vector", lambda e, blk=blk: e.tensor_scalar(
                out=scr[:, 320 + blk * 8:320 + blk * 8 + 8], in0=ltm[:, blk, :],
                scalar1=scr[:, 32 + blk:32 + blk + 1], scalar2=None,
                op0=ALU.is_equal))
        cx.op("vector", lambda e: e.tensor_sub(
            out=scr[:, 576:608], in0=scr[:, 0:32], in1=scr[:, 32:64]))
        scv = cx.val("ev")
        cx.op("scalar", lambda e: e.activation(
            out=scr[:, 576:608], in_=scr[:, 576:608], func=AF.Sigmoid),
            waits=[("ev", scv)])
        scs = cx.val("sc")
        cx.op("vector", lambda e: e.tensor_scalar(
            out=scr[:, 608:640], in0=scr[:, 576:608], scalar1=-1.0, scalar2=1.0,
            op0=ALU.mult, op1=ALU.add), waits=[("sc", scs)])
        for blk in range(32):
            cx.op("vector", lambda e, blk=blk: e.tensor_scalar(
                out=ltm[:, blk, :], in0=scr[:, 64 + blk * 8:64 + blk * 8 + 8],
                scalar1=scr[:, 576 + blk:576 + blk + 1], scalar2=None,
                op0=ALU.mult))
            cx.op("vector", lambda e, blk=blk: e.scalar_tensor_tensor(
                out=ltm[:, blk, :], in0=scr[:, 320 + blk * 8:320 + blk * 8 + 8],
                scalar=scr[:, 608 + blk:608 + blk + 1], in1=ltm[:, blk, :],
                op0=ALU.mult, op1=ALU.add))
        for blk4 in range(8):
            evv = cx.val("ev")
            for j in range(4):
                blk = blk4 * 4 + j
                cx.op("tensor", lambda e, blk=blk, j=j: e.transpose(
                    p.pb[6][0:8, j * 128:(j + 1) * 128],
                    ltm[:, blk, :], p.identf),
                    waits=([("ev", evv), p.bank_guard[6]] if j == 0 else ()),
                    inc="mm" if j == 3 else None)
            mmv = cx.val("mm")
            cx.op("vector", lambda e, blk4=blk4: e.tensor_copy(
                out=w_bf[0:8, blk4 * 512:(blk4 + 1) * 512],
                in_=p.pb[6][0:8, 0:512]), waits=[("mm", mmv)])
            p.bank_guard[6] = ("ev", cx.val("ev"))

    def emit_l0_moe_residual(self):
        cx = self.ctx
        p = self
        TW = 512
        nt = N // TW
        act = p.m_act
        df = p.m_df
        xs = p.m_xs
        wbc = p.m_wbc
        sab = p.m_sab
        b1a = p.m_b1a
        b1g = p.m_b1g
        b2s = p.m_b2
        lnbf = p.m_lnbf
        lnf = p.m_lnf
        lrows = p.m_lrows
        lrbf = p.m_lrbf

        cx.dma("gpsimd", b1a, p.di["b1a"], inc="bld")
        cx.dma("gpsimd", b1g, p.di["b1g"], inc="bld")
        cx.dma("gpsimd", b2s, p.di["b2"], inc="bld")
        bias_ready = ("bld", cx.val("bld"))

        for ti in range(nt):
            t0 = ti * TW
            wfb = ("ev", cx.val("ev"))
            for c in range(NC):
                bdf = 4 + c % 2
                cx.op("tensor", lambda e, c=c, t0=t0, bdf=bdf: e.matmul(
                    p.pb[bdf][:, 0:TW], b2s[0:8, c * 128:(c + 1) * 128],
                    p.w_bf[0:8, t0:t0 + TW], start=True, stop=True),
                    waits=[wfb, bias_ready, p.bank_guard[bdf]],
                    inc="mm")
                mmv = cx.val("mm")
                cx.op("vector", lambda e, c=c, bdf=bdf: e.tensor_copy(
                    out=df[:, c, :], in_=p.pb[bdf][:, 0:TW]),
                    waits=[("mm", mmv)])
                p.bank_guard[bdf] = ("ev", cx.val("ev"))

            for ex in range(8):
                cx.op("tensor", lambda e, ex=ex, t0=t0: e.matmul(
                    p.pb[5][:, 0:TW], p.sel[0:8, ex * 128:(ex + 1) * 128],
                    p.w_bf[0:8, t0:t0 + TW], start=True, stop=True),
                    waits=[p.bank_guard[5]], inc="mm")
                mmv = cx.val("mm")
                cx.op("vector", lambda e: e.tensor_copy(
                    out=wbc, in_=p.pb[5][:, 0:TW]), waits=[("mm", mmv)])
                p.bank_guard[5] = ("ev", cx.val("ev"))
                wbc_ready = cx.val("ev")

                act_ev = [None] * 16
                for grp in range(8):
                    par = grp % 3
                    wsem = f"wl{par}"
                    ring = p.m_ring[par]
                    cx.dma("gpsimd", ring,
                           p.w1_bf[ex, grp],
                           waits=[(wsem, cx.val(wsem)), p.w1_guard[par],
                                  p.w2_guard[par], p.wpj_guard[par]] + (
                               p.wcache_done if ti == 0 and ex == 0 and
                               grp < 2 else []),
                           inc=wsem)
                    wlv = (wsem, cx.val(wsem))
                    for kh in range(2):
                        kc = grp * 2 + kh
                        ba, bg = (0, 1) if kc % 2 == 0 else (2, 3)
                        for c in range(NC):
                            cx.op("tensor", lambda e, c=c, t0=t0, ba=ba,
                                  ring=ring, kh=kh: e.matmul(
                                      p.pb[ba][:, 0:TW],
                                      ring[:, kh * 2048 + c * 128:
                                           kh * 2048 + c * 128 + 128],
                                      p.xn[:, c, t0:t0 + TW],
                                      start=(c == 0), stop=(c == NC - 1)),
                                  waits=([wlv, p.bank_guard[ba]]
                                         if c == 0 else ()))
                        for c in range(NC):
                            cx.op("tensor", lambda e, c=c, t0=t0, bg=bg,
                                  ring=ring, kh=kh: e.matmul(
                                      p.pb[bg][:, 0:TW],
                                      ring[:, kh * 2048 + 1024 + c * 128:
                                           kh * 2048 + 1024 + c * 128 + 128],
                                      p.xn[:, c, t0:t0 + TW],
                                      start=(c == 0), stop=(c == NC - 1)),
                                  waits=([p.bank_guard[bg]] if c == 0 else ()),
                                  inc="mm" if c == NC - 1 else None)
                        mmv = cx.val("mm")
                        if kh == 1:
                            p.w1_guard[par] = ("mm", mmv)
                        cx.op("scalar", lambda e, ex=ex, kc=kc, ba=ba:
                              e.activation(
                                  out=sab, in_=p.pb[ba][:, 0:TW],
                                  func=AF.Silu,
                                  bias=b1a[:, ex * 16 + kc:
                                           ex * 16 + kc + 1]),
                              waits=[("mm", mmv), ("ev", cx.val("ev"))])
                        scv = cx.val("sc")
                        cx.op("vector", lambda e, ex=ex, kc=kc, bg=bg:
                              e.scalar_tensor_tensor(
                                  out=act[:, kc, :], in0=p.pb[bg][:, 0:TW],
                                  scalar=b1g[:, ex * 16 + kc:
                                             ex * 16 + kc + 1],
                                  in1=sab, op0=ALU.add, op1=ALU.mult),
                              waits=[("sc", scv)])
                        cx.op("vector", lambda e, kc=kc: e.tensor_mul(
                            out=act[:, kc, :], in0=act[:, kc, :], in1=wbc))
                        p.bank_guard[ba] = ("sc", scv)
                        p.bank_guard[bg] = ("ev", cx.val("ev"))
                        act_ev[kc] = ("ev", cx.val("ev"))
                act_done = ("ev", cx.val("ev"))

                for grp in range(8):
                    par = grp % 3
                    wsem = f"w2l{par}"
                    ring = p.m_ring[par]
                    cx.dma("gpsimd", ring[:, 0:2048],
                           p.w2_bf[ex, grp],
                           waits=[(wsem, cx.val(wsem)), p.w2_guard[par],
                                  p.w1_guard[par]],
                           inc=wsem)
                    wlv = (wsem, cx.val(wsem))
                    for kh in range(2):
                        kc = grp * 2 + kh
                        for oc in range(8):
                            w_ = [wlv, act_ev[kc]]
                            if kc == 0:
                                w_.append(p.bank_guard[oc])
                            cx.op("tensor", lambda e, kc=kc, oc=oc, kh=kh,
                                  ring=ring: e.matmul(
                                      p.pb[oc][:, 0:TW],
                                      ring[:, kh * 1024 + oc * 128:
                                           kh * 1024 + oc * 128 + 128],
                                      act[:, kc, :],
                                      start=(kc == 0), stop=(kc == 15)),
                                  waits=w_ if (oc == 0 or kc == 0) else (),
                                  inc="mm" if oc == 7 else None)
                        p.w2_guard[par] = ("mm", cx.val("mm"))
                mmv = cx.val("mm")
                for oc in range(8):
                    cx.op("vector", lambda e, oc=oc: e.tensor_add(
                        out=df[:, oc, :], in0=df[:, oc, :],
                        in1=p.pb[oc][:, 0:TW]),
                        waits=[("mm", mmv)] if oc == 0 else ())
                    p.bank_guard[oc] = ("ev", cx.val("ev"))

            for half in range(TW // 256):
                s0 = t0 + half * 256
                for og in range(2):
                    par = og % 2
                    wsem = f"wpl{par}"
                    ring = p.m_ring[par]
                    if half == 0:
                        cx.dma("gpsimd",
                               ring.rearrange("p (c o) -> p c o", c=NC),
                               p.di["WprojT"][:, og * 512:(og + 1) * 512]
                               .rearrange("(c p) o -> p c o", p=128),
                               waits=[(wsem, cx.val(wsem)),
                                      p.wpj_guard[par], p.w2_guard[par],
                                      p.w1_guard[par]],
                               inc=wsem)
                    wlv = (wsem, cx.val(wsem))
                    for oq in range(4):
                        oc = og * 4 + oq
                        bank = oc // 2
                        col = (oc % 2) * 256
                        for ec in range(NC):
                            w_ = [wlv]
                            if ec == 0:
                                w_.append(p.bank_guard[bank])
                            cx.op("tensor",
                                  lambda e, ec=ec, ring=ring, oq=oq,
                                  bank=bank, col=col, s0=s0: e.matmul(
                                      p.pb[bank][:, col:col + 256],
                                      ring[:, ec * 512 + oq * 128:
                                           ec * 512 + oq * 128 + 128],
                                      p.big[:, ec, s0:s0 + 256],
                                      start=(ec == 0), stop=(ec == NC - 1)),
                                  waits=w_,
                                  inc="mm" if ec == NC - 1 else None)
                    p.wpj_guard[par] = ("mm", cx.val("mm"))
                mmv_proj = cx.val("mm")

                lds = f"ldx{half % 2}"
                cx.dma("sync", xs,
                       p.di["x_fm"][:, s0:s0 + 256].rearrange(
                           "(c p) t -> p c t", p=128),
                       waits=[("ev", cx.val("ev")), ("sc", cx.val("sc")),
                              ("st", cx.val("st"))], inc=lds)
                ldv = (lds, cx.val(lds))
                for c in range(NC):
                    cx.op("vector", lambda e, c=c, half=half: e.tensor_add(
                        out=xs[:, c, :], in0=xs[:, c, :],
                        in1=df[:, c, half * 256:half * 256 + 256]),
                        waits=[ldv, ("mm", mmv_proj)] if c == 0 else ())
                    cx.op("vector", lambda e, c=c:
                          e.tensor_add(
                              out=xs[:, c, :], in0=xs[:, c, :],
                              in1=p.pb[c // 2][:, (c % 2) * 256:
                                               (c % 2) * 256 + 256]))
                x1_done = cx.val("ev")
                for bk in range(4):
                    p.bank_guard[bk] = ("ev", x1_done)
                cx.dma("sync", p.x1_dram[:, s0:s0 + 256].rearrange(
                    "(c p) t -> p c t", p=128), xs,
                    waits=[("ev", x1_done)], inc="st")
                p.emit_ln(p.xn[:, :, s0:s0 + 256], 256, xs,
                          ("ev", x1_done), (4, 5, 6, 7), lnbf, lnf,
                          lrows, lrbf)

    def emit_l1_attention(self, ncp=32, stages="ABCDEFG"):
        cx = self.ctx
        p = self
        nc = self.nc
        self.da1_dram = nc.dram_tensor("da1_scratch", [D, N], BF16,
                                       kind="Internal").ap()
        bigf = p.big.rearrange("p a b -> p (a b)")
        wq1 = bigf[:, 0:8192].rearrange("p (c o) -> p c o", c=NC)
        wk1 = bigf[:, 8192:16384].rearrange("p (c o) -> p c o", c=NC)
        wv1 = bigf[:, 16384:24576].rearrange("p (c o) -> p c o", c=NC)
        wo1 = bigf[:, 24576:32768].rearrange("p (c o) -> p c o", c=NC)

        p.pmark()
        al = p.palloc
        wgam = al(NC * 16 * 2, BF16, NC * 16).rearrange("p (c o) -> p c o", c=NC)
        wf1 = al(NC * 64 * 2, BF16, NC * 64).rearrange("p (c o) -> p c o", c=NC)
        wog1 = al(NC * 64 * 2, BF16, NC * 64).rearrange("p (c o) -> p c o", c=NC)
        wf2 = al(D * 2, BF16, D)
        wog2 = al(D * 2, BF16, D)
        nwb = al(D * 4, F32, D)
        nwrow = al(D * 2, BF16, D)[0:1, :]
        v_tm = al(D * 2, BF16, D)
        og_tm = al(D * 2, BF16, D)
        kh_tm = al(D * 2, BF16, D)
        a_fm = al(D * 2, BF16, D)
        ktfm = al(D * 2, BF16, D)
        qtfm = al(D * 2, BF16, D)
        Mqt = al(D * 2, BF16, D)
        srow = al(128 * 4, F32, 128)[0:1, :]
        s0 = al(D * 2, BF16, D)
        o_tm = al(D * 2, BF16, D)
        ysb = al(64 * 2, BF16, 64)
        yfm = al(128 * 2, BF16, 128)
        gcol = al(16 * 4, F32, 16)
        rnorm = al(16 * 4, F32, 16)
        lamc = al(32 * 4, F32, 32)
        totf = al(2048 * 4, F32, 2048)[0:1, :]
        trow = al(1024 * 2, BF16, 1024)[0:1, :]
        col1 = al(4, F32, 1)
        f32a = al(D * 4, F32, D)
        f32b = al(D * 4, F32, D)
        s1b = al(D * 2, BF16, D)
        s2b = al(D * 2, BF16, D)
        s3b = al(D * 2, BF16, D)
        s4b = al(D * 2, BF16, D)
        s5b = al(D * 2, BF16, D)
        s6b = al(D * 2, BF16, D)
        s7b = al(D * 2, BF16, D)
        s8b = al(D * 2, BF16, D)
        q_tm = qt_tm = ofm = s1b
        k_tm = kt_tm = dsb = s2b
        ghi = a_tm = obuf = s3b
        glo = bfs = s4b
        lam = Bm = rhs_sb = s5b
        ivl = Cm = u_sb = s6b
        rat = Rmat = sdec = s7b
        ktn = Rp = s8b
        Tt = Rp
        esig = o_tm
        lf32 = f32b

        hv = lambda h: (h % 2) * 64
        hc = lambda h: (h % 2) * 512 + (h // 2) * 64
        MSL = p.masks[:, 0:512]
        MIU = p.masks[:, 512:1024]
        MSU = p.masks[:, 1024:1536]
        PB = p.pb_s
        P01 = PB[:, 0:1024]
        P23 = PB[:, 1024:2048]

        for wdst, name in ((wgam, "Wgamma"), (wf1, "Wf1"), (wog1, "Wog1")):
            cx.dma("gpsimd", wdst,
                   p.di[name].rearrange("(c p) o -> p c o", p=128), inc="gld")
        cx.dma("gpsimd", wf2[0:64, :], p.di["Wf2"], inc="gld")
        cx.dma("gpsimd", wog2[0:64, :], p.di["Wog2"], inc="gld")
        for wdst, name in ((wq1, "Wq"), (wk1, "Wk"), (wv1, "Wv"), (wo1, "Wo")):
            cx.dma("gpsimd", wdst,
                   p.di[name].rearrange("(c p) o -> p c o", p=128), inc="gld")
        cx.dma("sync", f32a[0:1, :], p.di["norm_w"], inc="tbl")
        wl_all = ("gld", cx.val("gld"))
        nw_l = ("tbl", cx.val("tbl"))
        cx.op("scalar", lambda e: e.copy(out=nwrow, in_=f32a[0:1, :]),
              waits=[nw_l])
        scv = cx.val("sc")
        for half in range(2):
            cx.op("tensor", lambda e, half=half: e.matmul(
                p.pb[6][:, 0:512], p.ones[0:1, :],
                nwrow[0:1, half * 512:half * 512 + 512],
                start=True, stop=True),
                waits=[("sc", scv), p.bank_guard[6]], inc="mm")
            mmv = cx.val("mm")
            cx.op("vector", lambda e, half=half: e.tensor_copy(
                out=nwb[:, half * 512:half * 512 + 512],
                in_=p.pb[6][:, 0:512]), waits=[("mm", mmv)])
            p.bank_guard[6] = ("ev", cx.val("ev"))
        cx.op("vector", lambda e: e.memset(s0, 0.0))
        xn_ready = ("ev", cx.val("ev"))

        for cp in range(ncp):
            t0 = cp * 128

            def lowrank_path(wlow, w2low, kind):
                for c in range(NC):
                    cx.op("tensor", lambda e, c=c, t0=t0, wlow=wlow: e.matmul(
                        p.pb[2][:, 16:80], p.xn[:, c, t0:t0 + 128],
                        wlow[:, c, :],
                        start=(c == 0), stop=(c == NC - 1)),
                        waits=[p.bank_guard[2], wl_all, xn_ready]
                        if c == 0 else (),
                        inc="mm" if c == NC - 1 else None)
                mmv = cx.val("mm")
                cx.op("scalar", lambda e: e.copy(out=ysb,
                                                 in_=p.pb[2][:, 16:80]),
                      waits=[("mm", mmv)])
                scv = cx.val("sc")
                p.bank_guard[2] = ("sc", scv)
                cx.op("tensor", lambda e: e.transpose(
                    p.pb[2].bitcast(BF16)[0:64, 256:384], ysb, p.ident),
                    waits=[("sc", scv), p.bank_guard[2]], inc="mm")
                mmv = cx.val("mm")
                cx.op("vector", lambda e: e.tensor_copy(
                    out=yfm[0:64, :], in_=p.pb[2].bitcast(BF16)[0:64, 256:384]),
                    waits=[("mm", mmv)])
                yf_done = cx.val("ev")
                p.bank_guard[2] = ("ev", yf_done)
                for half in range(2):
                    cx.op("tensor", lambda e, half=half, w2low=w2low: e.matmul(
                        p.pb[3][:, 0:512], yfm[0:64, :],
                        w2low[0:64, half * 512:half * 512 + 512],
                        start=True, stop=True),
                        waits=[("ev", yf_done), p.bank_guard[3]], inc="mm")
                    mmv = cx.val("mm")
                    if kind == "z":
                        cx.op("scalar", lambda e, half=half: e.activation(
                            out=f32a[:, half * 512:half * 512 + 512],
                            in_=p.pb[3][:, 0:512], func=AF.Exp, scale=-1.0),
                            waits=[("mm", mmv), ("ev", cx.val("ev"))])
                    else:
                        cx.op("scalar", lambda e, half=half: e.copy(
                            out=og_tm[:, half * 512:half * 512 + 512],
                            in_=p.pb[3][:, 0:512]),
                            waits=[("mm", mmv), ("ev", cx.val("ev"))])
                    p.bank_guard[3] = ("sc", cx.val("sc"))
                if kind == "z":
                    cx.op("vector", lambda e: e.tensor_scalar(
                        out=f32a, in0=f32a, scalar1=1.0, scalar2=None,
                        op0=ALU.add), waits=[("sc", cx.val("sc"))])
                    evv = cx.val("ev")
                    cx.op("scalar", lambda e: e.activation(
                        out=lf32, in_=f32a, func=AF.Ln), waits=[("ev", evv)])
                    cx.op("scalar", lambda e: e.activation(
                        out=esig, in_=lf32, func=AF.Exp, scale=-1.0),
                        waits=[("ev", cx.val("ev"))])
                    cx.op("scalar", lambda e: e.copy(out=ghi, in_=lf32))
                    scv = cx.val("sc")
                    cx.op("vector", lambda e: e.tensor_sub(
                        out=glo, in0=lf32, in1=ghi), waits=[("sc", scv)])

            lowrank_path(wf1, wf2, "z")
            lhilo = ("ev", cx.val("ev"))

            for c in range(NC):
                cx.op("tensor", lambda e, c=c, t0=t0: e.matmul(
                    p.pb[2][:, 0:16], p.xn[:, c, t0:t0 + 128], wgam[:, c, :],
                    start=(c == 0), stop=(c == NC - 1)),
                    waits=[p.bank_guard[2]] if c == 0 else (),
                    inc="mm" if c == NC - 1 else None)
            mmv = cx.val("mm")
            cx.op("scalar", lambda e: e.activation(
                out=gcol, in_=p.pb[2][:, 0:16], func=AF.Sigmoid, scale=-1.0),
                waits=[("mm", mmv)])
            scv = cx.val("sc")
            p.bank_guard[2] = ("sc", scv)
            cx.op("vector", lambda e: e.tensor_scalar(
                out=gcol, in0=gcol, scalar1=-1.0, scalar2=None, op0=ALU.add),
                waits=[("sc", scv)])

            def qkv_unit(wsb, dst, func, half):
                def emit():
                    bk = half
                    for c in range(NC):
                        cx.op("tensor", lambda e, c=c, wsb=wsb, half=half,
                              t0=t0, bk=bk: e.matmul(
                                  p.pb[bk][:, 0:512],
                                  p.xn[:, c, t0:t0 + 128],
                                  wsb[:, c, half * 512:half * 512 + 512],
                                  start=(c == 0), stop=(c == NC - 1)),
                              waits=([wl_all, xn_ready,
                                      p.bank_guard[bk]] if c == 0 else ()),
                              inc="mm" if c == NC - 1 else None)
                    mmv = cx.val("mm")
                    cx.op("scalar", lambda e, dst=dst, half=half, func=func,
                          bk=bk: e.activation(
                              out=dst[:, half * 512:half * 512 + 512],
                              in_=p.pb[bk][:, 0:512], func=func),
                          waits=[("mm", mmv), ("da1s", cx.val("da1s"))])
                    p.bank_guard[bk] = ("sc", cx.val("sc"))
                    return ("sc", cx.val("sc"))
                return emit
            a_units = [qkv_unit(wk1, k_tm, AF.Silu, 0),
                       qkv_unit(wk1, k_tm, AF.Silu, 1),
                       qkv_unit(wq1, q_tm, AF.Silu, 0),
                       qkv_unit(wq1, q_tm, AF.Silu, 1),
                       qkv_unit(wv1, v_tm, AF.Identity, 0),
                       qkv_unit(wv1, v_tm, AF.Identity, 1)]

            if "B" not in stages:
                continue
            first = True
            for par in range(2):
                for half in range(2):
                    for si, srct in enumerate((ghi, glo)):
                        w_ = []
                        if first:
                            w_ = [lhilo, p.bank_guard[4], p.bank_guard[5]]
                            first = False
                        cx.op("tensor", lambda e, par=par, half=half,
                              srct=srct, si=si: e.matmul(
                                  p.pb[4 + half][par * 64:par * 64 + 64,
                                                 0:512],
                                  p.ut64[par * 64:par * 64 + 64, :],
                                  srct[par * 64:par * 64 + 64,
                                       half * 512:half * 512 + 512],
                                  start=(si == 0), stop=(si == 1),
                                  tile_position=(par * 64, par * 64)),
                              waits=w_,
                              inc="mm" if (par == 1 and half == 1 and
                                           si == 1) else None)
            mmv = cx.val("mm")
            for half in range(2):
                cx.op("scalar", lambda e, half=half: e.activation(
                    out=lam[:, half * 512:half * 512 + 512],
                    in_=p.pb[4 + half][:, 0:512], func=AF.Exp, scale=-1.0),
                    waits=[("mm", mmv)] if half == 0 else ())
                cx.op("scalar", lambda e, half=half: e.activation(
                    out=ivl[:, half * 512:half * 512 + 512],
                    in_=p.pb[4 + half][:, 0:512], func=AF.Exp))
            cum_done = cx.val("sc")
            p.bank_guard[4] = ("sc", cum_done)
            p.bank_guard[5] = ("sc", cum_done)
            a_units[0]()

            for par in range(2):
                mmv0 = None
                for half in range(2):
                    bk = 6 + half
                    for si, srct in enumerate((ghi, glo)):
                        w_ = []
                        if half == 0 and si == 0:
                            w_ = [lhilo, p.bank_guard[6], p.bank_guard[7]]
                        cx.op("tensor", lambda e, par=par, half=half,
                              srct=srct, si=si, bk=bk: e.matmul(
                                  p.pb[bk][0:1, 0:512],
                                  p.ones[par * 64:par * 64 + 64, 0:1],
                                  srct[par * 64:par * 64 + 64,
                                       half * 512:half * 512 + 512],
                                  start=(si == 0), stop=(si == 1),
                                  tile_position=(par * 64, 0)),
                              waits=w_,
                              inc="mm" if (half == 1 and si == 1) else None)
                mmv = cx.val("mm")
                for half in range(2):
                    bk = 6 + half
                    off = par * 1024 + half * 512
                    cx.op("scalar", lambda e, bk=bk, off=off: e.copy(
                        out=totf[0:1, off:off + 512],
                        in_=p.pb[bk][0:1, 0:512]),
                        waits=[("mm", mmv)] if half == 0 else ())
                p.bank_guard[6] = p.bank_guard[7] = ("sc", cx.val("sc"))
            tot_done = cx.val("sc")
            k_done = a_units[1]()

            swrows = (f32a[0:1, 0:1024], f32b[0:1, 0:1024])
            for par in range(2):
                cx.op("scalar", lambda e, par=par: e.copy(
                    out=swrows[par].rearrange("p (g q) -> p g q", g=8)
                    [:, :, 0:64],
                    in_=totf[0:1, par * 1024:(par + 1) * 1024].rearrange(
                        "p (g q) -> p g q", g=8)[:, :, 64:128]),
                    waits=[("sc", tot_done)] if par == 0 else ())
                cx.op("scalar", lambda e, par=par: e.copy(
                    out=swrows[par].rearrange("p (g q) -> p g q", g=8)
                    [:, :, 64:128],
                    in_=totf[0:1, par * 1024:(par + 1) * 1024].rearrange(
                        "p (g q) -> p g q", g=8)[:, :, 0:64]))
            swp_done = cx.val("sc")
            slots = (trow, ghi[0:1, 0:1024], glo[0:1, 0:1024],
                     Mqt[0:1, 0:1024])
            rounds = ((0, 0), (0, 1), (1, 0), (1, 1))
            for r, (par, half) in enumerate(rounds):
                off = par * 1024 + half * 512
                cx.op("scalar", lambda e, r=r, off=off: e.copy(
                    out=slots[r][0:1, 0:512], in_=totf[0:1, off:off + 512]))
                scv = cx.val("sc")
                cx.op("vector", lambda e, r=r, off=off: e.tensor_sub(
                    out=slots[r][0:1, 512:1024],
                    in0=totf[0:1, off:off + 512],
                    in1=slots[r][0:1, 0:512]), waits=[("sc", scv)])
            subs_done = cx.val("ev")
            first = True
            for par in range(2):
                for g in range(8):
                    base = par * 1024 + g * 128
                    colA = par * 8 + g
                    colB = 16 + par * 8 + g
                    w_ = []
                    if first:
                        w_ = [("sc", swp_done), p.bank_guard[4]]
                        first = False
                    cx.op("tensor", lambda e, base=base, colA=colA:
                          e.transpose(
                              p.pb[4][:, colA:colA + 1],
                              totf[0:1, base:base + 128],
                              p.identf[0:1, 0:1]),
                          waits=w_)
                    cx.op("tensor", lambda e, par=par, g=g, colB=colB:
                          e.transpose(
                              p.pb[4][:, colB:colB + 1],
                              swrows[par][0:1, g * 128:g * 128 + 128],
                              p.identf[0:1, 0:1]),
                          inc="mm")
            mmv_t = cx.val("mm")
            a_units[2]()
            for r, (par, half) in enumerate(rounds):
                bk = 6 + half
                for si in range(2):
                    w_ = ([("ev", subs_done), p.bank_guard[6],
                           p.bank_guard[7]] if r == 0 and si == 0 else ())
                    cx.op("tensor", lambda e, r=r, par=par, bk=bk, si=si:
                          e.matmul(
                              p.pb[bk][par * 64:par * 64 + 64, 0:512],
                              p.ones[0:1, 0:64],
                              slots[r][0:1, si * 512:si * 512 + 512],
                              start=(si == 0), stop=(si == 1),
                              tile_position=(0, par * 64)),
                          waits=w_,
                          inc="mm" if (r == 3 and si == 1) else None)
            mmv_b = cx.val("mm")
            q_done = a_units[3]()
            cx.op("scalar", lambda e: e.activation(
                out=lamc, in_=p.pb[4][:, 0:32], func=AF.Exp, scale=-1.0),
                waits=[("mm", mmv_t)])
            lamc_done = ("sc", cx.val("sc"))
            p.bank_guard[4] = lamc_done
            for r, (par, half) in enumerate(rounds):
                cx.op("scalar", lambda e, par=par, half=half: e.activation(
                    out=f32a[par * 64:par * 64 + 64,
                             half * 512:half * 512 + 512],
                    in_=p.pb[6 + half][par * 64:par * 64 + 64, 0:512],
                    func=AF.Exp, scale=-1.0),
                    waits=[("mm", mmv_b)] if r == 0 else ())
            b2_done = cx.val("sc")
            p.bank_guard[6] = p.bank_guard[7] = ("sc", b2_done)
            a_units[4]()
            v_done = a_units[5]()

            if "C" not in stages:
                continue
            cx.op("vector", lambda e: e.tensor_mul(out=f32b, in0=k_tm,
                                                   in1=k_tm),
                  waits=[("mm", mmv_t), k_done])
            cx.op("vector", lambda e: e.tensor_reduce(
                out=rnorm, in_=f32b.rearrange("p (h d) -> p h d", h=16),
                axis=mybir.AxisListType.X, op=ALU.add))
            cx.op("vector", lambda e: e.tensor_scalar(
                out=rnorm, in0=rnorm, scalar1=1e-24, scalar2=None,
                op0=ALU.max))
            evv = cx.val("ev")
            cx.op("scalar", lambda e: e.activation(out=rnorm, in_=rnorm,
                                                   func=AF.Ln),
                  waits=[("ev", evv)])
            cx.op("scalar", lambda e: e.activation(out=rnorm, in_=rnorm,
                                                   func=AF.Exp, scale=-0.5))
            scv = cx.val("sc")
            for h in range(16):
                cx.op("vector", lambda e, h=h: e.tensor_scalar(
                    out=ktn[:, h * 64:(h + 1) * 64],
                    in0=k_tm[:, h * 64:(h + 1) * 64],
                    scalar1=rnorm[:, h:h + 1], scalar2=None, op0=ALU.mult),
                    waits=[("sc", scv)] if h == 0 else ())
            cx.op("vector", lambda e: e.tensor_mul(out=kt_tm, in0=ktn,
                                                   in1=ivl))
            cx.op("vector", lambda e: e.tensor_mul(out=f32b, in0=ktn,
                                                   in1=esig),
                  waits=[("sc", cx.val("sc"))])
            cx.op("vector", lambda e: e.tensor_mul(out=bfs, in0=f32b,
                                                   in1=lam),
                  waits=[("mm", mmv_b)])
            for h in range(16):
                cx.op("vector", lambda e, h=h: e.tensor_scalar(
                    out=a_tm[:, h * 64:(h + 1) * 64],
                    in0=bfs[:, h * 64:(h + 1) * 64],
                    scalar1=gcol[:, h:h + 1], scalar2=None, op0=ALU.mult))
            cx.op("vector", lambda e: e.tensor_mul(out=qt_tm, in0=q_tm,
                                                   in1=lam), waits=[q_done])
            c1_done = ("ev", cx.val("ev"))
            lowrank_path(wog1, wog2, "og")

            cx.op("vector", lambda e: e.tensor_mul(out=rat, in0=ivl,
                                                   in1=f32a),
                  waits=[("sc", b2_done)])
            cx.op("vector", lambda e: e.tensor_mul(out=kh_tm, in0=ktn,
                                                   in1=rat),
                  waits=[c1_done])
            dec_done = cx.val("ev")
            dec_done_gp = ("gp", cx.val("gp"))

            if "D" not in stages:
                continue
            for src_t, dst_t in ((a_tm, a_fm), (kt_tm, ktfm), (qt_tm, qtfm)):
                for gb in range(2):
                    evv = cx.val("ev")
                    for j in range(4):
                        g = gb * 4 + j
                        cx.op("tensor", lambda e, g=g, j=j, src_t=src_t:
                              e.transpose(
                                  p.pb[6].bitcast(BF16)[:, j * 128:
                                                        j * 128 + 128],
                                  src_t[:, g * 128:g * 128 + 128], p.ident),
                              waits=([("ev", dec_done), dec_done_gp,
                                      ("ev", evv),
                                      p.bank_guard[6]] if j == 0 else ()),
                              inc="mm" if j == 3 else None)
                    mmv = cx.val("mm")
                    cx.op("vector", lambda e, gb=gb, dst_t=dst_t:
                          e.tensor_copy(
                              out=dst_t[:, gb * 512:gb * 512 + 512],
                              in_=p.pb[6].bitcast(BF16)[:, 0:512]),
                          waits=[("mm", mmv)])
                    p.bank_guard[6] = ("ev", cx.val("ev"))

            if "E" not in stages:
                continue
            def fmsl(t, h, par):
                g = h // 2
                return t[:, g * 128:(g + 1) * 128][hv(h):hv(h) + 64,
                                                   par * 64:par * 64 + 64]

            def mat_mm_fm(dst_pair_base, lhs_t, rhs_t, waits0):
                import os
                hs = range(0, 16, 2) if os.environ.get("L1E1MODE") == "heven"                     else range(16)
                prs = (0,) if os.environ.get("L1E1MODE") == "par0"                     else (0, 1)
                nmm = 0
                nmax = len(list(hs)) * len(prs) - 1
                for h in hs:
                    for par in prs:
                        w_ = list(waits0) if nmm == 0 else []
                        cx.op("tensor", lambda e, h=h, par=par, lhs_t=lhs_t,
                              rhs_t=rhs_t, b=dst_pair_base: e.matmul(
                                  PB[par * 64:par * 64 + 64,
                                     b + hc(h):b + hc(h) + 64],
                                  fmsl(lhs_t, h, par), fmsl(rhs_t, h, par),
                                  start=True, stop=True,
                                  tile_position=(hv(h), par * 64)),
                              waits=w_, inc="mm" if nmm == nmax else None)
                        nmm += 1
                return cx.val("mm")

            def mat_mm_tm(dst_pair_base, lhs_t, rhs_t, waits0):
                nmm = 0
                for h in range(16):
                    for par in range(2):
                        w_ = list(waits0) if nmm == 0 else []
                        cx.op("tensor", lambda e, h=h, par=par, lhs_t=lhs_t,
                              rhs_t=rhs_t, b=dst_pair_base: e.matmul(
                                  PB[par * 64:par * 64 + 64,
                                     b + hc(h):b + hc(h) + 64],
                                  lhs_t[par * 64:par * 64 + 64,
                                        hc(h):hc(h) + 64],
                                  rhs_t[par * 64:par * 64 + 64,
                                        hc(h):hc(h) + 64],
                                  start=True, stop=True,
                                  tile_position=(par * 64, par * 64)),
                              waits=w_, inc="mm" if nmm == 31 else None)
                        nmm += 1
                return cx.val("mm")

            def mask_evac(dst_sb, mask, base):
                import os
                mmv = cx.val("mm")
                if os.environ.get("L1E1MODE") == "nomask":
                    cx.op("vector", lambda e, dst_sb=dst_sb, base=base:
                          e.tensor_copy(out=dst_sb[:, 0:512],
                                        in_=PB[:, base:base + 512]),
                          waits=[("mm", mmv)])
                    cx.op("vector", lambda e, dst_sb=dst_sb, base=base:
                          e.tensor_copy(out=dst_sb[:, 512:1024],
                                        in_=PB[:, base + 512:base + 1024]))
                    return cx.val("ev")
                cx.op("vector", lambda e, dst_sb=dst_sb, mask=mask,
                      base=base: e.tensor_mul(
                          out=dst_sb[:, 0:512], in0=PB[:, base:base + 512],
                          in1=mask), waits=[("mm", mmv)])
                cx.op("vector", lambda e, dst_sb=dst_sb, mask=mask,
                      base=base: e.tensor_mul(
                          out=dst_sb[:, 512:1024],
                          in0=PB[:, base + 512:base + 1024], in1=mask))
                return cx.val("ev")

            mat_mm_fm(0, a_fm, ktfm,
                      [p.bank_guard[0], p.bank_guard[1], ("ev", cx.val("ev"))])
            ev_ = mask_evac(Bm, MSL, 0)
            p.bank_guard[0] = p.bank_guard[1] = ("ev", ev_)
            if "1" in stages and "2" not in stages:
                continue
            mat_mm_fm(1024, ktfm, a_fm, [p.bank_guard[2], p.bank_guard[3]])
            ev_ = mask_evac(Cm, MSU, 1024)
            p.bank_guard[2] = p.bank_guard[3] = ("ev", ev_)
            mat_mm_fm(0, ktfm, qtfm, [p.bank_guard[0], p.bank_guard[1],
                                      ("ev", cx.val("ev"))])
            ev_ = mask_evac(Mqt, MIU, 0)
            p.bank_guard[0] = p.bank_guard[1] = ("ev", ev_)
            if "2" in stages and "3" not in stages:
                continue

            cx.op("vector", lambda e: e.tensor_add(out=Rmat, in0=Bm,
                                                   in1=p.i64b))
            cx.op("vector", lambda e: e.tensor_add(out=Rp, in0=Cm,
                                                   in1=p.i64b))
            r0_done = cx.val("ev")

            def upd_mms(first_waits, with_r):
                nmm = 0
                nmax = (64 if with_r else 32) - 1
                pairs = ([(0, Cm, Rmat)] if with_r else []) + [(2, Rmat, Cm)]
                for pi, (bb, lhs_t, rhs_t) in enumerate(pairs):
                    for h in range(16):
                        for par in range(2):
                            w_ = first_waits if nmm == 0 else []
                            cx.op("tensor", lambda e, h=h, par=par, bb=bb,
                                  lhs_t=lhs_t, rhs_t=rhs_t: e.matmul(
                                      p.pb4[bb + h % 2][
                                          par * 64:par * 64 + 64,
                                          (h // 2) * 64:(h // 2) * 64 + 64],
                                      lhs_t[par * 64:par * 64 + 64,
                                            hc(h):hc(h) + 64],
                                      rhs_t[par * 64:par * 64 + 64,
                                            hc(h):hc(h) + 64],
                                      start=True, stop=True,
                                      tile_position=(par * 64, par * 64)),
                                  waits=w_, inc="mm" if nmm == nmax else None)
                            nmm += 1
                return cx.val("mm")

            def bc_evac(mmv):
                cx.op("vector", lambda e: e.tensor_copy(
                    out=Bm[:, 0:512], in_=PB[:, 0:512]),
                    waits=[("mm", mmv)])
                cx.op("scalar", lambda e: e.copy(
                    out=Bm[:, 512:1024], in_=PB[:, 512:1024]),
                    waits=[("mm", mmv)])
                cx.op("vector", lambda e: e.tensor_copy(
                    out=Cm[:, 0:512], in_=PB[:, 1024:1536]))
                cx.op("scalar", lambda e: e.copy(
                    out=Cm[:, 512:1024], in_=PB[:, 1536:2048]))
                evv, scv = cx.val("ev"), cx.val("sc")
                p.bank_guard[0] = p.bank_guard[2] = ("ev", evv)
                p.bank_guard[1] = p.bank_guard[3] = ("sc", scv)

            def r_evac(mmv, with_r):
                if with_r:
                    cx.op("vector", lambda e: e.tensor_add(
                        out=Rmat[:, 0:512], in0=Rmat[:, 0:512],
                        in1=p.pb4[0][:, 0:512]), waits=[("mm", mmv)])
                    cx.op("vector", lambda e: e.tensor_add(
                        out=Rmat[:, 512:1024], in0=Rmat[:, 512:1024],
                        in1=p.pb4[1][:, 0:512]))
                cx.op("vector", lambda e: e.tensor_add(
                    out=Rp[:, 0:512], in0=Rp[:, 0:512],
                    in1=p.pb4[2][:, 0:512]),
                    waits=[] if with_r else [("mm", mmv)])
                cx.op("vector", lambda e: e.tensor_add(
                    out=Rp[:, 512:1024], in0=Rp[:, 512:1024],
                    in1=p.pb4[3][:, 0:512]))
                evv = cx.val("ev")
                for b in range(4, 8):
                    p.bank_guard[b] = ("ev", evv)

            mat_mm_tm(0, Cm, Bm, [p.bank_guard[0], p.bank_guard[1],
                                  p.bank_guard[2], p.bank_guard[3],
                                  ("ev", cx.val("ev")),
                                  ("sc", cx.val("sc"))])
            mat_mm_tm(1024, Bm, Cm, [])
            bc_evac(cx.val("mm"))
            for s in range(1, 6):
                with_r = (s < 5)
                upd_mms([p.bank_guard[4], p.bank_guard[5],
                         p.bank_guard[6], p.bank_guard[7],
                         p.bank_guard[0], p.bank_guard[1],
                         p.bank_guard[2], p.bank_guard[3],
                         ("ev", r0_done)], with_r)
                if s < 5:
                    mat_mm_tm(0, Cm, Bm, [])
                    mat_mm_tm(1024, Bm, Cm, [])
                mmv = cx.val("mm")
                r_evac(mmv, with_r)
                if s < 5:
                    bc_evac(mmv)

            if "F" not in stages:
                continue
            cx.op("scalar", lambda e: e.activation(
                out=f32a, in_=og_tm, func=AF.Exp, scale=-1.0),
                waits=[("ev", cx.val("ev"))])
            scv = cx.val("sc")
            cx.op("vector", lambda e: e.tensor_scalar(
                out=f32a, in0=f32a, scalar1=1.0, scalar2=None, op0=ALU.add),
                waits=[("sc", scv)])
            evv = cx.val("ev")
            cx.op("scalar", lambda e: e.activation(
                out=f32a, in_=f32a, func=AF.Ln), waits=[("ev", evv)])
            cx.op("scalar", lambda e: e.activation(
                out=f32a, in_=f32a, func=AF.Exp, scale=-1.0))
            sig_done = ("sc", cx.val("sc"))

            def g_ln(waits0):
                sl = slice(0, 128)
                cx.op("vector", lambda e, sl=sl: e.tensor_mul(
                    out=f32b[sl, :], in0=o_tm[sl, :], in1=f32a[sl, :]),
                    waits=list(waits0) + [sig_done])
                cx.op("vector", lambda e, sl=sl: e.tensor_reduce(
                    out=col1[sl, :], in_=f32b[sl, :],
                    axis=mybir.AxisListType.X, op=ALU.add))
                cx.op("vector", lambda e, sl=sl: e.tensor_scalar(
                    out=col1[sl, :], in0=col1[sl, :], scalar1=1.0 / D,
                    scalar2=None, op0=ALU.mult))
                cx.op("vector", lambda e, sl=sl: e.tensor_scalar(
                    out=f32b[sl, :], in0=f32b[sl, :], scalar1=col1[sl, :],
                    scalar2=None, op0=ALU.subtract))
                evv = cx.val("ev")
                cx.op("scalar", lambda e, sl=sl: e.activation(
                    out=f32a[sl, :], in_=f32b[sl, :], func=AF.Square),
                    waits=[("ev", evv)])
                scv = cx.val("sc")
                cx.op("vector", lambda e, sl=sl: e.tensor_reduce(
                    out=col1[sl, :], in_=f32a[sl, :],
                    axis=mybir.AxisListType.X, op=ALU.add),
                    waits=[("sc", scv)])
                cx.op("vector", lambda e, sl=sl: e.tensor_scalar(
                    out=col1[sl, :], in0=col1[sl, :], scalar1=1.0 / D,
                    scalar2=None, op0=ALU.mult))
                evv = cx.val("ev")
                cx.op("scalar", lambda e, sl=sl: e.activation(
                    out=col1[sl, :], in_=col1[sl, :], func=AF.Ln,
                    bias=self.eps_col[sl, :]), waits=[("ev", evv)])
                cx.op("scalar", lambda e, sl=sl: e.activation(
                    out=col1[sl, :], in_=col1[sl, :], func=AF.Exp,
                    scale=-0.5))
                scv = cx.val("sc")
                cx.op("vector", lambda e, sl=sl: e.tensor_scalar(
                    out=f32b[sl, :], in0=f32b[sl, :], scalar1=col1[sl, :],
                    scalar2=None, op0=ALU.mult), waits=[("sc", scv)])
                cx.op("vector", lambda e, sl=sl: e.tensor_mul(
                    out=obuf[sl, :], in0=f32b[sl, :], in1=nwb[sl, :]))
                return ("ev", cx.val("ev"))

            g_done = [None, None]
            for par in range(2):
                evv0 = cx.val("ev")
                scv0 = cx.val("sc")
                sdec_thunks = []
                first_v = [True]
                first_s = [True]
                for h in range(16):
                    for cpy in range(2):
                        colx = par * 8 + h // 2 + (0 if cpy == (h % 2)
                                                   else 16)
                        if (h % 2) == 0:
                            def th(h=h, cpy=cpy, colx=colx):
                                w_ = ([("ev", evv0), ("sc", scv0)]
                                      if first_v[0] else ())
                                first_v[0] = False
                                cx.op("vector", lambda e, h=h, cpy=cpy,
                                      colx=colx: e.tensor_scalar(
                                          out=sdec[cpy * 64:cpy * 64 + 64,
                                                   hc(h):hc(h) + 64],
                                          in0=s0[cpy * 64:cpy * 64 + 64,
                                                 hc(h):hc(h) + 64],
                                          scalar1=lamc[cpy * 64:
                                                       cpy * 64 + 64,
                                                       colx:colx + 1],
                                          scalar2=None, op0=ALU.mult),
                                      waits=w_)
                        else:
                            def th(h=h, cpy=cpy, colx=colx):
                                w_ = ([("ev", evv0), ("sc", scv0)]
                                      if first_s[0] else ())
                                first_s[0] = False
                                cx.op("scalar", lambda e, h=h, cpy=cpy,
                                      colx=colx: e.activation(
                                          out=sdec[cpy * 64:cpy * 64 + 64,
                                                   hc(h):hc(h) + 64],
                                          in_=s0[cpy * 64:cpy * 64 + 64,
                                                 hc(h):hc(h) + 64],
                                          func=AF.Copy,
                                          scale=lamc[cpy * 64:cpy * 64 + 64,
                                                     colx:colx + 1]),
                                      waits=w_)
                        sdec_thunks.append(th)

                def sdec_drain(k):
                    while sdec_thunks and k > 0:
                        sdec_thunks.pop(0)()
                        k -= 1

                nmm = 0
                for h in range(16):
                    w_ = ([p.bank_guard[4], p.bank_guard[5],
                           ("ev", cx.val("ev"))] if nmm == 0 else [])
                    cx.op("tensor", lambda e, h=h, par=par: e.matmul(
                        p.pb4[h % 2][par * 64:par * 64 + 64,
                                     (h // 2) * 64:(h // 2) * 64 + 64],
                        fmsl(a_fm, h, par),
                        s0[hv(h):hv(h) + 64, hc(h):hc(h) + 64],
                        start=True, stop=True,
                        tile_position=(hv(h), par * 64)),
                        waits=w_, inc="mm" if nmm == 15 else None)
                    nmm += 1
                mmv = cx.val("mm")
                cx.op("vector", lambda e, par=par: e.tensor_add(
                    out=rhs_sb[par * 64:par * 64 + 64, 0:512],
                    in0=p.pb4[0][par * 64:par * 64 + 64, 0:512],
                    in1=v_tm[par * 64:par * 64 + 64, 0:512]),
                    waits=[("mm", mmv), v_done])
                cx.op("vector", lambda e, par=par: e.tensor_add(
                    out=rhs_sb[par * 64:par * 64 + 64, 512:1024],
                    in0=p.pb4[1][par * 64:par * 64 + 64, 0:512],
                    in1=v_tm[par * 64:par * 64 + 64, 512:1024]))
                rhs_done = cx.val("ev")
                p.bank_guard[4] = p.bank_guard[5] = ("ev", rhs_done)
                sdec_drain(12)

                nmm = 0
                for h in range(16):
                    w_ = ([p.bank_guard[6], p.bank_guard[7],
                           ("ev", rhs_done)] if nmm == 0 else [])
                    cx.op("tensor", lambda e, h=h, par=par: e.matmul(
                        p.pb4[2 + h % 2][par * 64:par * 64 + 64,
                                         (h // 2) * 64:(h // 2) * 64 + 64],
                        Tt[par * 64:par * 64 + 64, hc(h):hc(h) + 64],
                        rhs_sb[par * 64:par * 64 + 64,
                               hc(h):hc(h) + 64],
                        start=True, stop=True,
                        tile_position=(par * 64, par * 64)),
                        waits=w_, inc="mm" if nmm == 15 else None)
                    nmm += 1
                mmv = cx.val("mm")
                cx.op("vector", lambda e, par=par: e.tensor_copy(
                    out=u_sb[par * 64:par * 64 + 64, 0:512],
                    in_=p.pb4[2][par * 64:par * 64 + 64, 0:512]),
                    waits=[("mm", mmv)])
                cx.op("scalar", lambda e, par=par: e.copy(
                    out=u_sb[par * 64:par * 64 + 64, 512:1024],
                    in_=p.pb4[3][par * 64:par * 64 + 64, 0:512]),
                    waits=[("mm", mmv)])
                u_done = cx.val("ev")
                u_done_s = cx.val("sc")
                p.bank_guard[6] = ("ev", u_done)
                p.bank_guard[7] = ("sc", u_done_s)
                sdec_drain(12)

                nmm = 0
                for h in range(16):
                    w_ = ([p.bank_guard[4], p.bank_guard[5],
                           p.bank_guard[6], p.bank_guard[7],
                           ("ev", u_done), ("sc", u_done_s)]
                          if nmm == 0 else [])
                    cx.op("tensor", lambda e, h=h, par=par: e.matmul(
                        p.pb4[h % 2][par * 64:par * 64 + 64,
                                     (h // 2) * 64:(h // 2) * 64 + 64],
                        fmsl(qtfm, h, par),
                        s0[hv(h):hv(h) + 64, hc(h):hc(h) + 64],
                        start=True, stop=True,
                        tile_position=(hv(h), par * 64)),
                        waits=w_)
                    cx.op("tensor", lambda e, h=h, par=par: e.matmul(
                        p.pb4[2 + h % 2][par * 64:par * 64 + 64,
                                         (h // 2) * 64:(h // 2) * 64 + 64],
                        Mqt[par * 64:par * 64 + 64, hc(h):hc(h) + 64],
                        u_sb[par * 64:par * 64 + 64, hc(h):hc(h) + 64],
                        start=True, stop=True,
                        tile_position=(par * 64, par * 64)),
                        inc="mm" if nmm == 15 else None)
                    nmm += 1
                mmv = cx.val("mm")
                cx.op("scalar", lambda e, par=par: e.copy(
                    out=o_tm[par * 64:par * 64 + 64, 0:512],
                    in_=p.pb4[0][par * 64:par * 64 + 64, 0:512]),
                    waits=[("mm", mmv)])
                cx.op("scalar", lambda e, par=par: e.copy(
                    out=o_tm[par * 64:par * 64 + 64, 512:1024],
                    in_=p.pb4[1][par * 64:par * 64 + 64, 0:512]))
                scv_o = cx.val("sc")
                cx.op("vector", lambda e, par=par: e.tensor_add(
                    out=o_tm[par * 64:par * 64 + 64, 0:512],
                    in0=o_tm[par * 64:par * 64 + 64, 0:512],
                    in1=p.pb4[2][par * 64:par * 64 + 64, 0:512]),
                    waits=[("sc", scv_o)])
                cx.op("vector", lambda e, par=par: e.tensor_add(
                    out=o_tm[par * 64:par * 64 + 64, 512:1024],
                    in0=o_tm[par * 64:par * 64 + 64, 512:1024],
                    in1=p.pb4[3][par * 64:par * 64 + 64, 0:512]))
                o_done = cx.val("ev")
                p.bank_guard[4] = p.bank_guard[5] = ("sc", scv_o)
                p.bank_guard[6] = p.bank_guard[7] = ("ev", o_done)
                sdec_drain(99)
                sdec_done = [("ev", cx.val("ev")), ("sc", cx.val("sc"))]

                nmm = 0
                for h in range(16):
                    for cpy in range(2):
                        w_ = ([p.bank_guard[6], p.bank_guard[7],
                               ("ev", u_done), ("sc", u_done_s)]
                              if nmm == 0 else [])
                        cx.op("tensor", lambda e, h=h, par=par, cpy=cpy:
                              e.matmul(
                                  p.pb4[2 + h % 2][cpy * 64:cpy * 64 + 64,
                                                   (h // 2) * 64:
                                                   (h // 2) * 64 + 64],
                                  kh_tm[par * 64:par * 64 + 64,
                                        h * 64:h * 64 + 64],
                                  u_sb[par * 64:par * 64 + 64,
                                       hc(h):hc(h) + 64],
                                  start=True, stop=True,
                                  tile_position=(par * 64, cpy * 64)),
                              waits=w_, inc="mm" if nmm == 31 else None)
                        nmm += 1
                mmv = cx.val("mm")
                cx.op("vector", lambda e: e.tensor_add(
                    out=s0[:, 0:512], in0=sdec[:, 0:512],
                    in1=p.pb4[2][:, 0:512]),
                    waits=[("mm", mmv)] + sdec_done)
                cx.op("vector", lambda e: e.tensor_add(
                    out=s0[:, 512:1024], in0=sdec[:, 512:1024],
                    in1=p.pb4[3][:, 0:512]))
                s_done = cx.val("ev")
                p.bank_guard[6] = p.bank_guard[7] = ("ev", s_done)
                g_done[par] = ("ev", o_done)

            if "G" not in stages:
                continue
            g_ln([g_done[1]])
            oln_done = cx.val("ev")

            for gb in range(2):
                evv = cx.val("ev")
                for j in range(4):
                    g = gb * 4 + j
                    cx.op("tensor", lambda e, g=g, j=j: e.transpose(
                        p.pb[6].bitcast(BF16)[:, j * 128:j * 128 + 128],
                        obuf[:, g * 128:g * 128 + 128], p.ident),
                        waits=([("ev", oln_done), ("ev", evv),
                                p.bank_guard[6]] if j == 0 else ()),
                        inc="mm" if j == 3 else None)
                mmv = cx.val("mm")
                cx.op("vector", lambda e, gb=gb: e.tensor_copy(
                    out=ofm[:, gb * 512:gb * 512 + 512],
                    in_=p.pb[6].bitcast(BF16)[:, 0:512]),
                    waits=[("mm", mmv)])
                p.bank_guard[6] = ("ev", cx.val("ev"))
            ofm_done = cx.val("ev")

            nmm = 0
            for oc in range(8):
                bk = oc // 4
                col = (oc % 4) * 128
                for dc in range(NC):
                    w_ = []
                    if nmm == 0:
                        w_ = [p.bank_guard[0], p.bank_guard[1],
                              ("ev", ofm_done)]
                    cx.op("tensor", lambda e, oc=oc, dc=dc, bk=bk, col=col:
                          e.matmul(
                              p.pb[bk][:, col:col + 128],
                              wo1[:, dc, oc * 128:oc * 128 + 128],
                              ofm[:, dc * 128:dc * 128 + 128],
                              start=(dc == 0), stop=(dc == NC - 1)),
                          waits=w_, inc="mm" if dc == NC - 1 else None)
                    nmm += 1
            mmv = cx.val("mm")
            cx.op("vector", lambda e: e.tensor_copy(
                out=dsb[:, 0:512], in_=PB[:, 0:512]), waits=[("mm", mmv)])
            cx.op("vector", lambda e: e.tensor_copy(
                out=dsb[:, 512:1024], in_=PB[:, 512:1024]))
            d_done = cx.val("ev")
            p.bank_guard[0] = p.bank_guard[1] = ("ev", d_done)
            cx.dma("sync", p.da1_dram[:, t0:t0 + 128].rearrange(
                "(c p) t -> p c t", p=128),
                dsb.rearrange("p (c t) -> p c t", c=NC),
                waits=[("ev", d_done)], inc="da1s")

        p.prelease()

    def emit_l1_ffn_residual(self):
        cx = self.ctx
        p = self
        TW = 512
        p.pmark()
        al = p.palloc
        mu = al(NC * 4 * 2, F32, NC * 2)
        om = al(NC * 4 * 2, F32, NC * 2)
        xr = al(NC * TW * 2, BF16, NC * TW).rearrange("p (c t) -> p c t", c=NC)
        xk = al(NC * TW * 2, BF16, NC * TW).rearrange("p (c t) -> p c t", c=NC)
        rsb = al(NC * TW * 2, BF16, NC * TW).rearrange("p (c t) -> p c t", c=NC)
        kact = al(NC * TW * 2, BF16, NC * TW).rearrange("p (c t) -> p c t", c=NC)
        x1s = al(NC * TW * 4, F32, NC * TW).rearrange("p (c t) -> p c t", c=NC)
        das = al(NC * TW * 2, BF16, NC * TW).rearrange("p (c t) -> p c t", c=NC)
        wb = [al(NC * 128 * 2, BF16, NC * 128).rearrange(
            "p (c o) -> p c o", c=NC) for _ in range(2)]
        fsc = al(TW * 4, F32, TW)

        cx.dma("sync", mu[:, 0:NC], p.di["mu_r"], inc="tbl")
        cx.dma("sync", mu[:, NC:2 * NC], p.di["mu_k"], inc="tbl")
        muv = ("tbl", cx.val("tbl"))
        cx.op("vector", lambda e: e.tensor_scalar(
            out=om, in0=mu, scalar1=-1.0, scalar2=1.0,
            op0=ALU.mult, op1=ALU.add), waits=[muv])
        self.w4_guard = [("mm", 0), ("mm", 0)]
        xn_ready = ("ev", cx.val("ev"))

        for ti in range(N // TW):
            t0 = ti * TW
            lds = f"ldx{ti % 2}"
            cx.dma("sync", x1s, p.x1_dram[:, t0:t0 + TW].rearrange(
                "(c p) t -> p c t", p=128),
                waits=[("ev", cx.val("ev")), ("st", cx.val("st"))], inc=lds)
            cx.dma("sync", das, p.da1_dram[:, t0:t0 + TW].rearrange(
                "(c p) t -> p c t", p=128),
                waits=[("da1s", cx.val("da1s"))], inc=lds)
            dav = (lds, cx.val(lds))
            x1v = dav

            for dst, mcol in ((xr, 0), (xk, NC)):
                for c in range(NC):
                    w_ = [xn_ready] if (ti == 0 and c == 0 and mcol == 0)                         else []
                    cx.op("vector", lambda e, dst=dst, c=c, mcol=mcol, t0=t0:
                          e.tensor_scalar(
                              out=dst[:, c, :], in0=p.xn[:, c, t0:t0 + TW],
                              scalar1=mu[:, mcol + c:mcol + c + 1],
                              scalar2=None, op0=ALU.mult), waits=w_)
                    if ti == 0:
                        cx.op("vector", lambda e, dst=dst, c=c, mcol=mcol:
                              e.scalar_tensor_tensor(
                                  out=dst[:, c, 1:TW],
                                  in0=p.xn[:, c, 0:TW - 1],
                                  scalar=om[:, mcol + c:mcol + c + 1],
                                  in1=dst[:, c, 1:TW],
                                  op0=ALU.mult, op1=ALU.add))
                    else:
                        cx.op("vector", lambda e, dst=dst, c=c, mcol=mcol,
                              t0=t0: e.scalar_tensor_tensor(
                                  out=dst[:, c, :],
                                  in0=p.xn[:, c, t0 - 1:t0 + TW - 1],
                                  scalar=om[:, mcol + c:mcol + c + 1],
                                  in1=dst[:, c, :],
                                  op0=ALU.mult, op1=ALU.add))
            mix_done = cx.val("ev")

            for wname, src_t, dst, kind in (("WrT1", xr, rsb, "sig"),
                                            ("WkT1", xk, kact, "relu2")):
                for oc in range(8):
                    par = oc % 2
                    wsem = f"wl{par}"
                    cx.dma("gpsimd", wb[par],
                           p.di[wname][:, oc * 128:(oc + 1) * 128].rearrange(
                               "(c p) o -> p c o", p=128),
                           waits=[(wsem, cx.val(wsem)), self.w4_guard[par]],
                           inc=wsem)
                    wlv = (wsem, cx.val(wsem))
                    bk = oc % 2
                    for c in range(NC):
                        w_ = [wlv]
                        if c == 0:
                            w_ += [p.bank_guard[bk], ("ev", mix_done)]
                        cx.op("tensor", lambda e, oc=oc, c=c, par=par, bk=bk,
                              src_t=src_t: e.matmul(
                                  p.pb[bk][:, 0:TW], wb[par][:, c, :],
                                  src_t[:, c, :],
                                  start=(c == 0), stop=(c == NC - 1)),
                              waits=w_, inc="mm" if c == NC - 1 else None)
                    mmv = cx.val("mm")
                    self.w4_guard[par] = ("mm", mmv)
                    if kind == "sig":
                        cx.op("scalar", lambda e, oc=oc, dst=dst, bk=bk:
                              e.activation(out=dst[:, oc, :],
                                           in_=p.pb[bk][:, 0:TW],
                                           func=AF.Sigmoid),
                              waits=[("mm", mmv)])
                        p.bank_guard[bk] = ("sc", cx.val("sc"))
                    else:
                        cx.op("vector", lambda e, oc=oc, bk=bk:
                              e.tensor_scalar(
                                  out=fsc, in0=p.pb[bk][:, 0:TW],
                                  scalar1=0.0, scalar2=None, op0=ALU.max),
                              waits=[("mm", mmv), ("sc", cx.val("sc"))])
                        p.bank_guard[bk] = ("ev", cx.val("ev"))
                        evv = cx.val("ev")
                        cx.op("scalar", lambda e, oc=oc, dst=dst:
                              e.activation(out=dst[:, oc, :], in_=fsc,
                                           func=AF.Square),
                              waits=[("ev", evv)])

            for oc in range(8):
                par = oc % 2
                wsem = f"wl{par}"
                cx.dma("gpsimd", wb[par],
                       p.di["WvT1"][:, oc * 128:(oc + 1) * 128].rearrange(
                           "(c p) o -> p c o", p=128),
                       waits=[(wsem, cx.val(wsem)), self.w4_guard[par]],
                       inc=wsem)
                wlv = (wsem, cx.val(wsem))
                bk = oc % 2
                for c in range(NC):
                    w_ = [wlv]
                    if c == 0:
                        w_ += [p.bank_guard[bk], ("sc", cx.val("sc"))]
                    cx.op("tensor", lambda e, oc=oc, c=c, par=par, bk=bk:
                          e.matmul(
                              p.pb[bk][:, 0:TW], wb[par][:, c, :],
                              kact[:, c, :],
                              start=(c == 0), stop=(c == NC - 1)),
                          waits=w_, inc="mm" if c == NC - 1 else None)
                mmv = cx.val("mm")
                self.w4_guard[par] = ("mm", mmv)
                cx.op("vector", lambda e, oc=oc, bk=bk: e.tensor_mul(
                    out=fsc, in0=rsb[:, oc, :], in1=p.pb[bk][:, 0:TW]),
                    waits=[("mm", mmv), x1v, dav])
                p.bank_guard[bk] = ("ev", cx.val("ev"))
                cx.op("vector", lambda e, oc=oc: e.tensor_add(
                    out=fsc, in0=fsc, in1=das[:, oc, :]))
                cx.op("vector", lambda e, oc=oc: e.tensor_add(
                    out=x1s[:, oc, :], in0=x1s[:, oc, :], in1=fsc))
            y_done = cx.val("ev")
            cx.dma("sync", p.y[:, t0:t0 + TW].rearrange(
                "(c p) t -> p c t", p=128), x1s,
                waits=[("ev", y_done)], inc="st")
        p.prelease()



def build_program(dbg=(), phases=4):
    nc = bass.Bass("TRN2", target_bir_lowering=False, debug=False)
    p = Prog(nc, dbg=dbg)
    cx = p.ctx

    p.load_consts()

    TW = 512
    p.pmark()
    x_stage = [p.palloc(NC * TW * 4, F32, NC * TW).rearrange(
        "p (c t) -> p c t", c=NC) for _ in range(2)]
    scratch_bf = p.palloc(2 * NC * TW * 2, BF16, 2 * NC * TW)
    scratch_f32 = p.palloc(TW * 4, F32, TW)
    rows_f32 = p.palloc(3 * TW * 4, F32, 3 * TW)[0:1, :]
    rows_bf = p.palloc(2 * TW * 2, BF16, 2 * TW)[0:1, :]
    wr_f = p.palloc(NC * 8 * 4, F32, NC * 8).rearrange(
        "p (c o) -> p c o", c=NC)
    ncol = p.palloc(4, F32, 1)
    lrow = p.palloc(TW * 4, F32, TW)[0:8, :]

    cx.dma("sync", wr_f, p.di["Wr"].rearrange("(c p) o -> p c o", p=128),
           inc="wrf")
    wr_ready = cx.val("wrf")
    for c in range(NC):
        cx.op("tensor", lambda e, c=c: e.matmul(
            p.pb[5][0:1, 0:8], p.one_col, wr_f[:, c, :],
            start=(c == 0), stop=(c == NC - 1)),
            waits=[("wrf", wr_ready), p.sconst_ready] if c == 0 else (),
            inc="mm" if c == NC - 1 else None)
    cx.op("scalar", lambda e: e.copy(out=lrow[0:1, 0:8],
                                     in_=p.pb[5][0:1, 0:8]),
          waits=[("mm", cx.val("mm"))])
    scv_cs = cx.val("sc")
    cx.op("tensor", lambda e: e.transpose(
        p.pb[6][0:8, 0:1], lrow[0:1, 0:8], p.identf[0:1, 0:1]),
        waits=[("sc", scv_cs)], inc="mm")
    cx.op("scalar", lambda e: e.mul(out=ncol[0:8, :], in_=p.pb[6][0:8, 0:1],
                                    mul=-1.0), waits=[("mm", cx.val("mm"))])
    ncol_ready = cx.val("sc")

    for ti in range(N // TW):
        t0 = ti * TW
        buf = x_stage[ti % 2]
        lds = f"ldx{ti % 2}"
        waits = [("ev", cx.val("ev")), ("sc", cx.val("sc"))] if ti >= 2             else [("ld", p.consts_ready)]
        cx.dma("sync", buf,
               p.di["x_fm"][:, t0:t0 + TW].rearrange("(c p) t -> p c t", p=128),
               waits=waits, inc=lds)
        b4 = (0, 1, 2, 3)
        p.emit_ln(p.xn[:, :, t0:t0 + TW], TW, buf, (lds, cx.val(lds)),
                  b4, scratch_bf, scratch_f32, rows_f32, rows_bf)
        for c in range(NC):
            cx.op("tensor", lambda e, c=c, buf=buf: e.matmul(
                p.pb[4][0:8, 0:TW], wr_f[:, c, :], buf[:, c, :],
                start=(c == 0), stop=(c == NC - 1)),
                waits=([("sc", ncol_ready), p.bank_guard[4]] if c == 0 else ()),
                inc="mm" if c == NC - 1 else None)
        mmv = cx.val("mm")
        cx.op("scalar", lambda e: e.copy(out=lrow, in_=p.pb[4][0:8, 0:TW]),
              waits=[("mm", mmv), ("lgs", cx.val("lgs"))])
        p.bank_guard[4] = ("sc", cx.val("sc"))
        cx.op("vector", lambda e: e.scalar_tensor_tensor(
            out=lrow, in0=p.pb[2][0:8, 0:TW], scalar=ncol[0:8, :], in1=lrow,
            op0=ALU.mult, op1=ALU.add), waits=[("sc", cx.val("sc"))])
        cx.op("vector", lambda e: e.tensor_mul(
            out=lrow, in0=lrow, in1=p.pb[3][0:8, 0:TW]))
        lgv = cx.val("ev")
        cx.dma("sync", p.logits_dram[:, t0:t0 + TW], lrow,
               waits=[("ev", lgv)], inc="lgs")
    p.prelease()

    if "xn0" in p.dbg:
        out = p.dbg_out("xn0", [128, NC, N], BF16)
        cx.dma("sync", out, p.xn, waits=[("ev", cx.val("ev"))], inc="st")

    p.emit_l0_attention()

    if "o0" in p.dbg:
        out = p.dbg_out("o0", [128, NC, N], BF16)
        cx.dma("sync", out, p.big, waits=[("ev", cx.val("ev")),
                                          ("mm", cx.val("mm"))], inc="st")

    p.pmark()
    p.m_act = p.palloc(16 * 512 * 2, BF16, 16 * 512).rearrange(
        "p (a b) -> p a b", a=16)
    act_flat = p.m_act.rearrange("p a b -> p (a b)")
    dfxs = p.palloc(4096 * 4, F32, 4096)
    p.m_df = dfxs[:, 0:2048].bitcast(BF16).rearrange("p (a b) -> p a b", a=NC)
    p.m_xs = dfxs[:, 2048:4096].rearrange("p (a b) -> p a b", a=NC)
    p.m_ring = [p.palloc(4096 * 2, BF16, 4096) for _ in range(3)]
    p.m_wbc = p.palloc(512 * 4, F32, 512)
    p.m_sab = p.palloc(512 * 2, BF16, 512)
    p.m_b1a = p.palloc(128 * 4, F32, 128)
    p.m_b1g = p.palloc(128 * 4, F32, 128)
    p.m_b2 = p.palloc(1024 * 2, BF16, 1024)[0:8, :]
    p.w_bf = p.palloc(N * 2, BF16, N)[0:8, :]
    p.r_wr = p.palloc(NC * 8 * 2, BF16, NC * 8).rearrange(
        "p (c o) -> p c o", c=NC)
    p.r_br = p.palloc(4, F32, 1)[0:8, :]
    p.r_logit = dfxs[0:8, 0:4096]
    p.r_ltm = act_flat[:, 0:512].bitcast(F32).rearrange(
        "p (a b) -> p a b", a=32)
    p.r_scr = act_flat[:, 512:1792].bitcast(F32)
    p.m_lnbf = act_flat[:, 0:4096]
    p.m_lnf = act_flat[:, 4096:4608].bitcast(F32)
    p.m_lrows = act_flat[:, 4608:6144].bitcast(F32)[0:1, :]
    p.m_lrbf = act_flat[:, 6144:6656][0:1, :]

    p.emit_router()
    p.emit_l0_moe_residual()
    p.prelease()

    if "x1" in p.dbg:
        out = p.dbg_out("x1", [D, N], F32)
        cx.dma("sync", out, p.x1_dram, waits=[("st", cx.val("st"))], inc="st")
    if "xn1" in p.dbg:
        out = p.dbg_out("xn1", [128, NC, N], BF16)
        cx.dma("sync", out, p.xn, waits=[("ev", cx.val("ev"))], inc="st")

    if phases >= 3:
        import os
        p.emit_l1_attention(
            ncp=int(os.environ.get("L1CP", "32")),
            stages=os.environ.get("L1STAGES", "ABCDEFG"))

    if "da1" in p.dbg and phases >= 3:
        out = p.dbg_out("da1", [D, N], BF16)
        cx.dma("sync", out, p.da1_dram,
               waits=[("da1s", cx.val("da1s"))], inc="st")

    if phases >= 4:
        p.emit_l1_ffn_residual()
    else:
        cx.dma("sync", p.y, p.x1_dram, waits=[("st", cx.val("st"))], inc="st")

    cx.op("sync", lambda e: e.nop(), waits=[("st", cx.val("st")),
                                            ("ev", cx.val("ev"))])
    cx.finalize()
    return nc, p



def kernel(**inputs):
    hp = host_prepare(inputs)
    nc, p = build_program()
    shared = {k: v for k, v in hp.items() if k != "x_fm"}
    in_maps = [dict(shared, x_fm=hp["x_fm"][b]) for b in range(8)]
    res = run_bass_kernel_spmd(nc, in_maps, core_ids=list(range(8)))
    out = np.empty((8, N, D), np.float32)
    for b in range(8):
        out[b] = np.asarray(res.results[b]["y"]).T
    return out


# version 26
# speedup vs baseline: 1.1841x; 1.0183x over previous
import numpy as np
import ml_dtypes
import concourse.bass as bass
import concourse.mybir as mybir
from concourse.bass_utils import run_bass_kernel_spmd

F32 = mybir.dt.float32
BF16 = mybir.dt.bfloat16
AF = mybir.ActivationFunctionType
ALU = mybir.AluOpType

D = 1024
N = 4096
NC = 8
H = 16
HD = 64
NBANK_W = 512

BF = np.dtype(ml_dtypes.bfloat16)



class _BankView:
    def __init__(self, p):
        self.p = p
    def __getitem__(self, i):
        if i < 4:
            return self.p.pb_s[:, i * 512:(i + 1) * 512]
        return self.p.pb4[i - 4]


class Ctx:

    ENGINES = ("tensor", "vector", "scalar", "gpsimd", "sync")

    def __init__(self, nc):
        self.nc = nc
        self.q = {e: [] for e in self.ENGINES}
        self.sem = {}
        self.count = {}

    def new_sem(self, name):
        self.sem[name] = None
        self.count[name] = 0
        return name

    def val(self, sem):
        return self.count[sem]

    AUTOSEM = {"scalar": "sc", "vector": "ev", "gpsimd": "gp"}

    def op(self, eng_name, fn, waits=(), inc=None, inc_n=1):
        waits = list(waits)
        auto = self.AUTOSEM.get(eng_name)
        if auto is not None:
            assert inc is None or inc == auto, \
                f"{eng_name} ops always signal via {auto}"
            waits.append((auto, self.count[auto]))
            inc, inc_n = auto, 1
        if inc is not None:
            self.count[inc] += inc_n
        waits = tuple(waits)

        def closure(eng, fn=fn, waits=waits, inc=inc, inc_n=inc_n):
            for s, v in waits:
                if v > 0:
                    eng.wait_ge(self.sem[s], v)
            instr = fn(eng)
            if inc is not None:
                assert instr is not None
                instr.then_inc(self.sem[inc], inc_n)

        self.q[eng_name].append(closure)

    def dma(self, eng_name, out, in_, waits=(), inc=None):
        if inc is not None:
            self.count[inc] += 16
        waits = tuple(waits)

        def closure(eng, out=out, in_=in_, waits=waits, inc=inc):
            for s, v in waits:
                if v > 0:
                    eng.wait_ge(self.sem[s], v)
            instr = eng.dma_start(out=out, in_=in_)
            if inc is not None:
                instr.then_inc(self.sem[inc], 16)

        self.q[eng_name].append(closure)

    def fence(self):
        snap = [(s, v) for s, v in self.count.items() if v > 0]
        for e in self.ENGINES:
            def closure(eng, snap=tuple(snap)):
                for s, v in snap:
                    eng.wait_ge(self.sem[s], v)
            self.q[e].append(closure)

    def barrier(self, name):
        tgt = {}
        for e in self.ENGINES:
            s = f"bar_{e}"
            if s not in self.sem:
                self.new_sem(s)
            self.count[s] += 1
            tgt[e] = self.count[s]

        for e in self.ENGINES:
            def closure(eng, e=e, tgt=dict(tgt)):
                if e == "sync":
                    eng.nop().then_inc(self.sem[f"bar_{e}"], 1)
                else:
                    eng.engine_nop().then_inc(self.sem[f"bar_{e}"], 1) \
                        if hasattr(eng, "engine_nop") else \
                        eng.nop().then_inc(self.sem[f"bar_{e}"], 1)
                for e2 in self.ENGINES:
                    eng.wait_ge(self.sem[f"bar_{e2}"], tgt[e2])
            self.q[e].append(closure)

    def finalize(self):
        nc = self.nc
        import contextlib
        with contextlib.ExitStack() as stack:
            for name in list(self.sem):
                self.sem[name] = stack.enter_context(nc.semaphore(name))
            blk = stack.enter_context(nc.Block())
            for eng_name in self.ENGINES:
                ops = self.q[eng_name]
                getattr(blk, eng_name)(lambda eng, ops=ops: [f(eng) for f in ops])



def _bf16(a):
    return np.ascontiguousarray(a.astype(BF))


def host_prepare(inputs):
    d = {}
    x = inputs["L0_attn_x"]
    d["x_fm"] = [np.ascontiguousarray(x[b].T) for b in range(8)]

    Wqkv = inputs["L0_attn_W_qkv"]
    WqkvT = np.ascontiguousarray(Wqkv.T)
    perm = np.empty(HD, np.int64)
    perm[:32] = np.arange(0, 64, 2)
    perm[32:] = np.arange(1, 64, 2)
    cols = np.arange(3 * D)
    for h in range(H):
        cols[h * HD:(h + 1) * HD] = h * HD + perm
        cols[D + h * HD:D + (h + 1) * HD] = D + h * HD + perm
    d["WqkvT"] = np.ascontiguousarray(WqkvT[:, cols])
    d["WprojT"] = np.ascontiguousarray(inputs["L0_attn_W_proj"].T)

    inv = 1.0 / (10000.0 ** (np.arange(0, HD, 2, np.float32) / HD))
    tt = np.arange(N, dtype=np.float32)
    fr = np.outer(tt, inv)
    c_fm = np.ascontiguousarray(np.cos(fr).T.astype(np.float32))
    s_fm = np.ascontiguousarray(np.sin(fr).T.astype(np.float32))
    sc = 1.0 / np.sqrt(HD)
    C1q = np.tile(c_fm * sc, (4, 1))
    C2q = np.concatenate([-s_fm * sc, s_fm * sc, -s_fm * sc, s_fm * sc], 0)
    d["rope_c1q"], d["rope_c2q"] = _bf16(C1q), _bf16(C2q)

    W1 = inputs["L0_ffn_W1"]
    d["W1a"] = np.ascontiguousarray(W1[:, :, :2048])
    d["W1g"] = np.ascontiguousarray(W1[:, :, 2048:])
    d["W2"] = np.ascontiguousarray(inputs["L0_ffn_W2"])
    b1 = inputs["L0_ffn_b1"]
    d["b1a"] = np.ascontiguousarray(
        b1[:, :2048].reshape(8, 16, 128).transpose(2, 0, 1).reshape(128, 128))
    d["b1g"] = np.ascontiguousarray(
        b1[:, 2048:].reshape(8, 16, 128).transpose(2, 0, 1).reshape(128, 128))
    d["b2"] = np.ascontiguousarray(inputs["L0_ffn_b2"])
    d["Wr"] = np.ascontiguousarray(inputs["L0_ffn_Wr"])
    d["br"] = np.ascontiguousarray(inputs["L0_ffn_br"])

    hcperm = np.empty(D, np.int64)
    for h in range(H):
        hcperm[(h % 2) * 512 + (h // 2) * 64:(h % 2) * 512 + (h // 2) * 64
               + HD] = np.arange(h * HD, (h + 1) * HD)
    for k in ("Wq", "Wk", "Wgamma", "Wf1", "Wf2", "Wog1"):
        d[k] = np.ascontiguousarray(inputs[f"L1_attn_{k}"])
    d["Wv"] = np.ascontiguousarray(inputs["L1_attn_Wv"][:, hcperm])
    d["Wog2"] = np.ascontiguousarray(inputs["L1_attn_Wog2"][:, hcperm])
    d["norm_w"] = np.ascontiguousarray(inputs["L1_attn_norm_w"][hcperm])
    d["Wo"] = np.ascontiguousarray(inputs["L1_attn_Wo"][hcperm, :])

    d["WrT1"] = np.ascontiguousarray(inputs["L1_ffn_W_r"].T)
    d["WkT1"] = np.ascontiguousarray(inputs["L1_ffn_W_k"].T)
    d["WvT1"] = np.ascontiguousarray(inputs["L1_ffn_W_v"].T)
    d["mu_r"] = np.ascontiguousarray(inputs["L1_ffn_mu_r"].reshape(NC, 128).T)
    d["mu_k"] = np.ascontiguousarray(inputs["L1_ffn_mu_k"].reshape(NC, 128).T)
    return d


def _inline_consts(nc):
    c = {}
    ident = np.eye(128, dtype=BF)
    c["ident"] = nc.inline_tensor(ident, "c_ident")
    ones = np.ones((128, 128), dtype=BF)
    c["ones"] = nc.inline_tensor(ones, "c_ones")
    swap = np.zeros((128, 128), dtype=BF)
    for g in range(2):
        b = g * 64
        for i in range(32):
            swap[b + 32 + i, b + i] = 1.0
            swap[b + i, b + 32 + i] = 1.0
    c["swap"] = nc.inline_tensor(swap, "c_swap")
    ut = np.triu(np.ones((64, 64), np.float32)).astype(BF)
    c["ut64"] = nc.inline_tensor(np.tile(ut, (2, 1)), "c_ut64")
    c["ident_f32"] = nc.inline_tensor(np.eye(128, dtype=np.float32), "c_idf")
    sel = np.zeros((8, 8 * 128), dtype=BF)
    for e in range(8):
        sel[e, e * 128:(e + 1) * 128] = 1.0
    c["sel"] = nc.inline_tensor(sel, "c_sel")
    sl = np.tril(np.ones((64, 64), np.float32), -1).astype(np.float32)
    il = np.tril(np.ones((64, 64), np.float32)).astype(np.float32)
    iu = il.T.copy()
    su = sl.T.copy()
    c["mask_sl"] = nc.inline_tensor(
        np.tile(np.tile(sl, (1, 8)), (2, 1)).astype(BF), "c_msl")
    c["mask_iu"] = nc.inline_tensor(
        np.tile(np.tile(iu, (1, 8)), (2, 1)).astype(BF), "c_miu")
    c["mask_su"] = nc.inline_tensor(
        np.tile(np.tile(su, (1, 8)), (2, 1)).astype(BF), "c_msu")
    i64b = np.zeros((128, 1024), dtype=BF)
    for pr in range(128):
        for h in range(16):
            i64b[pr, h * 64 + (pr % 64)] = 1.0
    c["i64b"] = nc.inline_tensor(i64b, "c_i64b")
    return c



class Prog:

    def __init__(self, nc, dbg=()):
        self.nc = nc
        self.dbg = set(dbg)
        self.c = _inline_consts(nc)
        ap = lambda t: t.ap()

        di = {}
        di["x_fm"] = nc.dram_tensor("x_fm", [D, N], F32, kind="ExternalInput")
        di["WqkvT"] = nc.dram_tensor("WqkvT", [D, 3 * D], F32, kind="ExternalInput")
        di["WprojT"] = nc.dram_tensor("WprojT", [D, D], F32, kind="ExternalInput")
        for k in ("rope_c1q", "rope_c2q"):
            di[k] = nc.dram_tensor(k, [128, N], BF16, kind="ExternalInput")
        di["W1a"] = nc.dram_tensor("W1a", [8, D, 2048], F32, kind="ExternalInput")
        di["W1g"] = nc.dram_tensor("W1g", [8, D, 2048], F32, kind="ExternalInput")
        di["W2"] = nc.dram_tensor("W2", [8, 2048, D], F32, kind="ExternalInput")
        di["b1a"] = nc.dram_tensor("b1a", [128, 128], F32, kind="ExternalInput")
        di["b1g"] = nc.dram_tensor("b1g", [128, 128], F32, kind="ExternalInput")
        di["b2"] = nc.dram_tensor("b2", [8, D], F32, kind="ExternalInput")
        di["Wr"] = nc.dram_tensor("Wr", [D, 8], F32, kind="ExternalInput")
        di["br"] = nc.dram_tensor("br", [8], F32, kind="ExternalInput")
        for k in ("Wq", "Wk", "Wv", "Wo"):
            di[k] = nc.dram_tensor(k, [D, D], F32, kind="ExternalInput")
        di["Wgamma"] = nc.dram_tensor("Wgamma", [D, H], F32, kind="ExternalInput")
        di["Wf1"] = nc.dram_tensor("Wf1", [D, HD], F32, kind="ExternalInput")
        di["Wf2"] = nc.dram_tensor("Wf2", [HD, D], F32, kind="ExternalInput")
        di["Wog1"] = nc.dram_tensor("Wog1", [D, HD], F32, kind="ExternalInput")
        di["Wog2"] = nc.dram_tensor("Wog2", [HD, D], F32, kind="ExternalInput")
        di["norm_w"] = nc.dram_tensor("norm_w", [D], F32, kind="ExternalInput")
        di["WrT1"] = nc.dram_tensor("WrT1", [D, D], F32, kind="ExternalInput")
        di["WkT1"] = nc.dram_tensor("WkT1", [D, D], F32, kind="ExternalInput")
        di["WvT1"] = nc.dram_tensor("WvT1", [D, D], F32, kind="ExternalInput")
        di["mu_r"] = nc.dram_tensor("mu_r", [128, NC], F32, kind="ExternalInput")
        di["mu_k"] = nc.dram_tensor("mu_k", [128, NC], F32, kind="ExternalInput")
        self.di = {k: ap(v) for k, v in di.items()}

        self.y = ap(nc.dram_tensor("y", [D, N], F32, kind="ExternalOutput"))

        self.x1_dram = ap(nc.dram_tensor("x1_scratch", [D, N], F32, kind="Internal"))
        self.logits_dram = ap(nc.dram_tensor("logits_scratch", [8, N], F32,
                                             kind="Internal"))
        self.o_dram = ap(nc.dram_tensor("o_scratch", [8, 128, N], BF16, kind="Internal"))

        A = nc.alloc_sbuf_tensor
        self.xn = ap(A("xn", [128, NC, N], BF16))
        self.big = ap(A("big", [128, NC, N], BF16))
        self.consts = ap(A("consts", [128, 128 * 3], BF16))
        self.ut64 = ap(A("ut64", [128, 64], BF16))
        self.sel = ap(A("sel", [8, 8 * 128], BF16))
        self.identf = ap(A("identf", [128, 128], F32))
        self.i64b = ap(A("i64b", [128, 1024], BF16))
        self.masks = ap(A("masks", [128, 512 * 3], BF16))
        self.pool_bytes = 71 * 1024
        self.pool = ap(A("pool", [128, self.pool_bytes], mybir.dt.uint8))
        self.pool_off = 0
        self.pool_marks = []

        self.pb_s = ap(nc.alloc_psum_tensor("pbs", [128, 2048], F32))
        self.pb4 = [ap(nc.alloc_psum_tensor(f"pb{i}", [128, 512], F32))
                    for i in range(4)]
        self.sconst = ap(A("sconst", [128, 8], F32))
        self.pb = _BankView(self)

        self.ctx = Ctx(nc)
        for s in ("ld", "ldx0", "ldx1", "mm", "ev", "sc", "gp", "st",
                  "aux", "gld", "rld", "bld", "wrf", "tbl", "lgl", "lgs", "da1s",
                  "wl0", "wl1", "wl2", "w2l0", "w2l1", "w2l2",
                  "wpl0", "wpl1", "rt"):
            self.ctx.new_sem(s)
        self.dbg_outs = {}
        self.bank_guard = [("ev", 0)] * 8
        self.w1_guard = [("mm", 0), ("mm", 0), ("mm", 0)]
        self.w2_guard = [("mm", 0), ("mm", 0), ("mm", 0)]
        self.wpj_guard = [("mm", 0), ("mm", 0), ("mm", 0)]

    def palloc(self, nbytes, dtype, free_elems):
        nbytes_al = (nbytes + 31) & ~31
        assert self.pool_off + nbytes_al <= self.pool_bytes, \
            f"pool overflow: {self.pool_off}+{nbytes_al}"
        apx = self.pool[:, self.pool_off:self.pool_off + nbytes_al].bitcast(dtype)
        self.pool_off += nbytes_al
        return apx[:, :free_elems] if free_elems is not None else apx

    def pmark(self, fence=True):
        if fence:
            self.ctx.fence()
        self.pool_marks.append(self.pool_off)

    def prelease(self):
        self.pool_off = self.pool_marks.pop()

    def dbg_out(self, name, shape, dtype=F32):
        if name not in self.dbg_outs:
            self.dbg_outs[name] = self.nc.dram_tensor(
                f"dbg_{name}", list(shape), dtype, kind="ExternalOutput").ap()
        return self.dbg_outs[name]

    @property
    def eps_col(self):
        return self.sconst[:, 0:1]

    @property
    def one_col(self):
        return self.sconst[:, 1:2]

    def load_consts(self):
        cx, c = self.ctx, self.c
        cx.op("vector", lambda e: e.memset(self.sconst[:, 0:1], 1e-5))
        cx.op("vector", lambda e: e.memset(self.sconst[:, 1:2], 1.0))
        self.sconst_ready = ("ev", cx.val("ev"))
        cx.dma("sync", self.consts[:, 0:128], c["ident"].ap(), inc="ld")
        cx.dma("sync", self.consts[:, 128:256], c["ones"].ap(), inc="ld")
        cx.dma("sync", self.consts[:, 256:384], c["swap"].ap(), inc="ld")
        cx.dma("sync", self.ut64, c["ut64"].ap(), inc="ld")
        cx.dma("sync", self.masks[:, 0:512], c["mask_sl"].ap(), inc="ld")
        cx.dma("sync", self.masks[:, 512:1024], c["mask_iu"].ap(), inc="ld")
        cx.dma("sync", self.masks[:, 1024:1536], c["mask_su"].ap(), inc="ld")
        cx.dma("sync", self.sel, c["sel"].ap(), inc="ld")
        cx.dma("sync", self.identf, c["ident_f32"].ap(), inc="ld")
        cx.dma("sync", self.i64b, c["i64b"].ap(), inc="ld")
        self.consts_ready = cx.val("ld")

    @property
    def ident(self):
        return self.consts[:, 0:128]

    @property
    def ones(self):
        return self.consts[:, 128:256]

    @property
    def swap(self):
        return self.consts[:, 256:384]

    def emit_ln(self, dst, tw, x_stage, stage_wait, ib4,
                scratch_bf, scratch_f32, rows_f32, rows_bf, extra_mm_wait=None):
        cx = self.ctx
        ib_sum, ib_sq, ib_mb, ib_rb = ib4
        pb_sum, pb_sq = self.pb[ib_sum], self.pb[ib_sq]
        pb_mb, pb_rb = self.pb[ib_mb], self.pb[ib_rb]
        w_cast = [stage_wait, ("mm", cx.val("mm"))]
        for c in range(NC):
            cx.op("scalar", lambda e, c=c: e.copy(
                out=scratch_bf[:, c * tw:(c + 1) * tw], in_=x_stage[:, c, :]),
                waits=w_cast if c == 0 else ())
        cast_done = cx.val("sc")
        for c in range(NC):
            cx.op("scalar", lambda e, c=c: e.activation(
                out=scratch_bf[:, (NC + c) * tw:(NC + c + 1) * tw],
                in_=x_stage[:, c, :], func=AF.Square))
        sq_done = cx.val("sc")
        ones_col = self.ones[:, 0:1]
        w0 = [("sc", cast_done), self.bank_guard[ib_sum]]
        if extra_mm_wait is not None:
            w0.append(extra_mm_wait)
        for c in range(NC):
            cx.op("tensor", lambda e, c=c: e.matmul(
                pb_sum[0:1, 0:tw], ones_col, scratch_bf[:, c * tw:(c + 1) * tw],
                start=(c == 0), stop=(c == NC - 1)),
                waits=w0 if c == 0 else ())
        for c in range(NC):
            cx.op("tensor", lambda e, c=c: e.matmul(
                pb_sq[0:1, 0:tw], ones_col,
                scratch_bf[:, (NC + c) * tw:(NC + c + 1) * tw],
                start=(c == 0), stop=(c == NC - 1)),
                waits=[("sc", sq_done), self.bank_guard[ib_sq]] if c == 0 else (),
                inc="mm" if c == NC - 1 else None)
        mmv = cx.val("mm")
        cx.op("scalar", lambda e: e.mul(out=rows_f32[0:1, 0:tw],
                                        in_=pb_sum[0:1, 0:tw], mul=1.0 / D),
              waits=[("mm", mmv)])
        cx.op("scalar", lambda e: e.activation(out=rows_f32[0:1, tw:2 * tw],
                                               in_=rows_f32[0:1, 0:tw],
                                               func=AF.Square))
        scv = cx.val("sc")
        cx.op("vector", lambda e: e.scalar_tensor_tensor(
            out=rows_f32[0:1, 2 * tw:3 * tw], in0=pb_sq[0:1, 0:tw],
            scalar=1.0 / D, in1=rows_f32[0:1, tw:2 * tw],
            op0=ALU.mult, op1=ALU.subtract), waits=[("sc", scv)])
        self.bank_guard[ib_sum] = ("sc", cx.val("sc"))
        self.bank_guard[ib_sq] = ("ev", cx.val("ev"))
        cx.op("scalar", lambda e: e.activation(
            out=rows_f32[0:1, 2 * tw:3 * tw], in_=rows_f32[0:1, 2 * tw:3 * tw],
            func=AF.Ln, bias=self.eps_col[0:1, :]),
            waits=[("ev", cx.val("ev"))])
        cx.op("scalar", lambda e: e.activation(
            out=rows_f32[0:1, 2 * tw:3 * tw], in_=rows_f32[0:1, 2 * tw:3 * tw],
            func=AF.Exp, scale=-0.5))
        cx.op("scalar", lambda e: e.copy(
            out=rows_bf[0:1, 0:tw], in_=rows_f32[0:1, 0:tw]))
        cx.op("scalar", lambda e: e.copy(
            out=rows_bf[0:1, tw:2 * tw], in_=rows_f32[0:1, 2 * tw:3 * tw]))
        scv2 = cx.val("sc")
        ones_row = self.ones[0:1, :]
        cx.op("tensor", lambda e: e.matmul(
            pb_mb[:, 0:tw], ones_row, rows_bf[0:1, 0:tw],
            start=True, stop=True),
            waits=[("sc", scv2), self.bank_guard[ib_mb]])
        cx.op("tensor", lambda e: e.matmul(
            pb_rb[:, 0:tw], ones_row, rows_bf[0:1, tw:2 * tw],
            start=True, stop=True),
            waits=[self.bank_guard[ib_rb]], inc="mm")
        mmv2 = cx.val("mm")
        for c in range(NC):
            cx.op("vector", lambda e, c=c: e.tensor_sub(
                out=scratch_f32, in0=x_stage[:, c, :], in1=pb_mb[:, 0:tw]),
                waits=[("mm", mmv2)] if c == 0 else ())
            cx.op("vector", lambda e, c=c: e.tensor_mul(
                out=dst[:, c, :], in0=scratch_f32, in1=pb_rb[:, 0:tw]))
        self.bank_guard[ib_mb] = ("ev", cx.val("ev"))
        self.bank_guard[ib_rb] = ("ev", cx.val("ev"))


    def emit_l0_attention(self):
        cx = self.ctx
        p = self
        TW = 512
        p.pmark()
        tabs = p.palloc(2 * N * 2, BF16, 2 * N)
        qb = p.palloc(N * 2, BF16, N)
        kb = p.palloc(N * 2, BF16, N)
        vaug = p.palloc(32 * 130 * 2, BF16, 32 * 130).rearrange(
            "p (k c) -> p k c", k=32)
        pt = p.palloc(2048 * 2, BF16, 2048)
        rsc = p.palloc(TW * 4, F32, TW)
        rsc2 = p.palloc(TW * 4, F32, TW)
        otmp = p.palloc(128 * 2, BF16, 128)
        rcol = p.palloc(2 * 4, F32, 2)
        wqkv2 = [tuple(p.palloc(NC * 128 * 2, BF16, NC * 128).rearrange(
            "p (c o) -> p c o", c=NC) for _ in range(3)) for _ in range(2)]
        wcs0 = p.palloc(2048 * 2, BF16, 2048)
        wcs1 = p.palloc(2048 * 2, BF16, 2048)
        p.init_weight_cache(wcs0, wcs1)

        for i, name in enumerate(("rope_c1q", "rope_c2q")):
            cx.dma("sync", tabs[:, i * N:(i + 1) * N], p.di[name], inc="tbl")
        tabs_ready = cx.val("tbl")
        t_c1q = tabs[:, 0 * N:1 * N]
        t_c2q = tabs[:, 1 * N:2 * N]

        xn_ready = ("ev", cx.val("ev"))

        def load_pair_weights(hp, mmwait):
            wls = f"wl{hp % 2}"
            srcw = p.di["WqkvT"]
            wq_, wk_, wv_ = wqkv2[hp % 2]
            for wdst, base in ((wq_, hp * 128), (wk_, D + hp * 128),
                               (wv_, 2 * D + hp * 128)):
                cx.dma("gpsimd", wdst,
                       srcw[:, base:base + 128].rearrange(
                           "(c p) o -> p c o", p=128),
                       waits=[("mm", mmwait)], inc=wls)
            return (wls, cx.val(wls))

        next_wl = load_pair_weights(0, cx.val("mm"))
        for hp in range(8):
            wq, wk, wv = wqkv2[hp % 2]
            wl_ready = next_wl
            p.emit_cache_chunk(64)

            if hp == 0:
                cx.op("vector", lambda e: e.memset(vaug, 1.0))
            vaug_clear = cx.val("ev")
            for kc in range(32):
                bk = (2, 3, 6, 7)[kc % 4]
                cx.op("tensor", lambda e, kc=kc, bk=bk, wv=wv: e.matmul(
                    p.pb[bk][:, 0:128], p.xn[:, 0, kc * 128:(kc + 1) * 128],
                    wv[:, 0, :], start=True, stop=False),
                    waits=[p.bank_guard[bk]] + ([wl_ready, xn_ready]
                                                if kc == 0 else []))
                for c in range(1, NC):
                    cx.op("tensor", lambda e, kc=kc, c=c, bk=bk, wv=wv:
                          e.matmul(
                              p.pb[bk][:, 0:128],
                              p.xn[:, c, kc * 128:(kc + 1) * 128],
                              wv[:, c, :], start=False, stop=(c == NC - 1)),
                          inc="mm" if c == NC - 1 else None)
                mmv = cx.val("mm")
                cx.op("vector", lambda e, kc=kc, bk=bk: e.tensor_copy(
                    out=vaug[:, kc, 0:64], in_=p.pb[bk][:, 0:64]),
                    waits=[("mm", mmv), ("ev", vaug_clear)])
                cx.op("vector", lambda e, kc=kc, bk=bk: e.tensor_copy(
                    out=vaug[:, kc, 65:129], in_=p.pb[bk][:, 64:128]))
                p.bank_guard[bk] = ("ev", cx.val("ev"))
            for dst_sb, wsb, bank in ((qb, wq, 0), (kb, wk, 1)):
                for ti in range(N // TW):
                    t0 = ti * TW
                    for c in range(NC):
                        cx.op("tensor", lambda e, c=c, t0=t0, wsb=wsb, bank=bank:
                              e.matmul(p.pb[bank][:, 0:TW], wsb[:, c, :],
                                       p.xn[:, c, t0:t0 + TW],
                                       start=(c == 0), stop=(c == NC - 1)),
                              waits=([wl_ready, xn_ready, p.bank_guard[bank]]
                                     if c == 0 else ()),
                              inc="mm" if c == NC - 1 else None)
                    mmv = cx.val("mm")
                    cx.op("vector", lambda e, dst_sb=dst_sb, t0=t0, bank=bank:
                          e.tensor_copy(out=dst_sb[:, t0:t0 + TW],
                                        in_=p.pb[bank][:, 0:TW]),
                          waits=[("mm", mmv)])
                    p.bank_guard[bank] = ("ev", cx.val("ev"))

            if hp < 7:
                next_wl = load_pair_weights(hp + 1, cx.val("mm"))

            for buf, gain in ((qb, 1.0), (kb, 8.0)):
                for ti in range(N // TW):
                    t0 = ti * TW
                    evb = cx.val("ev")
                    cx.op("tensor", lambda e, buf=buf, t0=t0: e.matmul(
                        p.pb[3][:, 0:TW], p.swap, buf[:, t0:t0 + TW],
                        start=True, stop=True),
                        waits=[("ev", evb), ("tbl", tabs_ready),
                               p.bank_guard[3]], inc="mm")
                    mmv = cx.val("mm")
                    cx.op("vector", lambda e, t0=t0, gain=gain:
                          e.scalar_tensor_tensor(
                              out=rsc, in0=p.pb[3][:, 0:TW], scalar=gain,
                              in1=t_c2q[:, t0:t0 + TW],
                              op0=ALU.mult, op1=ALU.mult),
                          waits=[("mm", mmv)])
                    p.bank_guard[3] = ("ev", cx.val("ev"))
                    cx.op("vector", lambda e, buf=buf, t0=t0, gain=gain:
                          e.scalar_tensor_tensor(
                              out=rsc2, in0=buf[:, t0:t0 + TW], scalar=gain,
                              in1=t_c1q[:, t0:t0 + TW],
                              op0=ALU.mult, op1=ALU.mult))
                    cx.op("vector", lambda e, buf=buf, t0=t0: e.tensor_add(
                        out=buf[:, t0:t0 + TW], in0=rsc, in1=rsc2))
            rope_done = ("ev", cx.val("ev"))

            for qc in range(32):
                q0 = qc * 128
                exp_done = [0] * 8
                sdone = [0] * 8
                avdone = [0] * 8

                def emit_sbatch(b, qc=qc, q0=q0):
                    bb = 1024 * (b % 2)
                    w_ = [rope_done, p.bank_guard[0 + 2 * (b % 2)]]
                    for j in range(4):
                        kc = b * 4 + j
                        k0 = kc * 128
                        for hb in (0, 64):
                            scol = bb + (hb // 64) * 512 + j * 128
                            cx.op("tensor", lambda e, hb=hb, scol=scol,
                                  k0=k0, q0=q0: e.matmul(
                                      p.pb_s[:, scol:scol + 128],
                                      kb[hb:hb + 64, k0:k0 + 128],
                                      qb[hb:hb + 64, q0:q0 + 128],
                                      start=True, stop=True),
                                  waits=w_ if (j == 0 and hb == 0) else (),
                                  inc="mm" if (j == 3 and hb == 64)
                                  else None)
                    sdone[b] = cx.val("mm")

                def emit_exp(b):
                    bb = 1024 * (b % 2)
                    cx.op("scalar", lambda e, b=b, bb=bb: e.activation(
                        out=pt[:, (b % 2) * 1024:(b % 2) * 1024 + 1024],
                        in_=p.pb_s[:, bb:bb + 1024], func=AF.Exp),
                        waits=[("mm", sdone[b])])
                    exp_done[b] = cx.val("sc")
                    p.bank_guard[0 + 2 * (b % 2)] = ("sc", exp_done[b])
                    p.bank_guard[1 + 2 * (b % 2)] = ("sc", exp_done[b])

                def emit_av(b, qc=qc):
                    w_ = [("sc", exp_done[b])]
                    if b == 0:
                        w_ += [p.bank_guard[4], p.bank_guard[5]]
                    for j in range(4):
                        kc = b * 4 + j
                        for hslot, bank in ((0, 4), (1, 5)):
                            pcol = (b % 2) * 1024 + hslot * 512 + j * 128
                            vcol = hslot * 65
                            cx.op("tensor", lambda e, pcol=pcol, vcol=vcol,
                                  kc=kc, bank=bank, st=(b == 0 and j == 0),
                                  sp=(b == 7 and j == 3): e.matmul(
                                      p.pb[bank][:, 0:65],
                                      pt[:, pcol:pcol + 128],
                                      vaug[:, kc, vcol:vcol + 65],
                                      start=st, stop=sp),
                                  waits=w_ if (j == 0 and hslot == 0) else (),
                                  inc="mm" if (j == 3 and hslot == 1)
                                  else None)
                    avdone[b] = cx.val("mm")

                for b in range(8):
                    emit_sbatch(b)
                    if b >= 1:
                        emit_av(b - 1)
                    emit_exp(b)
                emit_av(7)
                av_done = cx.val("mm")
                cx.op("vector", lambda e: e.reciprocal(
                    out=rcol[:, 0:1], in_=p.pb[4][:, 64:65]),
                    waits=[("mm", av_done)])
                cx.op("vector", lambda e: e.reciprocal(
                    out=rcol[:, 1:2], in_=p.pb[5][:, 64:65]))
                cx.op("vector", lambda e: e.tensor_scalar(
                    out=otmp[:, 0:64], in0=p.pb[4][:, 0:64],
                    scalar1=rcol[:, 0:1], scalar2=None, op0=ALU.mult))
                cx.op("vector", lambda e: e.tensor_scalar(
                    out=otmp[:, 64:128], in0=p.pb[5][:, 0:64],
                    scalar1=rcol[:, 1:2], scalar2=None, op0=ALU.mult))
                p.bank_guard[4] = ("ev", cx.val("ev"))
                p.bank_guard[5] = ("ev", cx.val("ev"))
                evv = cx.val("ev")
                cx.op("tensor", lambda e: e.transpose(
                    p.pb[6].bitcast(BF16)[:, 0:128], otmp, p.ident),
                    waits=[("ev", evv), p.bank_guard[6]], inc="mm")
                mmv = cx.val("mm")
                cx.op("vector", lambda e, hp=hp, q0=q0: e.tensor_copy(
                    out=p.big[:, hp, q0:q0 + 128],
                    in_=p.pb[6].bitcast(BF16)[:, 0:128]),
                    waits=[("mm", mmv)])
                p.bank_guard[6] = ("ev", cx.val("ev"))
        p.prelease()

    def init_weight_cache(self, stage0, stage1):
        nc, cx = self.nc, self.ctx
        self.w1_bf = nc.dram_tensor("w1_bf", [8, 8, 128, 4096], BF16,
                                    kind="Internal").ap()
        self.w2_bf = nc.dram_tensor("w2_bf", [8, 8, 128, 2048], BF16,
                                    kind="Internal").ap()
        self.wc_stage = (stage0, stage1)
        for s in ("wci0", "wci1", "wco0", "wco1"):
            self.ctx.new_sem(s)
        jobs = []
        for e in range(8):
            for dc in range(8):
                for ag, wsrc in ((0, "W1a"), (1, "W1g")):
                    for kh in range(2):
                        s_ = self.di[wsrc][e, dc * 128:(dc + 1) * 128, :] \
                            .rearrange("p (g kh o) -> p g kh o",
                                       g=8, kh=2)[:, :, kh, :]
                        d_ = self.w1_bf[e].rearrange(
                            "g p (kh q) -> p g kh q", kh=2)[
                                :, :, kh, ag * 1024 + dc * 128:
                                ag * 1024 + dc * 128 + 128]
                        jobs.append((s_, d_, 1024, 8))
            for kc in range(16):
                jobs.append((self.di["W2"][e, kc * 128:(kc + 1) * 128, :],
                             self.w2_bf[e, kc // 2, :,
                                        (kc % 2) * 1024:(kc % 2) * 1024
                                        + 1024],
                             1024, None))
        self.wc_jobs = jobs
        self.wc_next = 0

    def emit_cache_chunk(self, njobs):
        cx = self.ctx
        end = min(self.wc_next + njobs, len(self.wc_jobs))
        for i in range(self.wc_next, end):
            src, dst, n, v4 = self.wc_jobs[i]
            par = i % 2
            stage = self.wc_stage[par]
            sview = stage[:, 0:n]
            if v4 is not None:
                sview = sview.rearrange("p (g o) -> p g o", g=v4)
            cx.dma("gpsimd", sview, src,
                   waits=[(f"wco{par}", cx.val(f"wco{par}"))],
                   inc=f"wci{par}")
            cx.dma("sync", dst, sview,
                   waits=[(f"wci{par}", cx.val(f"wci{par}"))],
                   inc=f"wco{par}")
        self.wc_next = end
        if end == len(self.wc_jobs):
            self.wcache_done = [("wco0", cx.val("wco0")),
                                ("wco1", cx.val("wco1"))]

    def emit_router(self):
        cx = self.ctx
        p = self
        wr = p.r_wr
        br_col = p.r_br
        logit = p.r_logit
        ltm = p.r_ltm
        scr = p.r_scr
        w_bf = p.w_bf

        cx.dma("gpsimd", br_col, p.di["br"].rearrange("(p o) -> p o", o=1),
               inc="rld")
        wl = ("rld", cx.val("rld"))
        cx.dma("sync", logit, p.logits_dram,
               waits=[("lgs", cx.val("lgs"))], inc="lgl")
        llv = cx.val("lgl")
        cx.op("vector", lambda e: e.tensor_scalar(
            out=logit, in0=logit, scalar1=br_col, scalar2=None, op0=ALU.add),
            waits=[wl, ("lgl", llv)])
        for blk4 in range(8):
            evv = cx.val("ev")
            for j in range(4):
                blk = blk4 * 4 + j
                cx.op("tensor", lambda e, blk=blk, j=j: e.transpose(
                    p.pb[5][:, j * 8:j * 8 + 8],
                    logit[:, blk * 128:(blk + 1) * 128], p.identf[0:8, 0:8]),
                    waits=([("ev", evv), p.bank_guard[5]] if j == 0 else ()),
                    inc="mm" if j == 3 else None)
            mmv = cx.val("mm")
            cx.op("vector", lambda e, blk4=blk4: e.tensor_copy(
                out=ltm[:, blk4 * 4:blk4 * 4 + 4, :].rearrange("p a b -> p (a b)"),
                in_=p.pb[5][:, 0:32]), waits=[("mm", mmv)])
            p.bank_guard[5] = ("ev", cx.val("ev"))
        cx.op("vector", lambda e: e.reduce_max(
            out=scr[:, 0:32], in_=ltm, axis=mybir.AxisListType.X))
        for blk in range(32):
            cx.op("vector", lambda e, blk=blk: e.tensor_scalar(
                out=scr[:, 64 + blk * 8:64 + blk * 8 + 8], in0=ltm[:, blk, :],
                scalar1=scr[:, blk:blk + 1], scalar2=None, op0=ALU.is_equal))
        for blk in range(32):
            cx.op("vector", lambda e, blk=blk: e.scalar_tensor_tensor(
                out=ltm[:, blk, :], in0=scr[:, 64 + blk * 8:64 + blk * 8 + 8],
                scalar=-1e30, in1=ltm[:, blk, :], op0=ALU.mult, op1=ALU.add))
        cx.op("vector", lambda e: e.reduce_max(
            out=scr[:, 32:64], in_=ltm, axis=mybir.AxisListType.X))
        for blk in range(32):
            cx.op("vector", lambda e, blk=blk: e.tensor_scalar(
                out=scr[:, 320 + blk * 8:320 + blk * 8 + 8], in0=ltm[:, blk, :],
                scalar1=scr[:, 32 + blk:32 + blk + 1], scalar2=None,
                op0=ALU.is_equal))
        cx.op("vector", lambda e: e.tensor_sub(
            out=scr[:, 576:608], in0=scr[:, 0:32], in1=scr[:, 32:64]))
        scv = cx.val("ev")
        cx.op("scalar", lambda e: e.activation(
            out=scr[:, 576:608], in_=scr[:, 576:608], func=AF.Sigmoid),
            waits=[("ev", scv)])
        scs = cx.val("sc")
        cx.op("vector", lambda e: e.tensor_scalar(
            out=scr[:, 608:640], in0=scr[:, 576:608], scalar1=-1.0, scalar2=1.0,
            op0=ALU.mult, op1=ALU.add), waits=[("sc", scs)])
        for blk in range(32):
            cx.op("vector", lambda e, blk=blk: e.tensor_scalar(
                out=ltm[:, blk, :], in0=scr[:, 64 + blk * 8:64 + blk * 8 + 8],
                scalar1=scr[:, 576 + blk:576 + blk + 1], scalar2=None,
                op0=ALU.mult))
            cx.op("vector", lambda e, blk=blk: e.scalar_tensor_tensor(
                out=ltm[:, blk, :], in0=scr[:, 320 + blk * 8:320 + blk * 8 + 8],
                scalar=scr[:, 608 + blk:608 + blk + 1], in1=ltm[:, blk, :],
                op0=ALU.mult, op1=ALU.add))
        for blk4 in range(8):
            evv = cx.val("ev")
            for j in range(4):
                blk = blk4 * 4 + j
                cx.op("tensor", lambda e, blk=blk, j=j: e.transpose(
                    p.pb[6][0:8, j * 128:(j + 1) * 128],
                    ltm[:, blk, :], p.identf),
                    waits=([("ev", evv), p.bank_guard[6]] if j == 0 else ()),
                    inc="mm" if j == 3 else None)
            mmv = cx.val("mm")
            cx.op("vector", lambda e, blk4=blk4: e.tensor_copy(
                out=w_bf[0:8, blk4 * 512:(blk4 + 1) * 512],
                in_=p.pb[6][0:8, 0:512]), waits=[("mm", mmv)])
            p.bank_guard[6] = ("ev", cx.val("ev"))

    def emit_l0_moe_residual(self):
        cx = self.ctx
        p = self
        TW = 512
        nt = N // TW
        act = p.m_act
        df = p.m_df
        xs = p.m_xs
        wbc = p.m_wbc
        sab = p.m_sab
        b1a = p.m_b1a
        b1g = p.m_b1g
        b2s = p.m_b2
        lnbf = p.m_lnbf
        lnf = p.m_lnf
        lrows = p.m_lrows
        lrbf = p.m_lrbf

        cx.dma("gpsimd", b1a, p.di["b1a"], inc="bld")
        cx.dma("gpsimd", b1g, p.di["b1g"], inc="bld")
        cx.dma("gpsimd", b2s, p.di["b2"], inc="bld")
        bias_ready = ("bld", cx.val("bld"))

        for ti in range(nt):
            t0 = ti * TW
            wfb = ("ev", cx.val("ev"))
            for c in range(NC):
                cx.op("tensor", lambda e, c=c, t0=t0: e.matmul(
                    p.pb[4][:, 0:TW], b2s[0:8, c * 128:(c + 1) * 128],
                    p.w_bf[0:8, t0:t0 + TW], start=True, stop=True),
                    waits=[wfb, bias_ready, p.bank_guard[4]],
                    inc="mm")
                mmv = cx.val("mm")
                cx.op("vector", lambda e, c=c: e.tensor_copy(
                    out=df[:, c, :], in_=p.pb[4][:, 0:TW]),
                    waits=[("mm", mmv)])
                p.bank_guard[4] = ("ev", cx.val("ev"))

            for ex in range(8):
                cx.op("tensor", lambda e, ex=ex, t0=t0: e.matmul(
                    p.pb[5][:, 0:TW], p.sel[0:8, ex * 128:(ex + 1) * 128],
                    p.w_bf[0:8, t0:t0 + TW], start=True, stop=True),
                    waits=[p.bank_guard[5]], inc="mm")
                mmv = cx.val("mm")
                cx.op("vector", lambda e: e.tensor_copy(
                    out=wbc, in_=p.pb[5][:, 0:TW]), waits=[("mm", mmv)])
                p.bank_guard[5] = ("ev", cx.val("ev"))
                wbc_ready = cx.val("ev")

                act_ev = [None] * 16
                for grp in range(8):
                    par = grp % 3
                    wsem = f"wl{par}"
                    ring = p.m_ring[par]
                    cx.dma("gpsimd", ring,
                           p.w1_bf[ex, grp],
                           waits=[(wsem, cx.val(wsem)), p.w1_guard[par],
                                  p.w2_guard[par], p.wpj_guard[par]] + (
                               p.wcache_done if ti == 0 and ex == 0 and
                               grp < 2 else []),
                           inc=wsem)
                    wlv = (wsem, cx.val(wsem))
                    for kh in range(2):
                        kc = grp * 2 + kh
                        ba, bg = (0, 1) if kc % 2 == 0 else (2, 3)
                        for c in range(NC):
                            cx.op("tensor", lambda e, c=c, t0=t0, ba=ba,
                                  ring=ring, kh=kh: e.matmul(
                                      p.pb[ba][:, 0:TW],
                                      ring[:, kh * 2048 + c * 128:
                                           kh * 2048 + c * 128 + 128],
                                      p.xn[:, c, t0:t0 + TW],
                                      start=(c == 0), stop=(c == NC - 1)),
                                  waits=([wlv, p.bank_guard[ba]]
                                         if c == 0 else ()))
                        for c in range(NC):
                            cx.op("tensor", lambda e, c=c, t0=t0, bg=bg,
                                  ring=ring, kh=kh: e.matmul(
                                      p.pb[bg][:, 0:TW],
                                      ring[:, kh * 2048 + 1024 + c * 128:
                                           kh * 2048 + 1024 + c * 128 + 128],
                                      p.xn[:, c, t0:t0 + TW],
                                      start=(c == 0), stop=(c == NC - 1)),
                                  waits=([p.bank_guard[bg]] if c == 0 else ()),
                                  inc="mm" if c == NC - 1 else None)
                        mmv = cx.val("mm")
                        if kh == 1:
                            p.w1_guard[par] = ("mm", mmv)
                        cx.op("scalar", lambda e, ex=ex, kc=kc, ba=ba:
                              e.activation(
                                  out=sab, in_=p.pb[ba][:, 0:TW],
                                  func=AF.Silu,
                                  bias=b1a[:, ex * 16 + kc:
                                           ex * 16 + kc + 1]),
                              waits=[("mm", mmv), ("ev", cx.val("ev"))])
                        scv = cx.val("sc")
                        cx.op("vector", lambda e, ex=ex, kc=kc, bg=bg:
                              e.scalar_tensor_tensor(
                                  out=act[:, kc, :], in0=p.pb[bg][:, 0:TW],
                                  scalar=b1g[:, ex * 16 + kc:
                                             ex * 16 + kc + 1],
                                  in1=sab, op0=ALU.add, op1=ALU.mult),
                              waits=[("sc", scv)])
                        cx.op("vector", lambda e, kc=kc: e.tensor_mul(
                            out=act[:, kc, :], in0=act[:, kc, :], in1=wbc))
                        p.bank_guard[ba] = ("sc", scv)
                        p.bank_guard[bg] = ("ev", cx.val("ev"))
                        act_ev[kc] = ("ev", cx.val("ev"))
                act_done = ("ev", cx.val("ev"))

                for grp in range(8):
                    par = grp % 3
                    wsem = f"w2l{par}"
                    ring = p.m_ring[par]
                    cx.dma("gpsimd", ring[:, 0:2048],
                           p.w2_bf[ex, grp],
                           waits=[(wsem, cx.val(wsem)), p.w2_guard[par],
                                  p.w1_guard[par]],
                           inc=wsem)
                    wlv = (wsem, cx.val(wsem))
                    for kh in range(2):
                        kc = grp * 2 + kh
                        for oc in range(8):
                            w_ = [wlv, act_ev[kc]]
                            if kc == 0:
                                w_.append(p.bank_guard[oc])
                            cx.op("tensor", lambda e, kc=kc, oc=oc, kh=kh,
                                  ring=ring: e.matmul(
                                      p.pb[oc][:, 0:TW],
                                      ring[:, kh * 1024 + oc * 128:
                                           kh * 1024 + oc * 128 + 128],
                                      act[:, kc, :],
                                      start=(kc == 0), stop=(kc == 15)),
                                  waits=w_ if (oc == 0 or kc == 0) else (),
                                  inc="mm" if oc == 7 else None)
                        p.w2_guard[par] = ("mm", cx.val("mm"))
                mmv = cx.val("mm")
                for oc in range(8):
                    cx.op("vector", lambda e, oc=oc: e.tensor_add(
                        out=df[:, oc, :], in0=df[:, oc, :],
                        in1=p.pb[oc][:, 0:TW]),
                        waits=[("mm", mmv)] if oc == 0 else ())
                    p.bank_guard[oc] = ("ev", cx.val("ev"))

            for half in range(TW // 256):
                s0 = t0 + half * 256
                for og in range(2):
                    par = og % 2
                    wsem = f"wpl{par}"
                    ring = p.m_ring[par]
                    if half == 0:
                        cx.dma("gpsimd",
                               ring.rearrange("p (c o) -> p c o", c=NC),
                               p.di["WprojT"][:, og * 512:(og + 1) * 512]
                               .rearrange("(c p) o -> p c o", p=128),
                               waits=[(wsem, cx.val(wsem)),
                                      p.wpj_guard[par], p.w2_guard[par],
                                      p.w1_guard[par]],
                               inc=wsem)
                    wlv = (wsem, cx.val(wsem))
                    for oq in range(4):
                        oc = og * 4 + oq
                        bank = oc // 2
                        col = (oc % 2) * 256
                        for ec in range(NC):
                            w_ = [wlv]
                            if ec == 0:
                                w_.append(p.bank_guard[bank])
                            cx.op("tensor",
                                  lambda e, ec=ec, ring=ring, oq=oq,
                                  bank=bank, col=col, s0=s0: e.matmul(
                                      p.pb[bank][:, col:col + 256],
                                      ring[:, ec * 512 + oq * 128:
                                           ec * 512 + oq * 128 + 128],
                                      p.big[:, ec, s0:s0 + 256],
                                      start=(ec == 0), stop=(ec == NC - 1)),
                                  waits=w_,
                                  inc="mm" if ec == NC - 1 else None)
                    p.wpj_guard[par] = ("mm", cx.val("mm"))
                mmv_proj = cx.val("mm")

                lds = f"ldx{half % 2}"
                cx.dma("sync", xs,
                       p.di["x_fm"][:, s0:s0 + 256].rearrange(
                           "(c p) t -> p c t", p=128),
                       waits=[("ev", cx.val("ev")), ("sc", cx.val("sc")),
                              ("st", cx.val("st"))], inc=lds)
                ldv = (lds, cx.val(lds))
                for c in range(NC):
                    cx.op("vector", lambda e, c=c, half=half: e.tensor_add(
                        out=xs[:, c, :], in0=xs[:, c, :],
                        in1=df[:, c, half * 256:half * 256 + 256]),
                        waits=[ldv, ("mm", mmv_proj)] if c == 0 else ())
                    cx.op("vector", lambda e, c=c:
                          e.tensor_add(
                              out=xs[:, c, :], in0=xs[:, c, :],
                              in1=p.pb[c // 2][:, (c % 2) * 256:
                                               (c % 2) * 256 + 256]))
                x1_done = cx.val("ev")
                for bk in range(4):
                    p.bank_guard[bk] = ("ev", x1_done)
                cx.dma("sync", p.x1_dram[:, s0:s0 + 256].rearrange(
                    "(c p) t -> p c t", p=128), xs,
                    waits=[("ev", x1_done)], inc="st")
                p.emit_ln(p.xn[:, :, s0:s0 + 256], 256, xs,
                          ("ev", x1_done), (4, 5, 6, 7), lnbf, lnf,
                          lrows, lrbf)

    def emit_l1_attention(self, ncp=32, stages="ABCDEFG"):
        cx = self.ctx
        p = self
        nc = self.nc
        self.da1_dram = nc.dram_tensor("da1_scratch", [D, N], BF16,
                                       kind="Internal").ap()
        bigf = p.big.rearrange("p a b -> p (a b)")
        wq1 = bigf[:, 0:8192].rearrange("p (c o) -> p c o", c=NC)
        wk1 = bigf[:, 8192:16384].rearrange("p (c o) -> p c o", c=NC)
        wv1 = bigf[:, 16384:24576].rearrange("p (c o) -> p c o", c=NC)
        wo1 = bigf[:, 24576:32768].rearrange("p (c o) -> p c o", c=NC)

        p.pmark()
        al = p.palloc
        wgam = al(NC * 16 * 2, BF16, NC * 16).rearrange("p (c o) -> p c o", c=NC)
        wf1 = al(NC * 64 * 2, BF16, NC * 64).rearrange("p (c o) -> p c o", c=NC)
        wog1 = al(NC * 64 * 2, BF16, NC * 64).rearrange("p (c o) -> p c o", c=NC)
        wf2 = al(D * 2, BF16, D)
        wog2 = al(D * 2, BF16, D)
        nwb = al(D * 4, F32, D)
        nwrow = al(D * 2, BF16, D)[0:1, :]
        v_tm = al(D * 2, BF16, D)
        og_tm = al(D * 2, BF16, D)
        kh_tm = al(D * 2, BF16, D)
        a_fm = al(D * 2, BF16, D)
        ktfm = al(D * 2, BF16, D)
        qtfm = al(D * 2, BF16, D)
        Mqt = al(D * 2, BF16, D)
        srow = al(128 * 4, F32, 128)[0:1, :]
        s0 = al(D * 2, BF16, D)
        o_tm = al(D * 2, BF16, D)
        ysb = al(64 * 2, BF16, 64)
        yfm = al(128 * 2, BF16, 128)
        gcol = al(16 * 4, F32, 16)
        rnorm = al(16 * 4, F32, 16)
        lamc = al(32 * 4, F32, 32)
        totf = al(2048 * 4, F32, 2048)[0:1, :]
        trow = al(1024 * 2, BF16, 1024)[0:1, :]
        col1 = al(4, F32, 1)
        f32a = al(D * 4, F32, D)
        f32b = al(D * 4, F32, D)
        s1b = al(D * 2, BF16, D)
        s2b = al(D * 2, BF16, D)
        s3b = al(D * 2, BF16, D)
        s4b = al(D * 2, BF16, D)
        s5b = al(D * 2, BF16, D)
        s6b = al(D * 2, BF16, D)
        s7b = al(D * 2, BF16, D)
        s8b = al(D * 2, BF16, D)
        q_tm = qt_tm = ofm = s1b
        k_tm = kt_tm = dsb = s2b
        ghi = a_tm = obuf = s3b
        glo = bfs = s4b
        lam = Bm = rhs_sb = s5b
        ivl = Cm = u_sb = s6b
        rat = Rmat = sdec = s7b
        ktn = Rp = s8b
        Tt = Rp
        esig = o_tm
        lf32 = f32b

        hv = lambda h: (h % 2) * 64
        hc = lambda h: (h % 2) * 512 + (h // 2) * 64
        MSL = p.masks[:, 0:512]
        MIU = p.masks[:, 512:1024]
        MSU = p.masks[:, 1024:1536]
        PB = p.pb_s
        P01 = PB[:, 0:1024]
        P23 = PB[:, 1024:2048]

        for wdst, name in ((wgam, "Wgamma"), (wf1, "Wf1"), (wog1, "Wog1")):
            cx.dma("gpsimd", wdst,
                   p.di[name].rearrange("(c p) o -> p c o", p=128), inc="gld")
        cx.dma("gpsimd", wf2[0:64, :], p.di["Wf2"], inc="gld")
        cx.dma("gpsimd", wog2[0:64, :], p.di["Wog2"], inc="gld")
        for wdst, name in ((wq1, "Wq"), (wk1, "Wk"), (wv1, "Wv"), (wo1, "Wo")):
            cx.dma("gpsimd", wdst,
                   p.di[name].rearrange("(c p) o -> p c o", p=128), inc="gld")
        cx.dma("sync", f32a[0:1, :], p.di["norm_w"], inc="tbl")
        wl_all = ("gld", cx.val("gld"))
        nw_l = ("tbl", cx.val("tbl"))
        cx.op("scalar", lambda e: e.copy(out=nwrow, in_=f32a[0:1, :]),
              waits=[nw_l])
        scv = cx.val("sc")
        for half in range(2):
            cx.op("tensor", lambda e, half=half: e.matmul(
                p.pb[6][:, 0:512], p.ones[0:1, :],
                nwrow[0:1, half * 512:half * 512 + 512],
                start=True, stop=True),
                waits=[("sc", scv), p.bank_guard[6]], inc="mm")
            mmv = cx.val("mm")
            cx.op("vector", lambda e, half=half: e.tensor_copy(
                out=nwb[:, half * 512:half * 512 + 512],
                in_=p.pb[6][:, 0:512]), waits=[("mm", mmv)])
            p.bank_guard[6] = ("ev", cx.val("ev"))
        cx.op("vector", lambda e: e.memset(s0, 0.0))
        xn_ready = ("ev", cx.val("ev"))

        for cp in range(ncp):
            t0 = cp * 128

            def lowrank_path(wlow, w2low, kind):
                for c in range(NC):
                    cx.op("tensor", lambda e, c=c, t0=t0, wlow=wlow: e.matmul(
                        p.pb[2][:, 16:80], p.xn[:, c, t0:t0 + 128],
                        wlow[:, c, :],
                        start=(c == 0), stop=(c == NC - 1)),
                        waits=[p.bank_guard[2], wl_all, xn_ready]
                        if c == 0 else (),
                        inc="mm" if c == NC - 1 else None)
                mmv = cx.val("mm")
                cx.op("scalar", lambda e: e.copy(out=ysb,
                                                 in_=p.pb[2][:, 16:80]),
                      waits=[("mm", mmv)])
                scv = cx.val("sc")
                p.bank_guard[2] = ("sc", scv)
                cx.op("tensor", lambda e: e.transpose(
                    p.pb[2].bitcast(BF16)[0:64, 256:384], ysb, p.ident),
                    waits=[("sc", scv), p.bank_guard[2]], inc="mm")
                mmv = cx.val("mm")
                cx.op("vector", lambda e: e.tensor_copy(
                    out=yfm[0:64, :], in_=p.pb[2].bitcast(BF16)[0:64, 256:384]),
                    waits=[("mm", mmv)])
                yf_done = cx.val("ev")
                p.bank_guard[2] = ("ev", yf_done)
                for half in range(2):
                    cx.op("tensor", lambda e, half=half, w2low=w2low: e.matmul(
                        p.pb[3][:, 0:512], yfm[0:64, :],
                        w2low[0:64, half * 512:half * 512 + 512],
                        start=True, stop=True),
                        waits=[("ev", yf_done), p.bank_guard[3]], inc="mm")
                    mmv = cx.val("mm")
                    if kind == "z":
                        cx.op("scalar", lambda e, half=half: e.activation(
                            out=f32a[:, half * 512:half * 512 + 512],
                            in_=p.pb[3][:, 0:512], func=AF.Exp, scale=-1.0),
                            waits=[("mm", mmv), ("ev", cx.val("ev"))])
                    else:
                        cx.op("scalar", lambda e, half=half: e.copy(
                            out=og_tm[:, half * 512:half * 512 + 512],
                            in_=p.pb[3][:, 0:512]),
                            waits=[("mm", mmv), ("ev", cx.val("ev"))])
                    p.bank_guard[3] = ("sc", cx.val("sc"))
                if kind == "z":
                    cx.op("vector", lambda e: e.tensor_scalar(
                        out=f32a, in0=f32a, scalar1=1.0, scalar2=None,
                        op0=ALU.add), waits=[("sc", cx.val("sc"))])
                    evv = cx.val("ev")
                    cx.op("scalar", lambda e: e.activation(
                        out=lf32, in_=f32a, func=AF.Ln), waits=[("ev", evv)])
                    cx.op("scalar", lambda e: e.activation(
                        out=esig, in_=lf32, func=AF.Exp, scale=-1.0),
                        waits=[("ev", cx.val("ev"))])
                    cx.op("scalar", lambda e: e.copy(out=ghi, in_=lf32))
                    scv = cx.val("sc")
                    cx.op("vector", lambda e: e.tensor_sub(
                        out=glo, in0=lf32, in1=ghi), waits=[("sc", scv)])

            lowrank_path(wf1, wf2, "z")
            lhilo = ("ev", cx.val("ev"))

            for c in range(NC):
                cx.op("tensor", lambda e, c=c, t0=t0: e.matmul(
                    p.pb[2][:, 0:16], p.xn[:, c, t0:t0 + 128], wgam[:, c, :],
                    start=(c == 0), stop=(c == NC - 1)),
                    waits=[p.bank_guard[2]] if c == 0 else (),
                    inc="mm" if c == NC - 1 else None)
            mmv = cx.val("mm")
            cx.op("scalar", lambda e: e.activation(
                out=gcol, in_=p.pb[2][:, 0:16], func=AF.Sigmoid, scale=-1.0),
                waits=[("mm", mmv)])
            scv = cx.val("sc")
            p.bank_guard[2] = ("sc", scv)
            cx.op("vector", lambda e: e.tensor_scalar(
                out=gcol, in0=gcol, scalar1=-1.0, scalar2=None, op0=ALU.add),
                waits=[("sc", scv)])

            def qkv_unit(wsb, dst, func, half):
                def emit():
                    bk = half
                    for c in range(NC):
                        cx.op("tensor", lambda e, c=c, wsb=wsb, half=half,
                              t0=t0, bk=bk: e.matmul(
                                  p.pb[bk][:, 0:512],
                                  p.xn[:, c, t0:t0 + 128],
                                  wsb[:, c, half * 512:half * 512 + 512],
                                  start=(c == 0), stop=(c == NC - 1)),
                              waits=([wl_all, xn_ready,
                                      p.bank_guard[bk]] if c == 0 else ()),
                              inc="mm" if c == NC - 1 else None)
                    mmv = cx.val("mm")
                    cx.op("scalar", lambda e, dst=dst, half=half, func=func,
                          bk=bk: e.activation(
                              out=dst[:, half * 512:half * 512 + 512],
                              in_=p.pb[bk][:, 0:512], func=func),
                          waits=[("mm", mmv), ("da1s", cx.val("da1s"))])
                    p.bank_guard[bk] = ("sc", cx.val("sc"))
                    return ("sc", cx.val("sc"))
                return emit
            a_units = [qkv_unit(wk1, k_tm, AF.Silu, 0),
                       qkv_unit(wk1, k_tm, AF.Silu, 1),
                       qkv_unit(wq1, q_tm, AF.Silu, 0),
                       qkv_unit(wq1, q_tm, AF.Silu, 1),
                       qkv_unit(wv1, v_tm, AF.Identity, 0),
                       qkv_unit(wv1, v_tm, AF.Identity, 1)]

            if "B" not in stages:
                continue
            first = True
            for par in range(2):
                for half in range(2):
                    for si, srct in enumerate((ghi, glo)):
                        w_ = []
                        if first:
                            w_ = [lhilo, p.bank_guard[4], p.bank_guard[5]]
                            first = False
                        cx.op("tensor", lambda e, par=par, half=half,
                              srct=srct, si=si: e.matmul(
                                  p.pb[4 + half][par * 64:par * 64 + 64,
                                                 0:512],
                                  p.ut64[par * 64:par * 64 + 64, :],
                                  srct[par * 64:par * 64 + 64,
                                       half * 512:half * 512 + 512],
                                  start=(si == 0), stop=(si == 1),
                                  tile_position=(par * 64, par * 64)),
                              waits=w_,
                              inc="mm" if (par == 1 and half == 1 and
                                           si == 1) else None)
            mmv = cx.val("mm")
            for half in range(2):
                cx.op("scalar", lambda e, half=half: e.activation(
                    out=lam[:, half * 512:half * 512 + 512],
                    in_=p.pb[4 + half][:, 0:512], func=AF.Exp, scale=-1.0),
                    waits=[("mm", mmv)] if half == 0 else ())
                cx.op("scalar", lambda e, half=half: e.activation(
                    out=ivl[:, half * 512:half * 512 + 512],
                    in_=p.pb[4 + half][:, 0:512], func=AF.Exp))
            cum_done = cx.val("sc")
            p.bank_guard[4] = ("sc", cum_done)
            p.bank_guard[5] = ("sc", cum_done)
            a_units[0]()

            for par in range(2):
                mmv0 = None
                for half in range(2):
                    bk = 6 + half
                    for si, srct in enumerate((ghi, glo)):
                        w_ = []
                        if half == 0 and si == 0:
                            w_ = [lhilo, p.bank_guard[6], p.bank_guard[7]]
                        cx.op("tensor", lambda e, par=par, half=half,
                              srct=srct, si=si, bk=bk: e.matmul(
                                  p.pb[bk][0:1, 0:512],
                                  p.ones[par * 64:par * 64 + 64, 0:1],
                                  srct[par * 64:par * 64 + 64,
                                       half * 512:half * 512 + 512],
                                  start=(si == 0), stop=(si == 1),
                                  tile_position=(par * 64, 0)),
                              waits=w_,
                              inc="mm" if (half == 1 and si == 1) else None)
                mmv = cx.val("mm")
                for half in range(2):
                    bk = 6 + half
                    off = par * 1024 + half * 512
                    cx.op("scalar", lambda e, bk=bk, off=off: e.copy(
                        out=totf[0:1, off:off + 512],
                        in_=p.pb[bk][0:1, 0:512]),
                        waits=[("mm", mmv)] if half == 0 else ())
                p.bank_guard[6] = p.bank_guard[7] = ("sc", cx.val("sc"))
            tot_done = cx.val("sc")
            k_done = a_units[1]()

            swrows = (f32a[0:1, 0:1024], f32b[0:1, 0:1024])
            for par in range(2):
                cx.op("scalar", lambda e, par=par: e.copy(
                    out=swrows[par].rearrange("p (g q) -> p g q", g=8)
                    [:, :, 0:64],
                    in_=totf[0:1, par * 1024:(par + 1) * 1024].rearrange(
                        "p (g q) -> p g q", g=8)[:, :, 64:128]),
                    waits=[("sc", tot_done)] if par == 0 else ())
                cx.op("scalar", lambda e, par=par: e.copy(
                    out=swrows[par].rearrange("p (g q) -> p g q", g=8)
                    [:, :, 64:128],
                    in_=totf[0:1, par * 1024:(par + 1) * 1024].rearrange(
                        "p (g q) -> p g q", g=8)[:, :, 0:64]))
            swp_done = cx.val("sc")
            slots = (trow, ghi[0:1, 0:1024], glo[0:1, 0:1024],
                     Mqt[0:1, 0:1024])
            rounds = ((0, 0), (0, 1), (1, 0), (1, 1))
            for r, (par, half) in enumerate(rounds):
                off = par * 1024 + half * 512
                cx.op("scalar", lambda e, r=r, off=off: e.copy(
                    out=slots[r][0:1, 0:512], in_=totf[0:1, off:off + 512]))
                scv = cx.val("sc")
                cx.op("vector", lambda e, r=r, off=off: e.tensor_sub(
                    out=slots[r][0:1, 512:1024],
                    in0=totf[0:1, off:off + 512],
                    in1=slots[r][0:1, 0:512]), waits=[("sc", scv)])
            subs_done = cx.val("ev")
            first = True
            for par in range(2):
                for g in range(8):
                    base = par * 1024 + g * 128
                    colA = par * 8 + g
                    colB = 16 + par * 8 + g
                    w_ = []
                    if first:
                        w_ = [("sc", swp_done), p.bank_guard[4]]
                        first = False
                    cx.op("tensor", lambda e, base=base, colA=colA:
                          e.transpose(
                              p.pb[4][:, colA:colA + 1],
                              totf[0:1, base:base + 128],
                              p.identf[0:1, 0:1]),
                          waits=w_)
                    cx.op("tensor", lambda e, par=par, g=g, colB=colB:
                          e.transpose(
                              p.pb[4][:, colB:colB + 1],
                              swrows[par][0:1, g * 128:g * 128 + 128],
                              p.identf[0:1, 0:1]),
                          inc="mm")
            mmv_t = cx.val("mm")
            a_units[2]()
            for r, (par, half) in enumerate(rounds):
                bk = 6 + half
                for si in range(2):
                    w_ = ([("ev", subs_done), p.bank_guard[6],
                           p.bank_guard[7]] if r == 0 and si == 0 else ())
                    cx.op("tensor", lambda e, r=r, par=par, bk=bk, si=si:
                          e.matmul(
                              p.pb[bk][par * 64:par * 64 + 64, 0:512],
                              p.ones[0:1, 0:64],
                              slots[r][0:1, si * 512:si * 512 + 512],
                              start=(si == 0), stop=(si == 1),
                              tile_position=(0, par * 64)),
                          waits=w_,
                          inc="mm" if (r == 3 and si == 1) else None)
            mmv_b = cx.val("mm")
            q_done = a_units[3]()
            cx.op("scalar", lambda e: e.activation(
                out=lamc, in_=p.pb[4][:, 0:32], func=AF.Exp, scale=-1.0),
                waits=[("mm", mmv_t)])
            lamc_done = ("sc", cx.val("sc"))
            p.bank_guard[4] = lamc_done
            for r, (par, half) in enumerate(rounds):
                cx.op("scalar", lambda e, par=par, half=half: e.activation(
                    out=f32a[par * 64:par * 64 + 64,
                             half * 512:half * 512 + 512],
                    in_=p.pb[6 + half][par * 64:par * 64 + 64, 0:512],
                    func=AF.Exp, scale=-1.0),
                    waits=[("mm", mmv_b)] if r == 0 else ())
            b2_done = cx.val("sc")
            p.bank_guard[6] = p.bank_guard[7] = ("sc", b2_done)
            a_units[4]()
            v_done = a_units[5]()

            if "C" not in stages:
                continue
            cx.op("vector", lambda e: e.tensor_mul(out=f32b, in0=k_tm,
                                                   in1=k_tm),
                  waits=[("mm", mmv_t), k_done])
            cx.op("vector", lambda e: e.tensor_reduce(
                out=rnorm, in_=f32b.rearrange("p (h d) -> p h d", h=16),
                axis=mybir.AxisListType.X, op=ALU.add))
            cx.op("vector", lambda e: e.tensor_scalar(
                out=rnorm, in0=rnorm, scalar1=1e-24, scalar2=None,
                op0=ALU.max))
            evv = cx.val("ev")
            cx.op("scalar", lambda e: e.activation(out=rnorm, in_=rnorm,
                                                   func=AF.Ln),
                  waits=[("ev", evv)])
            cx.op("scalar", lambda e: e.activation(out=rnorm, in_=rnorm,
                                                   func=AF.Exp, scale=-0.5))
            scv = cx.val("sc")
            for h in range(16):
                cx.op("vector", lambda e, h=h: e.tensor_scalar(
                    out=ktn[:, h * 64:(h + 1) * 64],
                    in0=k_tm[:, h * 64:(h + 1) * 64],
                    scalar1=rnorm[:, h:h + 1], scalar2=None, op0=ALU.mult),
                    waits=[("sc", scv)] if h == 0 else ())
            cx.op("vector", lambda e: e.tensor_mul(out=kt_tm, in0=ktn,
                                                   in1=ivl))
            cx.op("vector", lambda e: e.tensor_mul(out=f32b, in0=ktn,
                                                   in1=esig),
                  waits=[("sc", cx.val("sc"))])
            cx.op("vector", lambda e: e.tensor_mul(out=bfs, in0=f32b,
                                                   in1=lam),
                  waits=[("mm", mmv_b)])
            for h in range(16):
                cx.op("vector", lambda e, h=h: e.tensor_scalar(
                    out=a_tm[:, h * 64:(h + 1) * 64],
                    in0=bfs[:, h * 64:(h + 1) * 64],
                    scalar1=gcol[:, h:h + 1], scalar2=None, op0=ALU.mult))
            cx.op("vector", lambda e: e.tensor_mul(out=qt_tm, in0=q_tm,
                                                   in1=lam), waits=[q_done])
            c1_done = ("ev", cx.val("ev"))

            cx.op("vector", lambda e: e.tensor_mul(out=rat, in0=ivl,
                                                   in1=f32a),
                  waits=[("sc", b2_done)])
            cx.op("vector", lambda e: e.tensor_mul(out=kh_tm, in0=ktn,
                                                   in1=rat),
                  waits=[c1_done])
            dec_done = cx.val("ev")
            dec_done_gp = ("gp", cx.val("gp"))

            if "D" not in stages:
                continue
            for src_t, dst_t in ((a_tm, a_fm), (kt_tm, ktfm), (qt_tm, qtfm)):
                for gb in range(2):
                    evv = cx.val("ev")
                    for j in range(4):
                        g = gb * 4 + j
                        cx.op("tensor", lambda e, g=g, j=j, src_t=src_t:
                              e.transpose(
                                  p.pb[6].bitcast(BF16)[:, j * 128:
                                                        j * 128 + 128],
                                  src_t[:, g * 128:g * 128 + 128], p.ident),
                              waits=([("ev", dec_done), dec_done_gp,
                                      ("ev", evv),
                                      p.bank_guard[6]] if j == 0 else ()),
                              inc="mm" if j == 3 else None)
                    mmv = cx.val("mm")
                    cx.op("vector", lambda e, gb=gb, dst_t=dst_t:
                          e.tensor_copy(
                              out=dst_t[:, gb * 512:gb * 512 + 512],
                              in_=p.pb[6].bitcast(BF16)[:, 0:512]),
                          waits=[("mm", mmv)])
                    p.bank_guard[6] = ("ev", cx.val("ev"))

            lowrank_path(wog1, wog2, "og")

            if "E" not in stages:
                continue
            def fmsl(t, h, par):
                g = h // 2
                return t[:, g * 128:(g + 1) * 128][hv(h):hv(h) + 64,
                                                   par * 64:par * 64 + 64]

            def mat_mm_fm(dst_pair_base, lhs_t, rhs_t, waits0):
                import os
                hs = range(0, 16, 2) if os.environ.get("L1E1MODE") == "heven"                     else range(16)
                prs = (0,) if os.environ.get("L1E1MODE") == "par0"                     else (0, 1)
                nmm = 0
                nmax = len(list(hs)) * len(prs) - 1
                for h in hs:
                    for par in prs:
                        w_ = list(waits0) if nmm == 0 else []
                        cx.op("tensor", lambda e, h=h, par=par, lhs_t=lhs_t,
                              rhs_t=rhs_t, b=dst_pair_base: e.matmul(
                                  PB[par * 64:par * 64 + 64,
                                     b + hc(h):b + hc(h) + 64],
                                  fmsl(lhs_t, h, par), fmsl(rhs_t, h, par),
                                  start=True, stop=True,
                                  tile_position=(hv(h), par * 64)),
                              waits=w_, inc="mm" if nmm == nmax else None)
                        nmm += 1
                return cx.val("mm")

            def mat_mm_tm(dst_pair_base, lhs_t, rhs_t, waits0):
                nmm = 0
                for h in range(16):
                    for par in range(2):
                        w_ = list(waits0) if nmm == 0 else []
                        cx.op("tensor", lambda e, h=h, par=par, lhs_t=lhs_t,
                              rhs_t=rhs_t, b=dst_pair_base: e.matmul(
                                  PB[par * 64:par * 64 + 64,
                                     b + hc(h):b + hc(h) + 64],
                                  lhs_t[par * 64:par * 64 + 64,
                                        hc(h):hc(h) + 64],
                                  rhs_t[par * 64:par * 64 + 64,
                                        hc(h):hc(h) + 64],
                                  start=True, stop=True,
                                  tile_position=(par * 64, par * 64)),
                              waits=w_, inc="mm" if nmm == 31 else None)
                        nmm += 1
                return cx.val("mm")

            def mask_evac(dst_sb, mask, base):
                import os
                mmv = cx.val("mm")
                if os.environ.get("L1E1MODE") == "nomask":
                    cx.op("vector", lambda e, dst_sb=dst_sb, base=base:
                          e.tensor_copy(out=dst_sb[:, 0:512],
                                        in_=PB[:, base:base + 512]),
                          waits=[("mm", mmv)])
                    cx.op("vector", lambda e, dst_sb=dst_sb, base=base:
                          e.tensor_copy(out=dst_sb[:, 512:1024],
                                        in_=PB[:, base + 512:base + 1024]))
                    return cx.val("ev")
                cx.op("vector", lambda e, dst_sb=dst_sb, mask=mask,
                      base=base: e.tensor_mul(
                          out=dst_sb[:, 0:512], in0=PB[:, base:base + 512],
                          in1=mask), waits=[("mm", mmv)])
                cx.op("vector", lambda e, dst_sb=dst_sb, mask=mask,
                      base=base: e.tensor_mul(
                          out=dst_sb[:, 512:1024],
                          in0=PB[:, base + 512:base + 1024], in1=mask))
                return cx.val("ev")

            mat_mm_fm(0, a_fm, ktfm,
                      [p.bank_guard[0], p.bank_guard[1], ("ev", cx.val("ev"))])
            ev_ = mask_evac(Bm, MSL, 0)
            p.bank_guard[0] = p.bank_guard[1] = ("ev", ev_)
            if "1" in stages and "2" not in stages:
                continue
            mat_mm_fm(1024, ktfm, a_fm, [p.bank_guard[2], p.bank_guard[3]])
            ev_ = mask_evac(Cm, MSU, 1024)
            p.bank_guard[2] = p.bank_guard[3] = ("ev", ev_)
            mat_mm_fm(0, ktfm, qtfm, [p.bank_guard[0], p.bank_guard[1],
                                      ("ev", cx.val("ev"))])
            ev_ = mask_evac(Mqt, MIU, 0)
            p.bank_guard[0] = p.bank_guard[1] = ("ev", ev_)
            if "2" in stages and "3" not in stages:
                continue

            cx.op("vector", lambda e: e.tensor_add(out=Rmat, in0=Bm,
                                                   in1=p.i64b))
            cx.op("vector", lambda e: e.tensor_add(out=Rp, in0=Cm,
                                                   in1=p.i64b))
            r0_done = cx.val("ev")

            def upd_mms(first_waits, with_r):
                nmm = 0
                nmax = (64 if with_r else 32) - 1
                pairs = ([(0, Cm, Rmat)] if with_r else []) + [(2, Rmat, Cm)]
                for pi, (bb, lhs_t, rhs_t) in enumerate(pairs):
                    for h in range(16):
                        for par in range(2):
                            w_ = first_waits if nmm == 0 else []
                            cx.op("tensor", lambda e, h=h, par=par, bb=bb,
                                  lhs_t=lhs_t, rhs_t=rhs_t: e.matmul(
                                      p.pb4[bb + h % 2][
                                          par * 64:par * 64 + 64,
                                          (h // 2) * 64:(h // 2) * 64 + 64],
                                      lhs_t[par * 64:par * 64 + 64,
                                            hc(h):hc(h) + 64],
                                      rhs_t[par * 64:par * 64 + 64,
                                            hc(h):hc(h) + 64],
                                      start=True, stop=True,
                                      tile_position=(par * 64, par * 64)),
                                  waits=w_, inc="mm" if nmm == nmax else None)
                            nmm += 1
                return cx.val("mm")

            def bc_evac(mmv):
                cx.op("vector", lambda e: e.tensor_copy(
                    out=Bm[:, 0:512], in_=PB[:, 0:512]),
                    waits=[("mm", mmv)])
                cx.op("scalar", lambda e: e.copy(
                    out=Bm[:, 512:1024], in_=PB[:, 512:1024]),
                    waits=[("mm", mmv)])
                cx.op("vector", lambda e: e.tensor_copy(
                    out=Cm[:, 0:512], in_=PB[:, 1024:1536]))
                cx.op("scalar", lambda e: e.copy(
                    out=Cm[:, 512:1024], in_=PB[:, 1536:2048]))
                evv, scv = cx.val("ev"), cx.val("sc")
                p.bank_guard[0] = p.bank_guard[2] = ("ev", evv)
                p.bank_guard[1] = p.bank_guard[3] = ("sc", scv)

            def r_evac(mmv, with_r):
                if with_r:
                    cx.op("vector", lambda e: e.tensor_add(
                        out=Rmat[:, 0:512], in0=Rmat[:, 0:512],
                        in1=p.pb4[0][:, 0:512]), waits=[("mm", mmv)])
                    cx.op("vector", lambda e: e.tensor_add(
                        out=Rmat[:, 512:1024], in0=Rmat[:, 512:1024],
                        in1=p.pb4[1][:, 0:512]))
                cx.op("vector", lambda e: e.tensor_add(
                    out=Rp[:, 0:512], in0=Rp[:, 0:512],
                    in1=p.pb4[2][:, 0:512]),
                    waits=[] if with_r else [("mm", mmv)])
                cx.op("vector", lambda e: e.tensor_add(
                    out=Rp[:, 512:1024], in0=Rp[:, 512:1024],
                    in1=p.pb4[3][:, 0:512]))
                evv = cx.val("ev")
                for b in range(4, 8):
                    p.bank_guard[b] = ("ev", evv)

            mat_mm_tm(0, Cm, Bm, [p.bank_guard[0], p.bank_guard[1],
                                  p.bank_guard[2], p.bank_guard[3],
                                  ("ev", cx.val("ev")),
                                  ("sc", cx.val("sc"))])
            mat_mm_tm(1024, Bm, Cm, [])
            bc_evac(cx.val("mm"))
            for s in range(1, 6):
                with_r = (s < 5)
                upd_mms([p.bank_guard[4], p.bank_guard[5],
                         p.bank_guard[6], p.bank_guard[7],
                         p.bank_guard[0], p.bank_guard[1],
                         p.bank_guard[2], p.bank_guard[3],
                         ("ev", r0_done)], with_r)
                if s < 5:
                    mat_mm_tm(0, Cm, Bm, [])
                    mat_mm_tm(1024, Bm, Cm, [])
                mmv = cx.val("mm")
                r_evac(mmv, with_r)
                if s < 5:
                    bc_evac(mmv)

            if "F" not in stages:
                continue
            cx.op("scalar", lambda e: e.activation(
                out=f32a, in_=og_tm, func=AF.Exp, scale=-1.0),
                waits=[("ev", cx.val("ev"))])
            scv = cx.val("sc")
            cx.op("vector", lambda e: e.tensor_scalar(
                out=f32a, in0=f32a, scalar1=1.0, scalar2=None, op0=ALU.add),
                waits=[("sc", scv)])
            evv = cx.val("ev")
            cx.op("scalar", lambda e: e.activation(
                out=f32a, in_=f32a, func=AF.Ln), waits=[("ev", evv)])
            cx.op("scalar", lambda e: e.activation(
                out=f32a, in_=f32a, func=AF.Exp, scale=-1.0))
            sig_done = ("sc", cx.val("sc"))

            def g_ln(waits0):
                sl = slice(0, 128)
                cx.op("vector", lambda e, sl=sl: e.tensor_mul(
                    out=f32b[sl, :], in0=o_tm[sl, :], in1=f32a[sl, :]),
                    waits=list(waits0) + [sig_done])
                cx.op("vector", lambda e, sl=sl: e.tensor_reduce(
                    out=col1[sl, :], in_=f32b[sl, :],
                    axis=mybir.AxisListType.X, op=ALU.add))
                cx.op("vector", lambda e, sl=sl: e.tensor_scalar(
                    out=col1[sl, :], in0=col1[sl, :], scalar1=1.0 / D,
                    scalar2=None, op0=ALU.mult))
                cx.op("vector", lambda e, sl=sl: e.tensor_scalar(
                    out=f32b[sl, :], in0=f32b[sl, :], scalar1=col1[sl, :],
                    scalar2=None, op0=ALU.subtract))
                evv = cx.val("ev")
                cx.op("scalar", lambda e, sl=sl: e.activation(
                    out=f32a[sl, :], in_=f32b[sl, :], func=AF.Square),
                    waits=[("ev", evv)])
                scv = cx.val("sc")
                cx.op("vector", lambda e, sl=sl: e.tensor_reduce(
                    out=col1[sl, :], in_=f32a[sl, :],
                    axis=mybir.AxisListType.X, op=ALU.add),
                    waits=[("sc", scv)])
                cx.op("vector", lambda e, sl=sl: e.tensor_scalar(
                    out=col1[sl, :], in0=col1[sl, :], scalar1=1.0 / D,
                    scalar2=None, op0=ALU.mult))
                evv = cx.val("ev")
                cx.op("scalar", lambda e, sl=sl: e.activation(
                    out=col1[sl, :], in_=col1[sl, :], func=AF.Ln,
                    bias=self.eps_col[sl, :]), waits=[("ev", evv)])
                cx.op("scalar", lambda e, sl=sl: e.activation(
                    out=col1[sl, :], in_=col1[sl, :], func=AF.Exp,
                    scale=-0.5))
                scv = cx.val("sc")
                cx.op("vector", lambda e, sl=sl: e.tensor_scalar(
                    out=f32b[sl, :], in0=f32b[sl, :], scalar1=col1[sl, :],
                    scalar2=None, op0=ALU.mult), waits=[("sc", scv)])
                cx.op("vector", lambda e, sl=sl: e.tensor_mul(
                    out=obuf[sl, :], in0=f32b[sl, :], in1=nwb[sl, :]))
                return ("ev", cx.val("ev"))

            g_done = [None, None]
            for par in range(2):
                evv0 = cx.val("ev")
                scv0 = cx.val("sc")
                sdec_thunks = []
                first_v = [True]
                first_s = [True]
                for h in range(16):
                    for cpy in range(2):
                        colx = par * 8 + h // 2 + (0 if cpy == (h % 2)
                                                   else 16)
                        if (h % 2) == 0:
                            def th(h=h, cpy=cpy, colx=colx):
                                w_ = ([("ev", evv0), ("sc", scv0)]
                                      if first_v[0] else ())
                                first_v[0] = False
                                cx.op("vector", lambda e, h=h, cpy=cpy,
                                      colx=colx: e.tensor_scalar(
                                          out=sdec[cpy * 64:cpy * 64 + 64,
                                                   hc(h):hc(h) + 64],
                                          in0=s0[cpy * 64:cpy * 64 + 64,
                                                 hc(h):hc(h) + 64],
                                          scalar1=lamc[cpy * 64:
                                                       cpy * 64 + 64,
                                                       colx:colx + 1],
                                          scalar2=None, op0=ALU.mult),
                                      waits=w_)
                        else:
                            def th(h=h, cpy=cpy, colx=colx):
                                w_ = ([("ev", evv0), ("sc", scv0)]
                                      if first_s[0] else ())
                                first_s[0] = False
                                cx.op("scalar", lambda e, h=h, cpy=cpy,
                                      colx=colx: e.activation(
                                          out=sdec[cpy * 64:cpy * 64 + 64,
                                                   hc(h):hc(h) + 64],
                                          in_=s0[cpy * 64:cpy * 64 + 64,
                                                 hc(h):hc(h) + 64],
                                          func=AF.Copy,
                                          scale=lamc[cpy * 64:cpy * 64 + 64,
                                                     colx:colx + 1]),
                                      waits=w_)
                        sdec_thunks.append(th)

                def sdec_drain(k):
                    while sdec_thunks and k > 0:
                        sdec_thunks.pop(0)()
                        k -= 1

                nmm = 0
                for h in range(16):
                    w_ = ([p.bank_guard[4], p.bank_guard[5],
                           ("ev", cx.val("ev"))] if nmm == 0 else [])
                    cx.op("tensor", lambda e, h=h, par=par: e.matmul(
                        p.pb4[h % 2][par * 64:par * 64 + 64,
                                     (h // 2) * 64:(h // 2) * 64 + 64],
                        fmsl(a_fm, h, par),
                        s0[hv(h):hv(h) + 64, hc(h):hc(h) + 64],
                        start=True, stop=True,
                        tile_position=(hv(h), par * 64)),
                        waits=w_, inc="mm" if nmm == 15 else None)
                    nmm += 1
                mmv = cx.val("mm")
                cx.op("vector", lambda e, par=par: e.tensor_add(
                    out=rhs_sb[par * 64:par * 64 + 64, 0:512],
                    in0=p.pb4[0][par * 64:par * 64 + 64, 0:512],
                    in1=v_tm[par * 64:par * 64 + 64, 0:512]),
                    waits=[("mm", mmv), v_done])
                cx.op("vector", lambda e, par=par: e.tensor_add(
                    out=rhs_sb[par * 64:par * 64 + 64, 512:1024],
                    in0=p.pb4[1][par * 64:par * 64 + 64, 0:512],
                    in1=v_tm[par * 64:par * 64 + 64, 512:1024]))
                rhs_done = cx.val("ev")
                p.bank_guard[4] = p.bank_guard[5] = ("ev", rhs_done)
                sdec_drain(12)

                nmm = 0
                for h in range(16):
                    w_ = ([p.bank_guard[6], p.bank_guard[7],
                           ("ev", rhs_done)] if nmm == 0 else [])
                    cx.op("tensor", lambda e, h=h, par=par: e.matmul(
                        p.pb4[2 + h % 2][par * 64:par * 64 + 64,
                                         (h // 2) * 64:(h // 2) * 64 + 64],
                        Tt[par * 64:par * 64 + 64, hc(h):hc(h) + 64],
                        rhs_sb[par * 64:par * 64 + 64,
                               hc(h):hc(h) + 64],
                        start=True, stop=True,
                        tile_position=(par * 64, par * 64)),
                        waits=w_, inc="mm" if nmm == 15 else None)
                    nmm += 1
                mmv = cx.val("mm")
                cx.op("vector", lambda e, par=par: e.tensor_copy(
                    out=u_sb[par * 64:par * 64 + 64, 0:512],
                    in_=p.pb4[2][par * 64:par * 64 + 64, 0:512]),
                    waits=[("mm", mmv)])
                cx.op("scalar", lambda e, par=par: e.copy(
                    out=u_sb[par * 64:par * 64 + 64, 512:1024],
                    in_=p.pb4[3][par * 64:par * 64 + 64, 0:512]),
                    waits=[("mm", mmv)])
                u_done = cx.val("ev")
                u_done_s = cx.val("sc")
                p.bank_guard[6] = ("ev", u_done)
                p.bank_guard[7] = ("sc", u_done_s)
                sdec_drain(12)

                nmm = 0
                for h in range(16):
                    w_ = ([p.bank_guard[4], p.bank_guard[5],
                           p.bank_guard[6], p.bank_guard[7],
                           ("ev", u_done), ("sc", u_done_s)]
                          if nmm == 0 else [])
                    cx.op("tensor", lambda e, h=h, par=par: e.matmul(
                        p.pb4[h % 2][par * 64:par * 64 + 64,
                                     (h // 2) * 64:(h // 2) * 64 + 64],
                        fmsl(qtfm, h, par),
                        s0[hv(h):hv(h) + 64, hc(h):hc(h) + 64],
                        start=True, stop=True,
                        tile_position=(hv(h), par * 64)),
                        waits=w_)
                    cx.op("tensor", lambda e, h=h, par=par: e.matmul(
                        p.pb4[2 + h % 2][par * 64:par * 64 + 64,
                                         (h // 2) * 64:(h // 2) * 64 + 64],
                        Mqt[par * 64:par * 64 + 64, hc(h):hc(h) + 64],
                        u_sb[par * 64:par * 64 + 64, hc(h):hc(h) + 64],
                        start=True, stop=True,
                        tile_position=(par * 64, par * 64)),
                        inc="mm" if nmm == 15 else None)
                    nmm += 1
                mmv = cx.val("mm")
                cx.op("scalar", lambda e, par=par: e.copy(
                    out=o_tm[par * 64:par * 64 + 64, 0:512],
                    in_=p.pb4[0][par * 64:par * 64 + 64, 0:512]),
                    waits=[("mm", mmv)])
                cx.op("scalar", lambda e, par=par: e.copy(
                    out=o_tm[par * 64:par * 64 + 64, 512:1024],
                    in_=p.pb4[1][par * 64:par * 64 + 64, 0:512]))
                scv_o = cx.val("sc")
                cx.op("vector", lambda e, par=par: e.tensor_add(
                    out=o_tm[par * 64:par * 64 + 64, 0:512],
                    in0=o_tm[par * 64:par * 64 + 64, 0:512],
                    in1=p.pb4[2][par * 64:par * 64 + 64, 0:512]),
                    waits=[("sc", scv_o)])
                cx.op("vector", lambda e, par=par: e.tensor_add(
                    out=o_tm[par * 64:par * 64 + 64, 512:1024],
                    in0=o_tm[par * 64:par * 64 + 64, 512:1024],
                    in1=p.pb4[3][par * 64:par * 64 + 64, 0:512]))
                o_done = cx.val("ev")
                p.bank_guard[4] = p.bank_guard[5] = ("sc", scv_o)
                p.bank_guard[6] = p.bank_guard[7] = ("ev", o_done)
                sdec_drain(99)
                sdec_done = [("ev", cx.val("ev")), ("sc", cx.val("sc"))]

                nmm = 0
                for h in range(16):
                    for cpy in range(2):
                        w_ = ([p.bank_guard[6], p.bank_guard[7],
                               ("ev", u_done), ("sc", u_done_s)]
                              if nmm == 0 else [])
                        cx.op("tensor", lambda e, h=h, par=par, cpy=cpy:
                              e.matmul(
                                  p.pb4[2 + h % 2][cpy * 64:cpy * 64 + 64,
                                                   (h // 2) * 64:
                                                   (h // 2) * 64 + 64],
                                  kh_tm[par * 64:par * 64 + 64,
                                        h * 64:h * 64 + 64],
                                  u_sb[par * 64:par * 64 + 64,
                                       hc(h):hc(h) + 64],
                                  start=True, stop=True,
                                  tile_position=(par * 64, cpy * 64)),
                              waits=w_, inc="mm" if nmm == 31 else None)
                        nmm += 1
                mmv = cx.val("mm")
                cx.op("vector", lambda e: e.tensor_add(
                    out=s0[:, 0:512], in0=sdec[:, 0:512],
                    in1=p.pb4[2][:, 0:512]),
                    waits=[("mm", mmv)] + sdec_done)
                cx.op("vector", lambda e: e.tensor_add(
                    out=s0[:, 512:1024], in0=sdec[:, 512:1024],
                    in1=p.pb4[3][:, 0:512]))
                s_done = cx.val("ev")
                p.bank_guard[6] = p.bank_guard[7] = ("ev", s_done)
                g_done[par] = ("ev", o_done)

            if "G" not in stages:
                continue
            g_ln([g_done[1]])
            oln_done = cx.val("ev")

            for gb in range(2):
                evv = cx.val("ev")
                for j in range(4):
                    g = gb * 4 + j
                    cx.op("tensor", lambda e, g=g, j=j: e.transpose(
                        p.pb[6].bitcast(BF16)[:, j * 128:j * 128 + 128],
                        obuf[:, g * 128:g * 128 + 128], p.ident),
                        waits=([("ev", oln_done), ("ev", evv),
                                p.bank_guard[6]] if j == 0 else ()),
                        inc="mm" if j == 3 else None)
                mmv = cx.val("mm")
                cx.op("vector", lambda e, gb=gb: e.tensor_copy(
                    out=ofm[:, gb * 512:gb * 512 + 512],
                    in_=p.pb[6].bitcast(BF16)[:, 0:512]),
                    waits=[("mm", mmv)])
                p.bank_guard[6] = ("ev", cx.val("ev"))
            ofm_done = cx.val("ev")

            nmm = 0
            for oc in range(8):
                bk = oc // 4
                col = (oc % 4) * 128
                for dc in range(NC):
                    w_ = []
                    if nmm == 0:
                        w_ = [p.bank_guard[0], p.bank_guard[1],
                              ("ev", ofm_done)]
                    cx.op("tensor", lambda e, oc=oc, dc=dc, bk=bk, col=col:
                          e.matmul(
                              p.pb[bk][:, col:col + 128],
                              wo1[:, dc, oc * 128:oc * 128 + 128],
                              ofm[:, dc * 128:dc * 128 + 128],
                              start=(dc == 0), stop=(dc == NC - 1)),
                          waits=w_, inc="mm" if dc == NC - 1 else None)
                    nmm += 1
            mmv = cx.val("mm")
            cx.op("vector", lambda e: e.tensor_copy(
                out=dsb[:, 0:512], in_=PB[:, 0:512]), waits=[("mm", mmv)])
            cx.op("vector", lambda e: e.tensor_copy(
                out=dsb[:, 512:1024], in_=PB[:, 512:1024]))
            d_done = cx.val("ev")
            p.bank_guard[0] = p.bank_guard[1] = ("ev", d_done)
            cx.dma("sync", p.da1_dram[:, t0:t0 + 128].rearrange(
                "(c p) t -> p c t", p=128),
                dsb.rearrange("p (c t) -> p c t", c=NC),
                waits=[("ev", d_done)], inc="da1s")

        p.prelease()

    def emit_l1_ffn_residual(self):
        cx = self.ctx
        p = self
        TW = 512
        p.pmark()
        al = p.palloc
        mu = al(NC * 4 * 2, F32, NC * 2)
        om = al(NC * 4 * 2, F32, NC * 2)
        xr = al(NC * TW * 2, BF16, NC * TW).rearrange("p (c t) -> p c t", c=NC)
        xk = al(NC * TW * 2, BF16, NC * TW).rearrange("p (c t) -> p c t", c=NC)
        rsb = al(NC * TW * 2, BF16, NC * TW).rearrange("p (c t) -> p c t", c=NC)
        kact = al(NC * TW * 2, BF16, NC * TW).rearrange("p (c t) -> p c t", c=NC)
        x1s = al(NC * TW * 4, F32, NC * TW).rearrange("p (c t) -> p c t", c=NC)
        das = al(NC * TW * 2, BF16, NC * TW).rearrange("p (c t) -> p c t", c=NC)
        wb = [al(NC * 128 * 2, BF16, NC * 128).rearrange(
            "p (c o) -> p c o", c=NC) for _ in range(2)]
        fsc = al(TW * 4, F32, TW)

        cx.dma("sync", mu[:, 0:NC], p.di["mu_r"], inc="tbl")
        cx.dma("sync", mu[:, NC:2 * NC], p.di["mu_k"], inc="tbl")
        muv = ("tbl", cx.val("tbl"))
        cx.op("vector", lambda e: e.tensor_scalar(
            out=om, in0=mu, scalar1=-1.0, scalar2=1.0,
            op0=ALU.mult, op1=ALU.add), waits=[muv])
        self.w4_guard = [("mm", 0), ("mm", 0)]
        xn_ready = ("ev", cx.val("ev"))

        for ti in range(N // TW):
            t0 = ti * TW
            lds = f"ldx{ti % 2}"
            cx.dma("sync", x1s, p.x1_dram[:, t0:t0 + TW].rearrange(
                "(c p) t -> p c t", p=128),
                waits=[("ev", cx.val("ev")), ("st", cx.val("st"))], inc=lds)
            cx.dma("sync", das, p.da1_dram[:, t0:t0 + TW].rearrange(
                "(c p) t -> p c t", p=128),
                waits=[("da1s", cx.val("da1s"))], inc=lds)
            dav = (lds, cx.val(lds))
            x1v = dav

            for dst, mcol in ((xr, 0), (xk, NC)):
                for c in range(NC):
                    w_ = [xn_ready] if (ti == 0 and c == 0 and mcol == 0)                         else []
                    cx.op("vector", lambda e, dst=dst, c=c, mcol=mcol, t0=t0:
                          e.tensor_scalar(
                              out=dst[:, c, :], in0=p.xn[:, c, t0:t0 + TW],
                              scalar1=mu[:, mcol + c:mcol + c + 1],
                              scalar2=None, op0=ALU.mult), waits=w_)
                    if ti == 0:
                        cx.op("vector", lambda e, dst=dst, c=c, mcol=mcol:
                              e.scalar_tensor_tensor(
                                  out=dst[:, c, 1:TW],
                                  in0=p.xn[:, c, 0:TW - 1],
                                  scalar=om[:, mcol + c:mcol + c + 1],
                                  in1=dst[:, c, 1:TW],
                                  op0=ALU.mult, op1=ALU.add))
                    else:
                        cx.op("vector", lambda e, dst=dst, c=c, mcol=mcol,
                              t0=t0: e.scalar_tensor_tensor(
                                  out=dst[:, c, :],
                                  in0=p.xn[:, c, t0 - 1:t0 + TW - 1],
                                  scalar=om[:, mcol + c:mcol + c + 1],
                                  in1=dst[:, c, :],
                                  op0=ALU.mult, op1=ALU.add))
            mix_done = cx.val("ev")

            for wname, src_t, dst, kind in (("WrT1", xr, rsb, "sig"),
                                            ("WkT1", xk, kact, "relu2")):
                for oc in range(8):
                    par = oc % 2
                    wsem = f"wl{par}"
                    cx.dma("gpsimd", wb[par],
                           p.di[wname][:, oc * 128:(oc + 1) * 128].rearrange(
                               "(c p) o -> p c o", p=128),
                           waits=[(wsem, cx.val(wsem)), self.w4_guard[par]],
                           inc=wsem)
                    wlv = (wsem, cx.val(wsem))
                    bk = oc % 2
                    for c in range(NC):
                        w_ = [wlv]
                        if c == 0:
                            w_ += [p.bank_guard[bk], ("ev", mix_done)]
                        cx.op("tensor", lambda e, oc=oc, c=c, par=par, bk=bk,
                              src_t=src_t: e.matmul(
                                  p.pb[bk][:, 0:TW], wb[par][:, c, :],
                                  src_t[:, c, :],
                                  start=(c == 0), stop=(c == NC - 1)),
                              waits=w_, inc="mm" if c == NC - 1 else None)
                    mmv = cx.val("mm")
                    self.w4_guard[par] = ("mm", mmv)
                    if kind == "sig":
                        cx.op("scalar", lambda e, oc=oc, dst=dst, bk=bk:
                              e.activation(out=dst[:, oc, :],
                                           in_=p.pb[bk][:, 0:TW],
                                           func=AF.Sigmoid),
                              waits=[("mm", mmv)])
                        p.bank_guard[bk] = ("sc", cx.val("sc"))
                    else:
                        cx.op("vector", lambda e, oc=oc, bk=bk:
                              e.tensor_scalar(
                                  out=fsc, in0=p.pb[bk][:, 0:TW],
                                  scalar1=0.0, scalar2=None, op0=ALU.max),
                              waits=[("mm", mmv), ("sc", cx.val("sc"))])
                        p.bank_guard[bk] = ("ev", cx.val("ev"))
                        evv = cx.val("ev")
                        cx.op("scalar", lambda e, oc=oc, dst=dst:
                              e.activation(out=dst[:, oc, :], in_=fsc,
                                           func=AF.Square),
                              waits=[("ev", evv)])

            for oc in range(8):
                par = oc % 2
                wsem = f"wl{par}"
                cx.dma("gpsimd", wb[par],
                       p.di["WvT1"][:, oc * 128:(oc + 1) * 128].rearrange(
                           "(c p) o -> p c o", p=128),
                       waits=[(wsem, cx.val(wsem)), self.w4_guard[par]],
                       inc=wsem)
                wlv = (wsem, cx.val(wsem))
                bk = oc % 2
                for c in range(NC):
                    w_ = [wlv]
                    if c == 0:
                        w_ += [p.bank_guard[bk], ("sc", cx.val("sc"))]
                    cx.op("tensor", lambda e, oc=oc, c=c, par=par, bk=bk:
                          e.matmul(
                              p.pb[bk][:, 0:TW], wb[par][:, c, :],
                              kact[:, c, :],
                              start=(c == 0), stop=(c == NC - 1)),
                          waits=w_, inc="mm" if c == NC - 1 else None)
                mmv = cx.val("mm")
                self.w4_guard[par] = ("mm", mmv)
                cx.op("vector", lambda e, oc=oc, bk=bk: e.tensor_mul(
                    out=fsc, in0=rsb[:, oc, :], in1=p.pb[bk][:, 0:TW]),
                    waits=[("mm", mmv), x1v, dav])
                p.bank_guard[bk] = ("ev", cx.val("ev"))
                cx.op("vector", lambda e, oc=oc: e.tensor_add(
                    out=fsc, in0=fsc, in1=das[:, oc, :]))
                cx.op("vector", lambda e, oc=oc: e.tensor_add(
                    out=x1s[:, oc, :], in0=x1s[:, oc, :], in1=fsc))
            y_done = cx.val("ev")
            cx.dma("sync", p.y[:, t0:t0 + TW].rearrange(
                "(c p) t -> p c t", p=128), x1s,
                waits=[("ev", y_done)], inc="st")
        p.prelease()



def build_program(dbg=(), phases=4):
    nc = bass.Bass("TRN2", target_bir_lowering=False, debug=False)
    p = Prog(nc, dbg=dbg)
    cx = p.ctx

    p.load_consts()

    TW = 512
    p.pmark()
    x_stage = [p.palloc(NC * TW * 4, F32, NC * TW).rearrange(
        "p (c t) -> p c t", c=NC) for _ in range(2)]
    scratch_bf = p.palloc(2 * NC * TW * 2, BF16, 2 * NC * TW)
    scratch_f32 = p.palloc(TW * 4, F32, TW)
    rows_f32 = p.palloc(3 * TW * 4, F32, 3 * TW)[0:1, :]
    rows_bf = p.palloc(2 * TW * 2, BF16, 2 * TW)[0:1, :]
    wr_f = p.palloc(NC * 8 * 4, F32, NC * 8).rearrange(
        "p (c o) -> p c o", c=NC)
    ncol = p.palloc(4, F32, 1)
    lrow = p.palloc(TW * 4, F32, TW)[0:8, :]

    cx.dma("sync", wr_f, p.di["Wr"].rearrange("(c p) o -> p c o", p=128),
           inc="wrf")
    wr_ready = cx.val("wrf")
    for c in range(NC):
        cx.op("tensor", lambda e, c=c: e.matmul(
            p.pb[5][0:1, 0:8], p.one_col, wr_f[:, c, :],
            start=(c == 0), stop=(c == NC - 1)),
            waits=[("wrf", wr_ready), p.sconst_ready] if c == 0 else (),
            inc="mm" if c == NC - 1 else None)
    cx.op("scalar", lambda e: e.copy(out=lrow[0:1, 0:8],
                                     in_=p.pb[5][0:1, 0:8]),
          waits=[("mm", cx.val("mm"))])
    scv_cs = cx.val("sc")
    cx.op("tensor", lambda e: e.transpose(
        p.pb[6][0:8, 0:1], lrow[0:1, 0:8], p.identf[0:1, 0:1]),
        waits=[("sc", scv_cs)], inc="mm")
    cx.op("scalar", lambda e: e.mul(out=ncol[0:8, :], in_=p.pb[6][0:8, 0:1],
                                    mul=-1.0), waits=[("mm", cx.val("mm"))])
    ncol_ready = cx.val("sc")

    for ti in range(N // TW):
        t0 = ti * TW
        buf = x_stage[ti % 2]
        lds = f"ldx{ti % 2}"
        waits = [("ev", cx.val("ev")), ("sc", cx.val("sc"))] if ti >= 2             else [("ld", p.consts_ready)]
        cx.dma("sync", buf,
               p.di["x_fm"][:, t0:t0 + TW].rearrange("(c p) t -> p c t", p=128),
               waits=waits, inc=lds)
        b4 = (0, 1, 2, 3)
        p.emit_ln(p.xn[:, :, t0:t0 + TW], TW, buf, (lds, cx.val(lds)),
                  b4, scratch_bf, scratch_f32, rows_f32, rows_bf)
        for c in range(NC):
            cx.op("tensor", lambda e, c=c, buf=buf: e.matmul(
                p.pb[4][0:8, 0:TW], wr_f[:, c, :], buf[:, c, :],
                start=(c == 0), stop=(c == NC - 1)),
                waits=([("sc", ncol_ready), p.bank_guard[4]] if c == 0 else ()),
                inc="mm" if c == NC - 1 else None)
        mmv = cx.val("mm")
        cx.op("scalar", lambda e: e.copy(out=lrow, in_=p.pb[4][0:8, 0:TW]),
              waits=[("mm", mmv), ("lgs", cx.val("lgs"))])
        p.bank_guard[4] = ("sc", cx.val("sc"))
        cx.op("vector", lambda e: e.scalar_tensor_tensor(
            out=lrow, in0=p.pb[2][0:8, 0:TW], scalar=ncol[0:8, :], in1=lrow,
            op0=ALU.mult, op1=ALU.add), waits=[("sc", cx.val("sc"))])
        cx.op("vector", lambda e: e.tensor_mul(
            out=lrow, in0=lrow, in1=p.pb[3][0:8, 0:TW]))
        lgv = cx.val("ev")
        cx.dma("sync", p.logits_dram[:, t0:t0 + TW], lrow,
               waits=[("ev", lgv)], inc="lgs")
    p.prelease()

    if "xn0" in p.dbg:
        out = p.dbg_out("xn0", [128, NC, N], BF16)
        cx.dma("sync", out, p.xn, waits=[("ev", cx.val("ev"))], inc="st")

    p.emit_l0_attention()

    if "o0" in p.dbg:
        out = p.dbg_out("o0", [128, NC, N], BF16)
        cx.dma("sync", out, p.big, waits=[("ev", cx.val("ev")),
                                          ("mm", cx.val("mm"))], inc="st")

    p.pmark()
    p.m_act = p.palloc(16 * 512 * 2, BF16, 16 * 512).rearrange(
        "p (a b) -> p a b", a=16)
    act_flat = p.m_act.rearrange("p a b -> p (a b)")
    dfxs = p.palloc(4096 * 4, F32, 4096)
    p.m_df = dfxs[:, 0:2048].bitcast(BF16).rearrange("p (a b) -> p a b", a=NC)
    p.m_xs = dfxs[:, 2048:4096].rearrange("p (a b) -> p a b", a=NC)
    p.m_ring = [p.palloc(4096 * 2, BF16, 4096) for _ in range(3)]
    p.m_wbc = p.palloc(512 * 4, F32, 512)
    p.m_sab = p.palloc(512 * 2, BF16, 512)
    p.m_b1a = p.palloc(128 * 4, F32, 128)
    p.m_b1g = p.palloc(128 * 4, F32, 128)
    p.m_b2 = p.palloc(1024 * 2, BF16, 1024)[0:8, :]
    p.w_bf = p.palloc(N * 2, BF16, N)[0:8, :]
    p.r_wr = p.palloc(NC * 8 * 2, BF16, NC * 8).rearrange(
        "p (c o) -> p c o", c=NC)
    p.r_br = p.palloc(4, F32, 1)[0:8, :]
    p.r_logit = dfxs[0:8, 0:4096]
    p.r_ltm = act_flat[:, 0:512].bitcast(F32).rearrange(
        "p (a b) -> p a b", a=32)
    p.r_scr = act_flat[:, 512:1792].bitcast(F32)
    p.m_lnbf = act_flat[:, 0:4096]
    p.m_lnf = act_flat[:, 4096:4608].bitcast(F32)
    p.m_lrows = act_flat[:, 4608:6144].bitcast(F32)[0:1, :]
    p.m_lrbf = act_flat[:, 6144:6656][0:1, :]

    p.emit_router()
    p.emit_l0_moe_residual()
    p.prelease()

    if "x1" in p.dbg:
        out = p.dbg_out("x1", [D, N], F32)
        cx.dma("sync", out, p.x1_dram, waits=[("st", cx.val("st"))], inc="st")
    if "xn1" in p.dbg:
        out = p.dbg_out("xn1", [128, NC, N], BF16)
        cx.dma("sync", out, p.xn, waits=[("ev", cx.val("ev"))], inc="st")

    if phases >= 3:
        import os
        p.emit_l1_attention(
            ncp=int(os.environ.get("L1CP", "32")),
            stages=os.environ.get("L1STAGES", "ABCDEFG"))

    if "da1" in p.dbg and phases >= 3:
        out = p.dbg_out("da1", [D, N], BF16)
        cx.dma("sync", out, p.da1_dram,
               waits=[("da1s", cx.val("da1s"))], inc="st")

    if phases >= 4:
        p.emit_l1_ffn_residual()
    else:
        cx.dma("sync", p.y, p.x1_dram, waits=[("st", cx.val("st"))], inc="st")

    cx.op("sync", lambda e: e.nop(), waits=[("st", cx.val("st")),
                                            ("ev", cx.val("ev"))])
    cx.finalize()
    return nc, p



def kernel(**inputs):
    hp = host_prepare(inputs)
    nc, p = build_program()
    shared = {k: v for k, v in hp.items() if k != "x_fm"}
    in_maps = [dict(shared, x_fm=hp["x_fm"][b]) for b in range(8)]
    res = run_bass_kernel_spmd(nc, in_maps, core_ids=list(range(8)))
    out = np.empty((8, N, D), np.float32)
    for b in range(8):
        out[b] = np.asarray(res.results[b]["y"]).T
    return out
